# version 09
# baseline (speedup 1.0000x reference)
import math
import numpy as np
import ml_dtypes
from contextlib import ExitStack

import concourse.bass as bass
import concourse.mybir as mybir
from concourse.bass_utils import run_bass_kernel_spmd

F32 = mybir.dt.float32
BF16 = mybir.dt.bfloat16
U32 = mybir.dt.uint32
AF = mybir.ActivationFunctionType
ALU = mybir.AluOpType
AX = mybir.AxisListType
IOoA = bass.IndirectOffsetOnAxis

B, S, E = 2, 8192, 2048
G, D = 4, 128
NEXP, I = 32, 768
NT = S // 128
NOWN = 16
NC_ = E // 128
NIB = I // 128
CSH = 20.0
ALPHA = 1.0 / math.sqrt(D)
VST = 132
MOFF = [sum(4 * kk + 4 for kk in range(k)) for k in range(NOWN + 1)]
CAP_T = 5
CAP = CAP_T * 128
NSLOT = NEXP * CAP
NTILE = NEXP * CAP_T
TOKS_ROWS = NSLOT + 128
DUMMY_TOK = 2048.0
YROWS = NOWN * 128 + 128


class Sem:
    def __init__(self, ctx, nc, name):
        self.s = ctx.enter_context(nc.semaphore(name))
        self.v = 0


def build(phases=4, dbg=False, c_slots=NOWN):
    nc = bass.Bass("TRN2", target_bir_lowering=False, debug=False)

    def din(name, shape, dt):
        return nc.dram_tensor(name, shape, dt, kind="ExternalInput").ap()

    def dscr(name, shape, dt):
        kind = "ExternalOutput" if dbg else "Internal"
        return nc.dram_tensor(name, shape, dt, kind=kind).ap()

    x_own = din("x_own", [NOWN, 128, E], F32)
    x_full = din("x_full", [NT, 128, E], F32)
    wq_s = din("wq_s", [128, NC_, 2048], F32)
    wqb_s = din("wqb_s", [1, 2048], F32)
    wk_s = din("wk_s", [128, G * NC_ * 128], F32)
    bkT_s = din("bkT_s", [128, G], F32)
    wv_s = din("wv_s", [128, NC_ * 512], F32)
    bv_s = din("bv_s", [1, 512], F32)
    rw_s = din("rw_s", [128, NC_ * 128], F32)
    wgu_in = din("wgu_in", [NEXP, 128, 16 * 1536], F32)
    wd_in = din("wd_in", [NEXP, 128, NIB * 2048], F32)
    masks = din("masks", [128, NOWN * 4 * 128], BF16)
    ident = din("ident", [128, 128], BF16)
    tri_c = din("tri_c", [128, 128], BF16)
    ones_c = din("ones_c", [128, 128], BF16)
    ecap_c = din("ecap_c", [128, NOWN * 32], F32)
    tokid_c = din("tokid_c", [128, NOWN], F32)

    qT_scr = dscr("qT_scr", [NOWN, 128, 2048], BF16)
    xn_scr = dscr("xn_scr", [YROWS, E], BF16)
    toks2 = dscr("toks2", [TOKS_ROWS, 2], F32)
    xg_scr = dscr("xg_scr", [NTILE * 128, E], BF16)
    y = nc.dram_tensor("y", [YROWS, E], F32, kind="ExternalOutput").ap()

    ctx = ExitStack()
    EE = ctx.enter_context

    def sb(name, shape, dt, side="left"):
        return nc.sbuf_tensor(name, shape, dt, side=side)

    consts = EE(sb("consts", [128, 8], F32))
    ones_col = EE(sb("ones_col", [1, 128], BF16))
    id_sb = EE(sb("id_sb", [128, 128], BF16))
    w_sb = EE(sb("w_sb", [128, NOWN * 32], F32))
    stat6 = EE(sb("stat6", [128, 24], F32))
    mv = EE(sb("mv", [128, 8], F32))
    pay8 = EE(sb("pay8", [128, NOWN * 8 * 2], F32))
    pos8_u = EE(sb("pos8_u", [128, NOWN * 8], U32))

    s_x = Sem(ctx, nc, "s_x")
    s_g = Sem(ctx, nc, "s_g")
    s_out = Sem(ctx, nc, "s_out")
    s_pe = Sem(ctx, nc, "s_pe")
    s_tr = Sem(ctx, nc, "s_tr")
    s_pe2 = Sem(ctx, nc, "s_pe2")
    s_dve = Sem(ctx, nc, "s_dve")
    s_ev = Sem(ctx, nc, "s_ev")
    s_act = Sem(ctx, nc, "s_act")
    s_ti = Sem(ctx, nc, "s_ti")
    s_tev = Sem(ctx, nc, "s_tev")
    s_xw = Sem(ctx, nc, "s_xw")
    s_sc = Sem(ctx, nc, "s_sc")

    block = EE(nc.Block())
    ALLSEMS = [s_x, s_g, s_out, s_pe, s_tr, s_pe2, s_dve, s_ev, s_act, s_ti, s_tev, s_xw]

    def barrier():
        vals = [(sm.s, sm.v) for sm in ALLSEMS]
        for name in ("sync", "tensor", "vector", "scalar", "gpsimd"):
            def _f(eng, vals=vals):
                for s, v in vals:
                    if v > 0:
                        eng.wait_ge(s, v)
            getattr(block, name)(_f)

    def emit_ln(x_ready_val, xb, xn_f=None, xn_lo=None, xn_hi=None, extra_wait=None):
        def _v1(v, xv=x_ready_val):
            v.wait_ge(s_x.s, xv)
            for i in range(3):
                v.bn_stats(stat6[:, i * 6:(i + 1) * 6], xb[:, i * 512:(i + 1) * 512])
            v.bn_stats(stat6[:, 18:24], xb[:, 1536:2048]).then_inc(s_dve.s, 1)
            v.wait_ge(s_dve.s, s_dve.v + 1)
            v.bn_aggr(mv[:, 0:2], stat6[:]).then_inc(s_dve.s, 1)
        block.vector(_v1)
        s_dve.v += 2
        mv_ready = s_dve.v

        def _a1(s, mvv=mv_ready):
            s.wait_ge(s_dve.s, mvv)
            s.activation(mv[:, 2:3], mv[:, 1:2], AF.Ln, bias=consts[:, 2:3]).then_inc(s_act.s, 1)
            s.wait_ge(s_act.s, s_act.v + 1)
            s.activation(mv[:, 3:4], mv[:, 2:3], AF.Exp, scale=consts[:, 3:4]).then_inc(s_act.s, 1)
        block.scalar(_a1)
        s_act.v += 2
        rstd_ready = s_act.v

        def _v2(v, rv=rstd_ready, ew=extra_wait):
            v.wait_ge(s_act.s, rv)
            if ew:
                for sm, vv in ew:
                    if vv > 0:
                        v.wait_ge(sm, vv)
            if xn_lo is not None:
                v.tensor_scalar(xn_hi[:], xb[:], mv[:, 0:1], mv[:, 3:4], op0=ALU.subtract, op1=ALU.mult)
                v.tensor_scalar(xn_f[:], xb[:], mv[:, 0:1], mv[:, 3:4], op0=ALU.subtract, op1=ALU.mult).then_inc(s_dve.s, 1)
                v.wait_ge(s_dve.s, s_dve.v + 1)
                v.tensor_sub(xn_lo[:], xn_f[:], xn_hi[:]).then_inc(s_dve.s, 1)
            else:
                v.tensor_scalar(xn_hi[:], xb[:], mv[:, 0:1], mv[:, 3:4], op0=ALU.subtract, op1=ALU.mult).then_inc(s_dve.s, 1)
        block.vector(_v2)
        s_dve.v += 2 if xn_lo is not None else 1
        return s_dve.v

    tr_parity = [0]
    ps_tr = None

    def emit_transposes(pairs, pre_wait=None, ev_pre_wait=None):
        n = len(pairs)
        tr0 = s_tr.v

        def _pe(t, pre_wait=pre_wait, tr0=tr0):
            if pre_wait:
                for sm, vv in pre_wait:
                    if vv > 0:
                        t.wait_ge(sm, vv)
            for i, (src, dst) in enumerate(pairs):
                gi = tr0 + i
                if gi >= 2:
                    t.wait_ge(s_tev.s, gi - 1)
                t.transpose(ps_tr[(tr_parity[0] + i) % 2][:], src, id_sb[:]).then_inc(s_tr.s, 1)
        block.tensor(_pe)
        s_tr.v += n

        def _ve(v, tr0=tr0, par=tr_parity[0], epw=ev_pre_wait):
            if epw:
                for sm, vv in epw:
                    if vv > 0:
                        v.wait_ge(sm, vv)
            for i, (src, dst) in enumerate(pairs):
                v.wait_ge(s_tr.s, tr0 + i + 1)
                v.tensor_copy(dst, ps_tr[(par + i) % 2][:]).then_inc(s_tev.s, 1)
        block.vector(_ve)
        s_tev.v += n
        tr_parity[0] = (tr_parity[0] + n) % 2
        return s_tev.v

    def _pro_v(v):
        v.memset(consts[:, 0:1], -CSH)
        v.memset(consts[:, 1:2], 1.0 / 2048.0)
        v.memset(consts[:, 2:3], 1e-5)
        v.memset(consts[:, 3:4], -0.5)
        v.memset(consts[:, 4:5], DUMMY_TOK)
        v.memset(ones_col[:], 1.0).then_inc(s_dve.s, 1)
    block.vector(_pro_v)
    s_dve.v += 1
    pro_dve = s_dve.v

    ctxAB = ExitStack()
    if phases >= 1:
        ctxA = ExitStack()
        EA = ctxA.enter_context
        xbuf = [ctxAB.enter_context(sb(f"xbuf{i}", [128, E], F32, side="right")) for i in range(2)]
        xn_hi = ctxAB.enter_context(sb("xn_hi", [128, E], BF16, side="right"))
        wq_sb = EA(sb("wq_sb", [128, NC_ * 2048], BF16))
        wqb_sb = EA(sb("wqb_sb", [1, 2048], BF16))
        rw_f = EA(sb("rw_f", [128, NC_ * 128], F32))
        rw_hi = EA(sb("rw_hi", [128, NC_ * 128], BF16))
        rw_lo = EA(sb("rw_lo", [128, NC_ * 128], BF16))
        xn_f = EA(sb("xn_f", [128, E], F32))
        xn_lo = EA(sb("xn_lo", [128, E], BF16))
        xnT_sb = EA(sb("xnT_sb", [128, E], BF16))
        xnT_lo_sb = EA(sb("xnT_lo_sb", [128, E], BF16))
        qbf = EA(sb("qbf", [128, 2048], BF16))
        qt_sb = EA(sb("qt_sb", [128, 2048], BF16))
        pr_p = EA(sb("pr_p", [128, 32], F32))
        pr_m8 = EA(sb("pr_m8", [128, 8], F32))
        pr_zap = EA(sb("pr_zap", [128, 32], F32))
        pr_s = EA(sb("pr_s", [128, 2], F32))
        tri_sb = EA(sb("tri_sb", [128, 128], BF16))
        onesq_sb = EA(sb("onesq_sb", [128, 128], BF16))
        ecap_sb = EA(sb("ecap_sb", [128, NOWN * 32], F32))
        tokid_sb = EA(sb("tokid_sb", [128, NOWN], F32))
        init_sb = EA(sb("init_sb", [128, (TOKS_ROWS // 128) * 2], F32))
        a_bf = EA(sb("a_bf", [128, NOWN * 32], BF16))
        a_f = EA(sb("a_f", [128, NOWN * 32], F32))
        tmp512 = EA(sb("tmp512", [128, NOWN * 32], F32))
        cA = EA(sb("cA", [128, NOWN * 32], F32))
        cB = EA(sb("cB", [128, NOWN * 32], F32))
        mjt = EA(sb("mjt", [128, NOWN * 32], F32))
        mj2 = EA(sb("mj2", [128, NOWN * 32], F32))
        pos8_f = EA(sb("pos8_f", [128, NOWN * 8], F32))
        ps_tr = [EA(nc.psum_tensor(f"ps_trA{i}", [128, 128], BF16)) for i in range(2)]
        ps_q = EA(nc.psum_tensor("ps_q", [128, 2048], F32))
        ps_r = EA(nc.psum_tensor("ps_r", [128, 32], F32))
        ps_rank = EA(nc.psum_tensor("ps_rank", [128, NOWN * 32], F32))

        def _a_g(g):
            g.dma_start(wq_sb[:].rearrange("p (c n) -> p c n", c=NC_), wq_s).then_inc(s_g.s, 16)
            g.dma_start(wqb_sb[:], wqb_s).then_inc(s_g.s, 16)
            g.dma_start(id_sb[:], ident).then_inc(s_g.s, 16)
            g.dma_start(tri_sb[:], tri_c).then_inc(s_g.s, 16)
            g.dma_start(onesq_sb[:], ones_c).then_inc(s_g.s, 16)
            g.dma_start(ecap_sb[:], ecap_c).then_inc(s_g.s, 16)
            g.dma_start(tokid_sb[:], tokid_c).then_inc(s_g.s, 16)
        block.gpsimd(_a_g)
        s_g.v += 112
        gA_ready = s_g.v
        id_ready = s_g.v

        def _a_s(sy):
            sy.dma_start(rw_f[:], rw_s).then_inc(s_x.s, 16)
            sy.dma_start(xbuf[0][:], x_own[0]).then_inc(s_x.s, 16)
        block.sync(_a_s)
        s_x.v += 32

        def _a_v0(v):
            v.memset(qbf[:], 0.0).then_inc(s_dve.s, 1)
            v.memset(init_sb[:], 0.0).then_inc(s_dve.s, 1)
            v.wait_ge(s_dve.s, s_dve.v + 2)
            iv = init_sb[:].rearrange("p (r w) -> p r w", w=2)
            v.tensor_copy(iv[:, :, 0:1],
                          consts[:, 4:5].rearrange("p (a b) -> p a b", b=1)
                          .to_broadcast([128, TOKS_ROWS // 128, 1])).then_inc(s_dve.s, 1)
        block.vector(_a_v0)
        s_dve.v += 3
        init_done = s_dve.v

        def _a_g0(g, idn=init_done):
            g.wait_ge(s_dve.s, idn)
            g.dma_start(toks2.rearrange("(p r) w -> p (r w)", p=128), init_sb[:]).then_inc(s_ti.s, 16)
            g.dma_start(xn_scr[NOWN * 128:YROWS, :], qbf[:]).then_inc(s_ti.s, 16)
        block.gpsimd(_a_g0)
        s_ti.v += 32
        ti_done = s_ti.v

        def _a_v(v, xv=s_x.v):
            v.wait_ge(s_x.s, xv)
            v.tensor_copy(rw_hi[:], rw_f[:]).then_inc(s_dve.s, 1)
            v.wait_ge(s_dve.s, s_dve.v + 1)
            v.tensor_sub(rw_lo[:], rw_f[:], rw_hi[:]).then_inc(s_dve.s, 1)
        block.vector(_a_v)
        s_dve.v += 2
        rw_ready = s_dve.v

        prp_done_prev = 0
        w_done_prev = 0
        qt_out_prev = 0
        q_done_prev = 0
        xnw_prev = 0
        for k in range(NOWN):
            ew = [(s_xw.s, xnw_prev)] if xnw_prev else None
            ln_done = emit_ln(s_x.v, xbuf[k % 2], xn_f=xn_f, xn_lo=xn_lo, xn_hi=xn_hi,
                              extra_wait=ew)

            if k + 1 < NOWN:
                def _nx(sy, k=k, need=ln_done):
                    sy.wait_ge(s_dve.s, need)
                    sy.dma_start(xbuf[(k + 1) % 2][:], x_own[k + 1]).then_inc(s_x.s, 16)
                block.sync(_nx)
                s_x.v += 16

            def _xnw(g, k=k, need=ln_done):
                g.wait_ge(s_dve.s, need)
                g.dma_start(xn_scr[k * 128:(k + 1) * 128, :], xn_hi[:]).then_inc(s_xw.s, 16)
            block.gpsimd(_xnw)
            s_xw.v += 16
            xnw_prev = s_xw.v

            pw = [(s_dve.s, ln_done), (s_g.s, id_ready)]
            if qt_out_prev:
                pw.append((s_out.s, qt_out_prev))
            if q_done_prev:
                pw.append((s_pe.s, q_done_prev))
            pairs = [(xn_hi[:, c * 128:(c + 1) * 128], xnT_sb[:, c * 128:(c + 1) * 128]) for c in range(NC_)]
            pairs += [(xn_lo[:, c * 128:(c + 1) * 128], xnT_lo_sb[:, c * 128:(c + 1) * 128]) for c in range(NC_)]
            ev_hi_lo = emit_transposes(pairs, pre_wait=pw)

            def _q_pe(t, k=k, ev=ev_hi_lo, pd=prp_done_prev):
                t.wait_ge(s_tev.s, ev)
                if k == 0:
                    t.wait_ge(s_g.s, gA_ready)
                    t.wait_ge(s_dve.s, rw_ready)
                if pd:
                    t.wait_ge(s_act.s, pd)
                for c in range(NC_):
                    lhs = xnT_sb[:, c * 128:(c + 1) * 128]
                    for n4 in range(4):
                        t.matmul(ps_q[:, n4 * 512:(n4 + 1) * 512], lhs,
                                 wq_sb[:, c * 2048 + n4 * 512: c * 2048 + (n4 + 1) * 512],
                                 start=(c == 0), stop=False)
                    t.matmul(ps_r[:], lhs, rw_hi[:, c * 128:c * 128 + 32], start=(c == 0), stop=False)
                    t.matmul(ps_r[:], lhs, rw_lo[:, c * 128:c * 128 + 32], start=False, stop=False)
                    mm = t.matmul(ps_r[:], xnT_lo_sb[:, c * 128:(c + 1) * 128],
                                  rw_hi[:, c * 128:c * 128 + 32], start=False, stop=(c == NC_ - 1))
                    if c == NC_ - 1:
                        mm.then_inc(s_pe.s, 1)
                last = None
                for n4 in range(4):
                    last = t.matmul(ps_q[:, n4 * 512:(n4 + 1) * 512], ones_col[0:1, :],
                                    wqb_sb[0:1, n4 * 512:(n4 + 1) * 512], start=False, stop=True)
                last.then_inc(s_pe.s, 1)
            block.tensor(_q_pe)
            s_pe.v += 2
            q_done = s_pe.v
            q_done_prev = q_done

            def _q_v(v, k=k, qd=q_done):
                v.wait_ge(s_pe.s, qd)
                if k == 0:
                    v.wait_ge(s_ti.s, ti_done)
                v.tensor_scalar(qbf[:], ps_q[:], ALPHA, None, op0=ALU.mult).then_inc(s_dve.s, 1)
            block.vector(_q_v)
            s_dve.v += 1
            qbf_done = s_dve.v

            pairs = [(qbf[:, c * 128:(c + 1) * 128], qt_sb[:, c * 128:(c + 1) * 128]) for c in range(NC_)]
            ev_qt = emit_transposes(pairs, pre_wait=[(s_dve.s, qbf_done)])

            def _qtw(g, k=k, ev=ev_qt):
                g.wait_ge(s_tev.s, ev)
                g.dma_start(qT_scr[k], qt_sb[:]).then_inc(s_out.s, 16)
            block.gpsimd(_qtw)
            s_out.v += 16
            qt_out_prev = s_out.v

            def _r_a(s, qd=q_done, wdp=w_done_prev):
                s.wait_ge(s_pe.s, qd - 1)
                if wdp:
                    s.wait_ge(s_dve.s, wdp)
                s.activation(pr_p[:], ps_r[:], AF.Exp, bias=consts[:, 0:1]).then_inc(s_act.s, 1)
            block.scalar(_r_a)
            s_act.v += 1
            prp_done_prev = s_act.v

            def _r_v(v, k=k, pv=s_act.v):
                v.wait_ge(s_act.s, pv)
                v.max(out=pr_m8[:], in_=pr_p[:]).then_inc(s_dve.s, 1)
                v.wait_ge(s_dve.s, s_dve.v + 1)
                v.match_replace(out=pr_zap[:], in_to_replace=pr_m8[:], in_values=pr_p[:], imm_value=0.0).then_inc(s_dve.s, 1)
                v.wait_ge(s_dve.s, s_dve.v + 2)
                v.tensor_sub(pr_zap[:], pr_p[:], pr_zap[:]).then_inc(s_dve.s, 1)
                v.wait_ge(s_dve.s, s_dve.v + 3)
                v.reduce_sum(pr_s[:, 0:1], pr_zap[:], axis=AX.X).then_inc(s_dve.s, 1)
                v.wait_ge(s_dve.s, s_dve.v + 4)
                v.reciprocal(pr_s[:, 1:2], pr_s[:, 0:1]).then_inc(s_dve.s, 1)
                v.wait_ge(s_dve.s, s_dve.v + 5)
                v.tensor_scalar(w_sb[:, k * 32:(k + 1) * 32], pr_zap[:], pr_s[:, 1:2], None,
                                op0=ALU.mult).then_inc(s_dve.s, 1)
            block.vector(_r_v)
            s_dve.v += 6
            w_done_prev = s_dve.v

        def _ri_v(v, wd=w_done_prev):
            v.wait_ge(s_dve.s, wd)
            v.tensor_scalar(a_bf[:], w_sb[:], 0.0, None, op0=ALU.is_gt).then_inc(s_dve.s, 1)
            v.wait_ge(s_dve.s, s_dve.v + 1)
            v.tensor_scalar(a_f[:], w_sb[:], 0.0, None, op0=ALU.is_gt).then_inc(s_dve.s, 1)
        block.vector(_ri_v)
        s_dve.v += 2
        a_done = s_dve.v

        def _rk_pe(t, ad=a_done):
            t.wait_ge(s_dve.s, ad)
            last = None
            for k in range(NOWN):
                for kp in range(k + 1):
                    last = t.matmul(ps_rank[:, k * 32:(k + 1) * 32],
                                    tri_sb[:] if kp == k else onesq_sb[:],
                                    a_bf[:, kp * 32:(kp + 1) * 32],
                                    start=(kp == 0), stop=(kp == k))
            last.then_inc(s_pe.s, 1)
        block.tensor(_rk_pe)
        s_pe.v += 1
        rank_done = s_pe.v

        def _pos_v(v, rd=rank_done):
            base = s_dve.v
            cnt = [0]
            def stepped(ins):
                v.wait_ge(s_dve.s, base + cnt[0])
                ins().then_inc(s_dve.s, 1)
                cnt[0] += 1
            v.wait_ge(s_pe.s, rd)
            v.wait_ge(s_g.s, gA_ready)
            stepped(lambda: v.tensor_tensor(tmp512[:], ps_rank[:], ecap_sb[:], ALU.add))
            stepped(lambda: v.tensor_copy(cA[:], a_f[:]))
            src, dst = cA, cB
            for sh in (1, 2, 4, 8, 16):
                s3 = src[:].rearrange("p (k e) -> p k e", e=32)
                d3 = dst[:].rearrange("p (k e) -> p k e", e=32)
                v.wait_ge(s_dve.s, base + cnt[0])
                v.tensor_copy(d3[:, :, 0:sh], s3[:, :, 0:sh])
                v.tensor_tensor(d3[:, :, sh:32], s3[:, :, sh:32],
                                s3[:, :, 0:32 - sh], ALU.add).then_inc(s_dve.s, 1)
                cnt[0] += 1
                src, dst = dst, src
            pos3 = pos8_f[:].rearrange("p (k j) -> p k j", j=8)
            pay4 = pay8[:].rearrange("p (k j w) -> p k j w", j=8, w=2)
            for j in range(8):
                stepped(lambda j=j: v.tensor_scalar(mjt[:], src[:], float(j + 1), None, op0=ALU.is_equal))
                stepped(lambda: v.tensor_tensor(mjt[:], mjt[:], a_f[:], ALU.mult))
                stepped(lambda: v.tensor_tensor(mj2[:], mjt[:], tmp512[:], ALU.mult))
                stepped(lambda j=j: v.reduce_sum(pos3[:, :, j:j + 1],
                                                 mj2[:].rearrange("p (k e) -> p k e", e=32), axis=AX.X))
                stepped(lambda: v.tensor_tensor(mj2[:], mjt[:], w_sb[:], ALU.mult))
                stepped(lambda j=j: v.reduce_sum(pay4[:, :, j, 1:2],
                                                 mj2[:].rearrange("p (k e) -> p k e", e=32), axis=AX.X))
            tid = tokid_sb[:].rearrange("p (a b) -> p a b", b=1)
            for k in range(NOWN):
                stepped(lambda k=k: v.tensor_copy(pay4[:, k, :, 0:1],
                                                  tid[:, k:k + 1, :].to_broadcast([128, 8, 1])))
            stepped(lambda: v.tensor_copy(pos8_u[:], pos8_f[:]))
        block.vector(_pos_v)
        s_dve.v += 2 + 5 + 48 + 16 + 1
        pay_done = s_dve.v

        def emit_scatters():
            def _sc_g(g, pd=pay_done, tid=ti_done):
                g.wait_ge(s_ti.s, tid)
                g.wait_ge(s_dve.s, pd)
                for col in range(NOWN * 8):
                    g.indirect_dma_start(
                        toks2, IOoA(pos8_u[:, col:col + 1], 0),
                        pay8[:, col * 2:col * 2 + 2], None,
                    ).then_inc(s_sc.s, 16)
            block.gpsimd(_sc_g)
            s_sc.v += NOWN * 8 * 16
            return s_sc.v

        barrier()
        ctxA.close()

    ctxBC = ExitStack()
    if phases >= 2:
        KT_sb = [ctxBC.enter_context(sb(f"KT{g}", [128, S], BF16)) for g in range(G)]
        V_sb = ctxBC.enter_context(sb("V_sb", [128, G * NT * VST], BF16))
        ctxB = ExitStack()
        EB = ctxB.enter_context
        wk_sb = EB(sb("wk_sb", [128, G * NC_ * 128], BF16, side="right"))
        wv_sb = EB(sb("wv_sb", [128, NC_ * 512], BF16, side="right"))
        bkT_sb = EB(sb("bkT_sb", [128, G], F32, side="right"))
        bv_sb = EB(sb("bv_sb", [1, 512], BF16, side="right"))
        xnT4 = EB(sb("xnT4", [128, NC_ * 512], BF16, side="right"))
        ps_tr = [EB(nc.psum_tensor(f"ps_trB{i}", [128, 128], BF16)) for i in range(2)]
        ps_kt = [EB(nc.psum_tensor(f"ps_kt{i}", [128, 512], F32)) for i in range(2)]
        ps_v = [EB(nc.psum_tensor(f"ps_v{i}", [128, 512], F32)) for i in range(2)]
        tr_parity[0] = 0

        def _pro_vb(v):
            vv = V_sb[:].rearrange("p (t c) -> p t c", c=VST)
            v.memset(vv[:, :, 128:129], 1.0).then_inc(s_dve.s, 1)
        block.vector(_pro_vb)
        s_dve.v += 1
        vones_ready = s_dve.v

        def _b_g(g):
            g.dma_start(wk_sb[:], wk_s).then_inc(s_g.s, 16)
            g.dma_start(wv_sb[:], wv_s).then_inc(s_g.s, 16)
            g.dma_start(bv_sb[:], bv_s).then_inc(s_g.s, 16)
        block.gpsimd(_b_g)
        s_g.v += 48
        gB_ready = s_g.v

        sc_total = emit_scatters()

        def _b_s(sy):
            sy.dma_start(bkT_sb[:], bkT_s).then_inc(s_x.s, 16)
            sy.dma_start(xbuf[0][:], x_full[0]).then_inc(s_x.s, 16)
        block.sync(_b_s)
        s_x.v += 32
        bkT_val = s_x.v

        kt_ev = []
        v_ev = []
        for t4 in range(NT // 4):
            for ti in range(4):
                tt = 4 * t4 + ti
                ln_done = emit_ln(s_x.v, xbuf[tt % 2], xn_hi=xn_hi)
                if tt + 1 < NT:
                    def _nx(sy, tt=tt, need=ln_done):
                        sy.wait_ge(s_dve.s, need)
                        sy.dma_start(xbuf[(tt + 1) % 2][:], x_full[tt + 1]).then_inc(s_x.s, 16)
                    block.sync(_nx)
                    s_x.v += 16
                pairs = [(xn_hi[:, c * 128:(c + 1) * 128],
                          xnT4[:, c * 512 + ti * 128: c * 512 + ti * 128 + 128]) for c in range(NC_)]
                emit_transposes(pairs, pre_wait=[(s_dve.s, ln_done)])
            x4_ready = s_tev.v

            for g in range(G):
                def _kt_pe(t, g=g, t4=t4, x4=x4_ready):
                    t.wait_ge(s_tev.s, x4)
                    if t4 == 0 and g == 0:
                        t.wait_ge(s_g.s, gB_ready)
                    if len(kt_ev) >= 2:
                        t.wait_ge(s_dve.s, kt_ev[-2])
                    last = None
                    for c in range(NC_):
                        last = t.matmul(ps_kt[g % 2][:],
                                        wk_sb[:, (g * NC_ + c) * 128:(g * NC_ + c) * 128 + 128],
                                        xnT4[:, c * 512:(c + 1) * 512],
                                        start=(c == 0), stop=(c == NC_ - 1))
                    last.then_inc(s_pe.s, 1)
                block.tensor(_kt_pe)
                s_pe.v += 1
                kt_done = s_pe.v

                def _kt_v(v, g=g, t4=t4, kd=kt_done, bv=bkT_val):
                    v.wait_ge(s_pe.s, kd)
                    if t4 == 0 and g == 0:
                        v.wait_ge(s_x.s, bv)
                    v.tensor_scalar(KT_sb[g][:, t4 * 512:(t4 + 1) * 512], ps_kt[g % 2][:],
                                    bkT_sb[:, g:g + 1], None, op0=ALU.add).then_inc(s_dve.s, 1)
                block.vector(_kt_v)
                s_dve.v += 1
                kt_ev.append(s_dve.v)

            for ti in range(4):
                tt = 4 * t4 + ti
                def _v_pe(t, ti=ti, t4=t4, x4=x4_ready):
                    t.wait_ge(s_tev.s, x4)
                    if t4 == 0 and ti == 0:
                        t.wait_ge(s_g.s, gB_ready)
                        t.wait_ge(s_dve.s, pro_dve)
                    if len(v_ev) >= 2:
                        t.wait_ge(s_dve.s, v_ev[-2])
                    for c in range(NC_):
                        t.matmul(ps_v[ti % 2][:],
                                 xnT4[:, c * 512 + ti * 128: c * 512 + ti * 128 + 128],
                                 wv_sb[:, c * 512:(c + 1) * 512],
                                 start=(c == 0), stop=False)
                    t.matmul(ps_v[ti % 2][:], ones_col[0:1, :], bv_sb[:],
                             start=False, stop=True).then_inc(s_pe.s, 1)
                block.tensor(_v_pe)
                s_pe.v += 1
                v_done = s_pe.v

                def _v_v(v, tt=tt, ti=ti, vd=v_done, vo=vones_ready):
                    v.wait_ge(s_pe.s, vd)
                    if tt == 0:
                        v.wait_ge(s_dve.s, vo)
                    dst = V_sb[:].rearrange("p (gt c) -> p gt c", c=VST)
                    v.tensor_copy(dst[:, tt::NT, 0:128],
                                  ps_v[ti % 2][:].rearrange("p (g d) -> p g d", g=G)).then_inc(s_dve.s, 1)
                block.vector(_v_v)
                s_dve.v += 1
                v_ev.append(s_dve.v)

        barrier()
        ctxB.close()
        ctxAB.close()
    else:
        ctxAB.close()

    if phases >= 3:
        ctxC = ExitStack()
        EC = ctxC.enter_context
        qt4 = [EC(sb(f"qt4_{i}", [128, 2048], BF16, side="right")) for i in range(2)]
        mask_sb = EC(sb("mask_sb", [128, 4 * 128], BF16, side="right"))
        pt_sb = [EC(sb(f"pt_sb{i}", [128, 512], BF16, side="right")) for i in range(2)]
        attn_sb = EC(sb("attn_sb", [128, 2048], F32, side="right"))
        xob = [EC(sb(f"xob{i}", [128, E], F32, side="right")) for i in range(2)]
        base_sb = [EC(sb(f"base_sb{i}", [128, E], F32, side="right")) for i in range(2)]
        linv_sb = EC(sb("linv_sb", [128, 4], F32, side="right"))
        xq_pg = [EC(sb(f"xq_pg{i}", [128, E], BF16, side="right")) for i in range(3)]
        tkp = [EC(sb(f"tkp{i}", [128, CAP_T * 2], F32, side="right")) for i in range(2)]
        gip = [EC(sb(f"gip{i}", [128, CAP_T], U32, side="right")) for i in range(2)]
        ps_st = [EC(nc.psum_tensor(f"ps_st{i}", [128, 512], F32)) for i in range(2)]
        ps_o = [EC(nc.psum_tensor(f"ps_o{i}", [128, 258], F32)) for i in range(2)]

        s_tkp2 = [Sem(ctx, nc, f"s_tkp{i}") for i in range(2)]
        s_icp = Sem(ctx, nc, "s_icp")
        s_gxp3 = [Sem(ctx, nc, f"s_gxp{i}") for i in range(3)]
        s_wxp3 = [Sem(ctx, nc, f"s_wxp{i}") for i in range(3)]
        ALLSEMS.extend(s_tkp2 + s_gxp3 + s_wxp3 + [s_icp])
        pg_gx = {}
        pg_wx = {}

        def emit_pg_write(i):
            def _wx_g(g, i=i, gv=pg_gx[i]):
                g.wait_ge(gv[0], gv[1])
                g.dma_start(xg_scr[i * 128:(i + 1) * 128, :],
                            xq_pg[i % 3][:]).then_inc(s_wxp3[i % 3].s, 16)
            block.gpsimd(_wx_g)
            s_wxp3[i % 3].v += 16
            pg_wx[i] = (s_wxp3[i % 3].s, s_wxp3[i % 3].v)

        def emit_pg(e):
            def _tk_s(sy, e=e, icn=(5 * (e - 1) if e >= 2 else 0)):
                if e == 0:
                    sy.wait_ge(s_sc.s, sc_total)
                if icn:
                    sy.wait_ge(s_icp.s, icn)
                sy.dma_start(
                    tkp[e % 2][:].rearrange("p (c w) -> p c w", w=2),
                    toks2[e * CAP:(e + 1) * CAP, :].rearrange("(c p) w -> p c w", p=128),
                ).then_inc(s_tkp2[e % 2].s, 16)
            block.sync(_tk_s)
            s_tkp2[e % 2].v += 16
            tkv = s_tkp2[e % 2].v

            def _ic_v(v, e=e, tkv=tkv, gwait=(pg_gx.get(5 * (e - 2) + 4) if e >= 2 else None)):
                v.wait_ge(s_tkp2[e % 2].s, tkv)
                if gwait:
                    v.wait_ge(gwait[0], gwait[1])
                t2 = tkp[e % 2][:].rearrange("p (c w) -> p c w", w=2)
                for c in range(CAP_T):
                    v.tensor_copy(gip[e % 2][:, c:c + 1], t2[:, c, 0:1]).then_inc(s_icp.s, 1)
            block.vector(_ic_v)
            s_icp.v += CAP_T

            for c in range(CAP_T):
                i = e * CAP_T + c
                def _gx_g(g, i=i, e=e, c=c, wprev=pg_wx.get(i - 3)):
                    g.wait_ge(s_icp.s, 5 * e + c + 1)
                    if wprev:
                        g.wait_ge(wprev[0], wprev[1])
                    g.indirect_dma_start(
                        xq_pg[i % 3][:], None, xn_scr,
                        IOoA(gip[e % 2][:, c:c + 1], 0),
                    ).then_inc(s_gxp3[i % 3].s, 16)
                block.gpsimd(_gx_g)
                s_gxp3[i % 3].v += 16
                pg_gx[i] = (s_gxp3[i % 3].s, s_gxp3[i % 3].v)
                if i >= 2:
                    emit_pg_write(i - 2)

        def _c_pre(sy):
            sy.dma_start(qt4[0][:], qT_scr[0]).then_inc(s_x.s, 16)
            sy.dma_start(xob[0][:], x_own[0]).then_inc(s_x.s, 16)
            sy.dma_start(mask_sb[:], masks[:, 0:512]).then_inc(s_x.s, 16)
        block.sync(_c_pre)
        s_x.v += 48

        c_st0 = s_pe.v
        c_exp0 = s_act.v
        c_av0 = s_pe2.v
        c_ev0 = s_ev.v
        n_unit = 0
        n_grp = 0
        base_w_vals = []

        for k in range(c_slots):
            J = 4 * k + 4
            qt_ready = s_x.v

            for g in range(G):
                u0 = n_unit
                m = n_grp

                mul_base = s_dve.v

                def _t_grp(t, k=k, g=g, J=J, u0=u0, m=m, qt_ready=qt_ready, mul_base=mul_base):
                    def emit_st(j):
                        u = u0 + j
                        if u >= 2:
                            t.wait_ge(s_act.s, c_exp0 + u - 1)
                        t.matmul(ps_st[u % 2][:], KT_sb[g][:, j * 128:(j + 1) * 128],
                                 qt4[k % 2][:, g * 512:(g + 1) * 512],
                                 start=True, stop=True).then_inc(s_pe.s, 1)

                    def emit_av(j):
                        if j == 0 and m >= 1:
                            t.wait_ge(s_ev.s, c_ev0 + 8 * m)
                        if j >= J - 4:
                            t.wait_ge(s_dve.s, mul_base + (j - (J - 4)) + 1)
                        else:
                            t.wait_ge(s_act.s, c_exp0 + u0 + j + 1)
                        for h4 in range(4):
                            t.matmul(ps_o[h4 // 2][:, (h4 % 2) * 129:(h4 % 2) * 129 + 129],
                                     pt_sb[(u0 + j) % 2][:, h4 * 128:(h4 + 1) * 128],
                                     V_sb[:, (g * NT + j) * VST:(g * NT + j) * VST + 129],
                                     start=(j == 0 and h4 % 2 == 0),
                                     stop=(j == J - 1 and h4 % 2 == 1)).then_inc(s_pe2.s, 1)

                    if g == 0:
                        t.wait_ge(s_x.s, qt_ready)
                    emit_st(0)
                    if J > 1:
                        emit_st(1)
                    for j in range(J):
                        if j + 2 < J:
                            emit_st(j + 2)
                        emit_av(j)
                block.tensor(_t_grp)
                s_pe.v += J
                s_pe2.v += 4 * J
                av_all = s_pe2.v

                def _exp_a(s, J=J, u0=u0):
                    for j in range(J):
                        u = u0 + j
                        s.wait_ge(s_pe.s, c_st0 + u + 1)
                        if u >= 2:
                            s.wait_ge(s_pe2.s, c_av0 + 4 * (u - 1))
                        s.activation(pt_sb[u % 2][:], ps_st[u % 2][:], AF.Exp,
                                     bias=consts[:, 0:1]).then_inc(s_act.s, 1)
                block.scalar(_exp_a)
                s_act.v += J

                def _mul_v(v, J=J, u0=u0, g=g, qt_ready=qt_ready):
                    if g == 0:
                        v.wait_ge(s_x.s, qt_ready)
                    for j in range(J - 4, J):
                        u = u0 + j
                        v.wait_ge(s_act.s, c_exp0 + u + 1)
                        m_b = mask_sb[:, (j - (J - 4)) * 128:(j - (J - 4) + 1) * 128].unsqueeze(1).to_broadcast([128, 4, 128])
                        p_v = pt_sb[u % 2][:].rearrange("p (r n) -> p r n", r=4)
                        v.tensor_tensor(p_v, p_v, m_b, ALU.mult).then_inc(s_dve.s, 1)
                block.vector(_mul_v)
                s_dve.v += 4

                def _ev_v(v, g=g, av=av_all, ev_base=s_ev.v):
                    v.wait_ge(s_pe2.s, av)
                    for h4 in range(4):
                        v.reciprocal(linv_sb[:, h4:h4 + 1],
                                     ps_o[h4 // 2][:, (h4 % 2) * 129 + 128:(h4 % 2) * 129 + 129]).then_inc(s_ev.s, 1)
                        v.wait_ge(s_ev.s, ev_base + 2 * h4 + 1)
                        v.tensor_scalar(attn_sb[:, (4 * g + h4) * 128:(4 * g + h4 + 1) * 128],
                                        ps_o[h4 // 2][:, (h4 % 2) * 129:(h4 % 2) * 129 + 128],
                                        linv_sb[:, h4:h4 + 1], None, op0=ALU.mult).then_inc(s_ev.s, 1)
                block.vector(_ev_v)
                s_ev.v += 8

                n_unit += J
                n_grp += 1

            def _base_v(v, k=k, ev=s_ev.v, qr=qt_ready,
                        bw=(base_w_vals[k - 1] if k >= 1 else 0)):
                v.wait_ge(s_ev.s, ev)
                v.wait_ge(s_x.s, qr)
                if bw:
                    v.wait_ge(s_out.s, bw)
                v.tensor_add(base_sb[k % 2][:], attn_sb[:], xob[k % 2][:]).then_inc(s_dve.s, 1)
            block.vector(_base_v)
            s_dve.v += 1
            base_done = s_dve.v

            def _base_w(g, k=k, bd=base_done):
                g.wait_ge(s_dve.s, bd)
                g.dma_start(y[k * 128:(k + 1) * 128, :], base_sb[k % 2][:]).then_inc(s_out.s, 16)
            block.scalar(_base_w)
            s_out.v += 16
            base_w_vals.append(s_out.v)

            emit_pg(2 * k)
            emit_pg(2 * k + 1)
            if k == NOWN - 1:
                emit_pg_write(NTILE - 2)
                emit_pg_write(NTILE - 1)

            if k + 1 < NOWN:
                def _c_nx(sy, k=k, stv=s_pe.v, mulv=s_dve.v):
                    sy.wait_ge(s_pe.s, stv)
                    sy.wait_ge(s_dve.s, mulv)
                    sy.dma_start(qt4[(k + 1) % 2][:], qT_scr[k + 1]).then_inc(s_x.s, 16)
                    sy.dma_start(xob[(k + 1) % 2][:], x_own[k + 1]).then_inc(s_x.s, 16)
                    sy.dma_start(mask_sb[:],
                                 masks[:, (k + 1) * 512:(k + 2) * 512]).then_inc(s_x.s, 16)
                block.sync(_c_nx)
                s_x.v += 48

        barrier()
        ctxC.close()
    ctxBC.close()

    if phases >= 4:
        base_total = s_out.v
        ctxD = ExitStack()
        ED = ctxD.enter_context
        wgu_sb = [ED(sb(f"wgu{i}", [128, 16 * 1536], BF16)) for i in range(2)]
        wd_sb = [ED(sb(f"wdd{i}", [128, NIB * 2048], BF16)) for i in range(2)]
        xg = [ED(sb(f"xg{i}", [128, E], BF16)) for i in range(2)]
        xgT = [ED(sb(f"xgT{i}", [128, E], BF16)) for i in range(2)]
        Hg = ED(sb("Hg", [128, I], BF16))
        H1 = ED(sb("H1", [128, I], BF16))
        H1T = [ED(sb(f"H1T{i}", [128, NIB * 128], BF16)) for i in range(2)]
        ye = [ED(sb(f"ye{i}", [128, E], F32)) for i in range(2)]
        tk_sb = [ED(sb(f"tk{i}", [128, CAP_T * 2], F32)) for i in range(2)]
        gi_u = [ED(sb(f"gi{i}", [128, CAP_T], U32)) for i in range(2)]
        stag = [ED(sb(f"stag{i}", [128, 1024], F32)) for i in range(4)]
        ps_gate = ED(nc.psum_tensor("ps_gate", [128, I], F32))
        ps_up = ED(nc.psum_tensor("ps_up", [128, I], F32))
        ps_d = [ED(nc.psum_tensor(f"ps_d{i}", [128, 512], F32)) for i in range(2)]
        ps_tr = [ED(nc.psum_tensor(f"ps_trD{i}", [128, 128], BF16)) for i in range(2)]

        s_st = [Sem(ctx, nc, f"s_st{i}") for i in range(4)]
        s_cw = Sem(ctx, nc, "s_cw")
        s_tk = Sem(ctx, nc, "s_tk")
        s_ic = Sem(ctx, nc, "s_ic")
        s_gx2 = [Sem(ctx, nc, f"s_gx{i}") for i in range(2)]
        s_sa2 = [Sem(ctx, nc, f"s_sa{i}") for i in range(2)]
        s_sil = Sem(ctx, nc, "s_sil")
        s_h1 = Sem(ctx, nc, "s_h1")
        s_ye = Sem(ctx, nc, "s_ye")
        ALLSEMS.extend(s_st + s_gx2 + s_sa2 + [s_cw, s_tk, s_ic, s_sil, s_h1, s_ye])

        ye_base = s_ye.v
        pe2_base = s_pe2.v

        up_pe = {}
        gate_pe = {}
        dn_pe2 = {}
        sil_v = {}
        h1_v = {}
        htr_ev = {}
        xgtr_ev = {}
        xgtr_tr = {}
        gx_v = {}
        sa_v = {}
        ic_v = {}
        tk_v = {}
        gu_rdy = {}
        dn_rdy = {}
        wq_cnt = [0]

        def load_toks(e, extra=None):
            def _t_s(sy, e=e, extra=extra):
                if extra:
                    for sm, vv in extra:
                        if vv > 0:
                            sy.wait_ge(sm, vv)
                sy.dma_start(
                    tk_sb[e % 2][:].rearrange("p (c w) -> p c w", w=2),
                    toks2[e * CAP:(e + 1) * CAP, :].rearrange("(c p) w -> p c w", p=128),
                ).then_inc(s_tk.s, 16)
            block.sync(_t_s)
            s_tk.v += 16
            tk_v[e] = s_tk.v

        def cast_idx(e, extra=None):
            def _c_v(v, e=e, extra=extra):
                v.wait_ge(s_tk.s, tk_v[e])
                if extra:
                    for sm, vv in extra:
                        if vv > 0:
                            v.wait_ge(sm, vv)
                tk2 = tk_sb[e % 2][:].rearrange("p (c w) -> p c w", w=2)
                for c in range(CAP_T):
                    v.tensor_copy(gi_u[e % 2][:, c:c + 1], tk2[:, c, 0:1]).then_inc(s_ic.s, 1)
            block.vector(_c_v)
            s_ic.v += CAP_T
            ic_v[e] = s_ic.v

        def emit_wchunks(e, c0, c1):
            for c in range(c0, c1):
                q = wq_cnt[0]
                slot = q % 4
                src = (wgu_in[e][:, c * 1024:(c + 1) * 1024] if c < 24
                       else wd_in[e][:, (c - 24) * 1024:(c - 23) * 1024])
                need = q - 3
                def _ld(eng, slot=slot, src=src, need=need):
                    if need > 0:
                        eng.wait_ge(s_cw.s, need)
                    eng.dma_start(stag[slot][:], src).then_inc(s_st[slot].s, 16)
                (block.sync if q % 2 == 0 else block.scalar)(_ld)
                s_st[slot].v += 16
                stv = s_st[slot].v
                dst = (wgu_sb[e % 2][:, c * 1024:(c + 1) * 1024] if c < 24
                       else wd_sb[e % 2][:, (c - 24) * 1024:(c - 23) * 1024])
                def _ca(a, slot=slot, stv=stv, dst=dst, e=e, c=c):
                    a.wait_ge(s_st[slot].s, stv)
                    if c == 0 and e >= 2:
                        a.wait_ge(s_pe.s, up_pe[5 * (e - 2) + 4])
                    if c == 24 and e >= 2:
                        a.wait_ge(s_pe2.s, dn_pe2[5 * (e - 2) + 4])
                    a.activation(dst, stag[slot][:], AF.Copy).then_inc(s_cw.s, 1)
                block.scalar(_ca)
                wq_cnt[0] += 1
                if c == 23:
                    gu_rdy[e] = wq_cnt[0]
                if c == 35:
                    dn_rdy[e] = wq_cnt[0]

        def gather(i, first=False):
            def _gx_a(a, i=i):
                if i >= 2:
                    a.wait_ge(s_tr.s, xgtr_tr[i - 2])
                a.dma_start(xg[i % 2][:],
                            xg_scr[i * 128:(i + 1) * 128, :]).then_inc(s_gx2[i % 2].s, 16)
            block.sync(_gx_a)
            s_gx2[i % 2].v += 16
            gx_v[i] = s_gx2[i % 2].v

        def sa(i):
            e, c = i // CAP_T, i % CAP_T
            w0, w1 = s_sa2[0].v, s_sa2[1].v
            def _sa_g(g, i=i, e=e, c=c, w0=w0, w1=w1):
                g.wait_ge(s_ye.s, ye_base + 4 * (i + 1))
                if i == 0:
                    g.wait_ge(s_out.s, base_total)
                elif c == 0:
                    if w0:
                        g.wait_ge(s_sa2[0].s, w0)
                    if w1:
                        g.wait_ge(s_sa2[1].s, w1)
                g.indirect_dma_start(
                    y, IOoA(gi_u[e % 2][:, c:c + 1], 0),
                    ye[i % 2][:], None,
                    compute_op=ALU.add,
                ).then_inc(s_sa2[i % 2].s, 16)
            block.gpsimd(_sa_g)
            s_sa2[i % 2].v += 16
            sa_v[i] = s_sa2[i % 2].v

        def emit_xgtr(i):
            pw = [(s_gx2[i % 2].s, gx_v[i])]
            epw = []
            if i >= 2:
                epw.append((s_pe.s, up_pe[i - 2]))
            pairs = [(xg[i % 2][:, c * 128:(c + 1) * 128], xgT[i % 2][:, c * 128:(c + 1) * 128])
                     for c in range(NC_)]
            xgtr_tr[i] = s_tr.v + NC_
            xgtr_ev[i] = emit_transposes(pairs, pre_wait=pw, ev_pre_wait=epw or None)

        load_toks(0, extra=[(s_sc.s, sc_total)])
        cast_idx(0)
        load_toks(1)
        cast_idx(1)
        emit_wchunks(0, 0, 36)
        emit_wchunks(1, 0, 36)
        gather(0, first=True)
        gather(1)
        emit_xgtr(0)

        for i in range(NTILE):
            e, c = i // CAP_T, i % CAP_T
            es = e % 2

            if c == 0 and e >= 1 and e + 1 < NEXP:
                load_toks(e + 1, extra=[(s_ye.s, ye_base + 4 * (5 * e)),
                                        (s_ic.s, ic_v[e])])
            if c == 1 and e >= 1 and e + 1 < NEXP:
                cast_idx(e + 1, extra=[(s_sa2[(5 * e - 1) % 2].s, sa_v[5 * e - 1])])

            def _gu_pe(t, i=i, es=es, e=e):
                t.wait_ge(s_tev.s, xgtr_ev[i])
                t.wait_ge(s_cw.s, gu_rdy[e])
                if i >= 1:
                    t.wait_ge(s_sil.s, sil_v[i - 1])
                last = None
                for c16 in range(NC_):
                    lhs = xgT[i % 2][:, c16 * 128:(c16 + 1) * 128]
                    b = c16 * 1536
                    t.matmul(ps_gate[:, 0:512], lhs, wgu_sb[es][:, b:b + 512],
                             start=(c16 == 0), stop=False)
                    last = t.matmul(ps_gate[:, 512:768], lhs, wgu_sb[es][:, b + 512:b + 768],
                                    start=(c16 == 0), stop=(c16 == NC_ - 1))
                last.then_inc(s_pe.s, 1)
                if i >= 1:
                    t.wait_ge(s_h1.s, h1_v[i - 1])
                last = None
                for c16 in range(NC_):
                    lhs = xgT[i % 2][:, c16 * 128:(c16 + 1) * 128]
                    b = c16 * 1536 + 768
                    t.matmul(ps_up[:, 0:512], lhs, wgu_sb[es][:, b:b + 512],
                             start=(c16 == 0), stop=False)
                    last = t.matmul(ps_up[:, 512:768], lhs, wgu_sb[es][:, b + 512:b + 768],
                                    start=(c16 == 0), stop=(c16 == NC_ - 1))
                last.then_inc(s_pe.s, 1)
            block.tensor(_gu_pe)
            s_pe.v += 2
            gate_pe[i] = s_pe.v - 1
            up_pe[i] = s_pe.v

            def _sil_a(s, i=i):
                s.wait_ge(s_pe.s, gate_pe[i])
                if i >= 1:
                    s.wait_ge(s_h1.s, h1_v[i - 1])
                s.activation(Hg[:], ps_gate[:], AF.Silu).then_inc(s_sil.s, 1)
            block.scalar(_sil_a)
            s_sil.v += 1
            sil_v[i] = s_sil.v

            if i >= 1:
                ip = i - 1
                ep, cp = ip // CAP_T, ip % CAP_T
                def _ye_v(v, ip=ip, ep=ep, cp=cp):
                    if ip >= 2:
                        v.wait_ge(s_sa2[ip % 2].s, sa_v[ip - 2])
                    tk2 = tk_sb[ep % 2][:].rearrange("p (c w) -> p c w", w=2)
                    for j in range(4):
                        v.wait_ge(s_pe2.s, pe2_base + 4 * ip + j + 1)
                        v.tensor_scalar(ye[ip % 2][:, j * 512:(j + 1) * 512],
                                        ps_d[j % 2][:], tk2[:, cp, 1:2], None,
                                        op0=ALU.mult).then_inc(s_ye.s, 1)
                block.vector(_ye_v)
                s_ye.v += 4

            def _h1_v(v, i=i):
                v.wait_ge(s_sil.s, sil_v[i])
                v.wait_ge(s_pe.s, up_pe[i])
                v.tensor_tensor(H1[:], ps_up[:], Hg[:], ALU.mult).then_inc(s_h1.s, 1)
            block.vector(_h1_v)
            s_h1.v += 1
            h1_v[i] = s_h1.v

            if i + 1 < NTILE:
                emit_xgtr(i + 1)

            epw = []
            if i >= 2:
                epw.append((s_pe2.s, dn_pe2[i - 2]))
            pairs = [(H1[:, c2 * 128:(c2 + 1) * 128], H1T[i % 2][:, c2 * 128:(c2 + 1) * 128])
                     for c2 in range(NIB)]
            htr_ev[i] = emit_transposes(
                pairs, pre_wait=[(s_h1.s, h1_v[i])], ev_pre_wait=epw or None)

            def _dn_pe(t, i=i, es=es, e=e):
                t.wait_ge(s_tev.s, htr_ev[i])
                t.wait_ge(s_cw.s, dn_rdy[e])
                for j in range(4):
                    gidx = 4 * i + j
                    if gidx >= 2:
                        t.wait_ge(s_ye.s, ye_base + gidx - 1)
                    last = None
                    for ib in range(NIB):
                        last = t.matmul(ps_d[j % 2][:],
                                        H1T[i % 2][:, ib * 128:(ib + 1) * 128],
                                        wd_sb[es][:, ib * 2048 + j * 512: ib * 2048 + (j + 1) * 512],
                                        start=(ib == 0), stop=(ib == NIB - 1))
                    last.then_inc(s_pe2.s, 1)
            block.tensor(_dn_pe)
            s_pe2.v += 4
            dn_pe2[i] = s_pe2.v

            if e >= 1 and e + 1 < NEXP:
                rngs = [(0, 8), (8, 16), (16, 24), (24, 32), (32, 36)]
                emit_wchunks(e + 1, *rngs[c])

            if i + 2 < NTILE:
                gather(i + 2)
            if i >= 1:
                sa(i - 1)

        ip = NTILE - 1
        ep, cp = ip // CAP_T, ip % CAP_T
        def _ye_last(v, ip=ip, ep=ep, cp=cp):
            v.wait_ge(s_sa2[ip % 2].s, sa_v[ip - 2])
            tk2 = tk_sb[ep % 2][:].rearrange("p (c w) -> p c w", w=2)
            for j in range(4):
                v.wait_ge(s_pe2.s, pe2_base + 4 * ip + j + 1)
                v.tensor_scalar(ye[ip % 2][:, j * 512:(j + 1) * 512],
                                ps_d[j % 2][:], tk2[:, cp, 1:2], None,
                                op0=ALU.mult).then_inc(s_ye.s, 1)
        block.vector(_ye_last)
        s_ye.v += 4
        sa(NTILE - 1)

        def _fin(g, sv0=s_sa2[0].v, sv1=s_sa2[1].v, outv=s_out.v):
            g.wait_ge(s_sa2[0].s, sv0)
            g.wait_ge(s_sa2[1].s, sv1)
            g.wait_ge(s_out.s, outv)
        block.gpsimd(_fin)
        ctxD.close()

    ctx.close()
    return nc


def _prep_inputs(L0_attn_x, L0_attn_Wq, L0_attn_bq, L0_attn_Wk, L0_attn_bk,
                 L0_attn_Wv, L0_attn_bv, L0_ffn_router_w, L0_ffn_w_gate,
                 L0_ffn_w_up, L0_ffn_w_down):
    f = np.ascontiguousarray
    x = np.asarray(L0_attn_x, np.float32)
    wq = np.asarray(L0_attn_Wq, np.float32)
    bq = np.asarray(L0_attn_bq, np.float32)
    wk = np.asarray(L0_attn_Wk, np.float32)
    bk = np.asarray(L0_attn_bk, np.float32)
    wv = np.asarray(L0_attn_Wv, np.float32)
    bv = np.asarray(L0_attn_bv, np.float32)
    rw = np.asarray(L0_ffn_router_w, np.float32)
    wg = np.asarray(L0_ffn_w_gate, np.float32)
    wu = np.asarray(L0_ffn_w_up, np.float32)
    wd = np.asarray(L0_ffn_w_down, np.float32)

    wq_s = f(wq.transpose(1, 0, 2).reshape(2048, 2048).reshape(NC_, 128, 2048).transpose(1, 0, 2))
    wqb_s = f(bq.reshape(1, 2048))
    wk_s = f(wk.reshape(G, NC_, 128, 128).transpose(2, 0, 1, 3).reshape(128, G * NC_ * 128))
    bkT_s = f(bk.T)
    wv_s = f(wv.transpose(1, 0, 2).reshape(2048, 512).reshape(NC_, 128, 512).transpose(1, 0, 2).reshape(128, NC_ * 512))
    bv_s = f(bv.reshape(1, 512))
    rw_pad = np.zeros((2048, 128), np.float32)
    rw_pad[:, :32] = rw.T
    rw_s = f(rw_pad.reshape(NC_, 128, 128).transpose(1, 0, 2).reshape(128, NC_ * 128))
    ident = np.eye(128, dtype=ml_dtypes.bfloat16)

    wgt = wg.reshape(NEXP, 16, 128, I).transpose(0, 2, 1, 3)
    wut = wu.reshape(NEXP, 16, 128, I).transpose(0, 2, 1, 3)
    wgu_s = f(np.concatenate([wgt, wut], axis=3).reshape(NEXP, 128, 16 * 1536))
    wd_s = f(wd.reshape(NEXP, NIB, 128, 2048).transpose(0, 2, 1, 3).reshape(NEXP, 128, NIB * 2048))

    tri_c = f(np.triu(np.ones((128, 128), np.float32)).astype(ml_dtypes.bfloat16))
    ones_c = np.ones((128, 128), ml_dtypes.bfloat16)
    ecap_c = f(np.broadcast_to(np.tile(np.arange(NEXP, dtype=np.float32) * CAP - 1.0, NOWN),
                               (128, NOWN * 32)).copy())
    tokid_c = f((np.arange(NOWN, dtype=np.float32)[None, :] * 128
                 + np.arange(128, dtype=np.float32)[:, None]))

    kk_ar = np.arange(128)
    masks_by_r = []
    for r in range(4):
        m = np.zeros((NOWN * 4, 128, 128), np.float32)
        for k in range(NOWN):
            qpos = (4 * k + r) * 128 + kk_ar
            for jj in range(4):
                kpos = (4 * k + jj) * 128 + kk_ar
                m[k * 4 + jj] = (kpos[:, None] <= qpos[None, :])
        masks_by_r.append(f(m.astype(ml_dtypes.bfloat16).transpose(1, 0, 2).reshape(128, NOWN * 4 * 128)))

    shared = dict(wq_s=wq_s, wqb_s=wqb_s, wk_s=wk_s, bkT_s=bkT_s, wv_s=wv_s,
                  bv_s=bv_s, rw_s=rw_s, wgu_in=wgu_s, wd_in=wd_s, ident=ident,
                  tri_c=tri_c, ones_c=ones_c, ecap_c=ecap_c, tokid_c=tokid_c)
    in_maps = []
    for c in range(8):
        b, r = c // 4, c % 4
        xb = x[b].reshape(NT, 128, 2048)
        m = dict(shared)
        m["x_own"] = f(xb[r::4])
        m["x_full"] = f(xb)
        m["masks"] = masks_by_r[r]
        in_maps.append(m)
    return in_maps


_NC_CACHE = {}


def kernel(**inputs):
    key = (5, False)
    if key not in _NC_CACHE:
        _NC_CACHE[key] = build(phases=4, dbg=False)
    nc = _NC_CACHE[key]
    in_maps = _prep_inputs(**inputs)
    res = run_bass_kernel_spmd(nc, in_maps, core_ids=list(range(8)))
    out = np.empty((B, S, E), np.float32)
    for c in range(8):
        b, r = c // 4, c % 4
        out[b].reshape(NT, 128, 2048)[r::4] = res.results[c]["y"][:NOWN * 128].reshape(NOWN, 128, E)
    return out


# version 10
# speedup vs baseline: 1.0659x; 1.0659x over previous
import math
import numpy as np
import ml_dtypes
from contextlib import ExitStack

import concourse.bass as bass
import concourse.mybir as mybir
from concourse.bass_utils import run_bass_kernel_spmd

F32 = mybir.dt.float32
BF16 = mybir.dt.bfloat16
U32 = mybir.dt.uint32
AF = mybir.ActivationFunctionType
ALU = mybir.AluOpType
AX = mybir.AxisListType
IOoA = bass.IndirectOffsetOnAxis

B, S, E = 2, 8192, 2048
G, D = 4, 128
NEXP, I = 32, 768
NT = S // 128
NOWN = 16
NC_ = E // 128
NIB = I // 128
CSH = 20.0
ALPHA = 1.0 / math.sqrt(D)
VST = 132
MOFF = [sum(4 * kk + 4 for kk in range(k)) for k in range(NOWN + 1)]
CAP_T = 5
CAP = CAP_T * 128
NSLOT = NEXP * CAP
NTILE = NEXP * CAP_T
TOKS_ROWS = NSLOT + 128
DUMMY_TOK = 2048.0
YROWS = NOWN * 128 + 128


class Sem:
    def __init__(self, ctx, nc, name):
        self.s = ctx.enter_context(nc.semaphore(name))
        self.v = 0


def build(phases=4, dbg=False, c_slots=NOWN):
    nc = bass.Bass("TRN2", target_bir_lowering=False, debug=False)

    def din(name, shape, dt):
        return nc.dram_tensor(name, shape, dt, kind="ExternalInput").ap()

    def dscr(name, shape, dt):
        kind = "ExternalOutput" if dbg else "Internal"
        return nc.dram_tensor(name, shape, dt, kind=kind).ap()

    x_own = din("x_own", [NOWN, 128, E], F32)
    x_full = din("x_full", [NT, 128, E], F32)
    wq_s = din("wq_s", [128, NC_, 2048], F32)
    wqb_s = din("wqb_s", [1, 2048], F32)
    wk_s = din("wk_s", [128, G * NC_ * 128], F32)
    bkT_s = din("bkT_s", [128, G], F32)
    wv_s = din("wv_s", [128, NC_ * 512], F32)
    bv_s = din("bv_s", [1, 512], F32)
    rw_s = din("rw_s", [128, NC_ * 128], F32)
    wgu_in = din("wgu_in", [NEXP, 128, 16 * 1536], F32)
    wd_in = din("wd_in", [NEXP, 128, NIB * 2048], F32)
    masks = din("masks", [128, NOWN * 4 * 128], BF16)
    ident = din("ident", [128, 128], BF16)
    tri_c = din("tri_c", [128, 128], BF16)
    ones_c = din("ones_c", [128, 128], BF16)
    ecap_c = din("ecap_c", [128, NOWN * 32], F32)
    tokid_c = din("tokid_c", [128, NOWN], F32)

    qT_scr = dscr("qT_scr", [NOWN, 128, 2048], BF16)
    xn_scr = dscr("xn_scr", [YROWS, E], BF16)
    toks2 = dscr("toks2", [TOKS_ROWS, 2], F32)
    xg_scr = dscr("xg_scr", [NTILE * 128, E], BF16)
    y = nc.dram_tensor("y", [YROWS, E], F32, kind="ExternalOutput").ap()

    ctx = ExitStack()
    EE = ctx.enter_context

    def sb(name, shape, dt, side="left"):
        return nc.sbuf_tensor(name, shape, dt, side=side)

    consts = EE(sb("consts", [128, 8], F32))
    ones_col = EE(sb("ones_col", [1, 128], BF16))
    id_sb = EE(sb("id_sb", [128, 128], BF16))
    w_sb = EE(sb("w_sb", [128, NOWN * 32], F32))
    stat6 = EE(sb("stat6", [128, 24], F32))
    mv = EE(sb("mv", [128, 8], F32))
    pay8 = EE(sb("pay8", [128, NOWN * 8 * 2], F32))
    pos8_u = EE(sb("pos8_u", [128, NOWN * 8], U32))

    s_x = Sem(ctx, nc, "s_x")
    s_g = Sem(ctx, nc, "s_g")
    s_out = Sem(ctx, nc, "s_out")
    s_pe = Sem(ctx, nc, "s_pe")
    s_tr = Sem(ctx, nc, "s_tr")
    s_pe2 = Sem(ctx, nc, "s_pe2")
    s_dve = Sem(ctx, nc, "s_dve")
    s_ev = Sem(ctx, nc, "s_ev")
    s_act = Sem(ctx, nc, "s_act")
    s_ti = Sem(ctx, nc, "s_ti")
    s_tev = Sem(ctx, nc, "s_tev")
    s_xw = Sem(ctx, nc, "s_xw")
    s_sc = Sem(ctx, nc, "s_sc")

    block = EE(nc.Block())
    ALLSEMS = [s_x, s_g, s_out, s_pe, s_tr, s_pe2, s_dve, s_ev, s_act, s_ti, s_tev, s_xw]

    def barrier():
        vals = [(sm.s, sm.v) for sm in ALLSEMS]
        for name in ("sync", "tensor", "vector", "scalar", "gpsimd"):
            def _f(eng, vals=vals):
                for s, v in vals:
                    if v > 0:
                        eng.wait_ge(s, v)
            getattr(block, name)(_f)

    def emit_ln(x_ready_val, xb, xn_f=None, xn_lo=None, xn_hi=None, extra_wait=None):
        def _v1(v, xv=x_ready_val):
            v.wait_ge(s_x.s, xv)
            for i in range(3):
                v.bn_stats(stat6[:, i * 6:(i + 1) * 6], xb[:, i * 512:(i + 1) * 512])
            v.bn_stats(stat6[:, 18:24], xb[:, 1536:2048]).then_inc(s_dve.s, 1)
            v.wait_ge(s_dve.s, s_dve.v + 1)
            v.bn_aggr(mv[:, 0:2], stat6[:]).then_inc(s_dve.s, 1)
        block.vector(_v1)
        s_dve.v += 2
        mv_ready = s_dve.v

        def _a1(s, mvv=mv_ready):
            s.wait_ge(s_dve.s, mvv)
            s.activation(mv[:, 2:3], mv[:, 1:2], AF.Ln, bias=consts[:, 2:3]).then_inc(s_act.s, 1)
            s.wait_ge(s_act.s, s_act.v + 1)
            s.activation(mv[:, 3:4], mv[:, 2:3], AF.Exp, scale=consts[:, 3:4]).then_inc(s_act.s, 1)
        block.scalar(_a1)
        s_act.v += 2
        rstd_ready = s_act.v

        def _v2(v, rv=rstd_ready, ew=extra_wait):
            v.wait_ge(s_act.s, rv)
            if ew:
                for sm, vv in ew:
                    if vv > 0:
                        v.wait_ge(sm, vv)
            if xn_lo is not None:
                v.tensor_scalar(xn_hi[:], xb[:], mv[:, 0:1], mv[:, 3:4], op0=ALU.subtract, op1=ALU.mult)
                v.tensor_scalar(xn_f[:], xb[:], mv[:, 0:1], mv[:, 3:4], op0=ALU.subtract, op1=ALU.mult).then_inc(s_dve.s, 1)
                v.wait_ge(s_dve.s, s_dve.v + 1)
                v.tensor_sub(xn_lo[:], xn_f[:], xn_hi[:]).then_inc(s_dve.s, 1)
            else:
                v.tensor_scalar(xn_hi[:], xb[:], mv[:, 0:1], mv[:, 3:4], op0=ALU.subtract, op1=ALU.mult).then_inc(s_dve.s, 1)
        block.vector(_v2)
        s_dve.v += 2 if xn_lo is not None else 1
        return s_dve.v

    tr_parity = [0]
    ps_tr = None

    def emit_transposes(pairs, pre_wait=None, ev_pre_wait=None):
        n = len(pairs)
        tr0 = s_tr.v

        def _pe(t, pre_wait=pre_wait, tr0=tr0):
            if pre_wait:
                for sm, vv in pre_wait:
                    if vv > 0:
                        t.wait_ge(sm, vv)
            for i, (src, dst) in enumerate(pairs):
                gi = tr0 + i
                if gi >= 2:
                    t.wait_ge(s_tev.s, gi - 1)
                t.transpose(ps_tr[(tr_parity[0] + i) % 2][:], src, id_sb[:]).then_inc(s_tr.s, 1)
        block.tensor(_pe)
        s_tr.v += n

        def _ve(v, tr0=tr0, par=tr_parity[0], epw=ev_pre_wait):
            if epw:
                for sm, vv in epw:
                    if vv > 0:
                        v.wait_ge(sm, vv)
            for i, (src, dst) in enumerate(pairs):
                v.wait_ge(s_tr.s, tr0 + i + 1)
                v.tensor_copy(dst, ps_tr[(par + i) % 2][:]).then_inc(s_tev.s, 1)
        block.vector(_ve)
        s_tev.v += n
        tr_parity[0] = (tr_parity[0] + n) % 2
        return s_tev.v

    def _pro_v(v):
        v.memset(consts[:, 0:1], -CSH)
        v.memset(consts[:, 1:2], 1.0 / 2048.0)
        v.memset(consts[:, 2:3], 1e-5)
        v.memset(consts[:, 3:4], -0.5)
        v.memset(consts[:, 4:5], DUMMY_TOK)
        v.memset(ones_col[:], 1.0).then_inc(s_dve.s, 1)
    block.vector(_pro_v)
    s_dve.v += 1
    pro_dve = s_dve.v

    ctxAB = ExitStack()
    if phases >= 1:
        ctxA = ExitStack()
        EA = ctxA.enter_context
        xbuf = [ctxAB.enter_context(sb(f"xbuf{i}", [128, E], F32, side="right")) for i in range(2)]
        xn_hi = ctxAB.enter_context(sb("xn_hi", [128, E], BF16, side="right"))
        wq_sb = EA(sb("wq_sb", [128, NC_ * 2048], BF16))
        wqb_sb = EA(sb("wqb_sb", [1, 2048], BF16))
        rw_f = EA(sb("rw_f", [128, NC_ * 128], F32))
        rw_hi = EA(sb("rw_hi", [128, NC_ * 128], BF16))
        rw_lo = EA(sb("rw_lo", [128, NC_ * 128], BF16))
        xn_f = EA(sb("xn_f", [128, E], F32))
        xn_lo = EA(sb("xn_lo", [128, E], BF16))
        xnT_sb = EA(sb("xnT_sb", [128, E], BF16))
        xnT_lo_sb = EA(sb("xnT_lo_sb", [128, E], BF16))
        qbf = EA(sb("qbf", [128, 2048], BF16))
        qt_sb = EA(sb("qt_sb", [128, 2048], BF16))
        pr_p = EA(sb("pr_p", [128, 32], F32))
        pr_m8 = EA(sb("pr_m8", [128, 8], F32))
        pr_zap = EA(sb("pr_zap", [128, 32], F32))
        pr_s = EA(sb("pr_s", [128, 2], F32))
        tri_sb = EA(sb("tri_sb", [128, 128], BF16))
        onesq_sb = EA(sb("onesq_sb", [128, 128], BF16))
        ecap_sb = EA(sb("ecap_sb", [128, NOWN * 32], F32))
        tokid_sb = EA(sb("tokid_sb", [128, NOWN], F32))
        init_sb = EA(sb("init_sb", [128, (TOKS_ROWS // 128) * 2], F32))
        a_bf = EA(sb("a_bf", [128, NOWN * 32], BF16))
        a_f = EA(sb("a_f", [128, NOWN * 32], F32))
        tmp512 = EA(sb("tmp512", [128, NOWN * 32], F32))
        cA = EA(sb("cA", [128, NOWN * 32], F32))
        cB = EA(sb("cB", [128, NOWN * 32], F32))
        mjt = EA(sb("mjt", [128, NOWN * 32], F32))
        mj2 = EA(sb("mj2", [128, NOWN * 32], F32))
        pos8_f = EA(sb("pos8_f", [128, NOWN * 8], F32))
        ps_tr = [EA(nc.psum_tensor(f"ps_trA{i}", [128, 128], BF16)) for i in range(2)]
        ps_q = EA(nc.psum_tensor("ps_q", [128, 2048], F32))
        ps_r = EA(nc.psum_tensor("ps_r", [128, 32], F32))
        ps_rank = EA(nc.psum_tensor("ps_rank", [128, NOWN * 32], F32))

        def _a_g(g):
            g.dma_start(wq_sb[:].rearrange("p (c n) -> p c n", c=NC_), wq_s).then_inc(s_g.s, 16)
            g.dma_start(wqb_sb[:], wqb_s).then_inc(s_g.s, 16)
            g.dma_start(id_sb[:], ident).then_inc(s_g.s, 16)
            g.dma_start(tri_sb[:], tri_c).then_inc(s_g.s, 16)
            g.dma_start(onesq_sb[:], ones_c).then_inc(s_g.s, 16)
            g.dma_start(ecap_sb[:], ecap_c).then_inc(s_g.s, 16)
            g.dma_start(tokid_sb[:], tokid_c).then_inc(s_g.s, 16)
        block.gpsimd(_a_g)
        s_g.v += 112
        gA_ready = s_g.v
        id_ready = s_g.v

        def _a_s(sy):
            sy.dma_start(rw_f[:], rw_s).then_inc(s_x.s, 16)
            sy.dma_start(xbuf[0][:], x_own[0]).then_inc(s_x.s, 16)
        block.sync(_a_s)
        s_x.v += 32

        def _a_v0(v):
            v.memset(qbf[:], 0.0).then_inc(s_dve.s, 1)
            v.memset(init_sb[:], 0.0).then_inc(s_dve.s, 1)
            v.wait_ge(s_dve.s, s_dve.v + 2)
            iv = init_sb[:].rearrange("p (r w) -> p r w", w=2)
            v.tensor_copy(iv[:, :, 0:1],
                          consts[:, 4:5].rearrange("p (a b) -> p a b", b=1)
                          .to_broadcast([128, TOKS_ROWS // 128, 1])).then_inc(s_dve.s, 1)
        block.vector(_a_v0)
        s_dve.v += 3
        init_done = s_dve.v

        def _a_g0(g, idn=init_done):
            g.wait_ge(s_dve.s, idn)
            g.dma_start(toks2.rearrange("(p r) w -> p (r w)", p=128), init_sb[:]).then_inc(s_ti.s, 16)
            g.dma_start(xn_scr[NOWN * 128:YROWS, :], qbf[:]).then_inc(s_ti.s, 16)
        block.gpsimd(_a_g0)
        s_ti.v += 32
        ti_done = s_ti.v

        def _a_v(v, xv=s_x.v):
            v.wait_ge(s_x.s, xv)
            v.tensor_copy(rw_hi[:], rw_f[:]).then_inc(s_dve.s, 1)
            v.wait_ge(s_dve.s, s_dve.v + 1)
            v.tensor_sub(rw_lo[:], rw_f[:], rw_hi[:]).then_inc(s_dve.s, 1)
        block.vector(_a_v)
        s_dve.v += 2
        rw_ready = s_dve.v

        prp_done_prev = 0
        w_done_prev = 0
        qt_out_prev = 0
        q_done_prev = 0
        xnw_prev = 0
        for k in range(NOWN):
            ew = [(s_xw.s, xnw_prev)] if xnw_prev else None
            ln_done = emit_ln(s_x.v, xbuf[k % 2], xn_f=xn_f, xn_lo=xn_lo, xn_hi=xn_hi,
                              extra_wait=ew)

            if k + 1 < NOWN:
                def _nx(sy, k=k, need=ln_done):
                    sy.wait_ge(s_dve.s, need)
                    sy.dma_start(xbuf[(k + 1) % 2][:], x_own[k + 1]).then_inc(s_x.s, 16)
                block.sync(_nx)
                s_x.v += 16

            def _xnw(g, k=k, need=ln_done):
                g.wait_ge(s_dve.s, need)
                g.dma_start(xn_scr[k * 128:(k + 1) * 128, :], xn_hi[:]).then_inc(s_xw.s, 16)
            block.gpsimd(_xnw)
            s_xw.v += 16
            xnw_prev = s_xw.v

            pw = [(s_dve.s, ln_done), (s_g.s, id_ready)]
            if qt_out_prev:
                pw.append((s_out.s, qt_out_prev))
            if q_done_prev:
                pw.append((s_pe.s, q_done_prev))
            pairs = [(xn_hi[:, c * 128:(c + 1) * 128], xnT_sb[:, c * 128:(c + 1) * 128]) for c in range(NC_)]
            pairs += [(xn_lo[:, c * 128:(c + 1) * 128], xnT_lo_sb[:, c * 128:(c + 1) * 128]) for c in range(NC_)]
            ev_hi_lo = emit_transposes(pairs, pre_wait=pw)

            def _q_pe(t, k=k, ev=ev_hi_lo, pd=prp_done_prev):
                t.wait_ge(s_tev.s, ev)
                if k == 0:
                    t.wait_ge(s_g.s, gA_ready)
                    t.wait_ge(s_dve.s, rw_ready)
                if pd:
                    t.wait_ge(s_act.s, pd)
                for c in range(NC_):
                    lhs = xnT_sb[:, c * 128:(c + 1) * 128]
                    for n4 in range(4):
                        t.matmul(ps_q[:, n4 * 512:(n4 + 1) * 512], lhs,
                                 wq_sb[:, c * 2048 + n4 * 512: c * 2048 + (n4 + 1) * 512],
                                 start=(c == 0), stop=False)
                    t.matmul(ps_r[:], lhs, rw_hi[:, c * 128:c * 128 + 32], start=(c == 0), stop=False)
                    t.matmul(ps_r[:], lhs, rw_lo[:, c * 128:c * 128 + 32], start=False, stop=False)
                    mm = t.matmul(ps_r[:], xnT_lo_sb[:, c * 128:(c + 1) * 128],
                                  rw_hi[:, c * 128:c * 128 + 32], start=False, stop=(c == NC_ - 1))
                    if c == NC_ - 1:
                        mm.then_inc(s_pe.s, 1)
                last = None
                for n4 in range(4):
                    last = t.matmul(ps_q[:, n4 * 512:(n4 + 1) * 512], ones_col[0:1, :],
                                    wqb_sb[0:1, n4 * 512:(n4 + 1) * 512], start=False, stop=True)
                last.then_inc(s_pe.s, 1)
            block.tensor(_q_pe)
            s_pe.v += 2
            q_done = s_pe.v
            q_done_prev = q_done

            def _q_v(v, k=k, qd=q_done):
                v.wait_ge(s_pe.s, qd)
                if k == 0:
                    v.wait_ge(s_ti.s, ti_done)
                v.tensor_scalar(qbf[:], ps_q[:], ALPHA, None, op0=ALU.mult).then_inc(s_dve.s, 1)
            block.vector(_q_v)
            s_dve.v += 1
            qbf_done = s_dve.v

            pairs = [(qbf[:, c * 128:(c + 1) * 128], qt_sb[:, c * 128:(c + 1) * 128]) for c in range(NC_)]
            ev_qt = emit_transposes(pairs, pre_wait=[(s_dve.s, qbf_done)])

            def _qtw(g, k=k, ev=ev_qt):
                g.wait_ge(s_tev.s, ev)
                g.dma_start(qT_scr[k], qt_sb[:]).then_inc(s_out.s, 16)
            block.gpsimd(_qtw)
            s_out.v += 16
            qt_out_prev = s_out.v

            def _r_a(s, qd=q_done, wdp=w_done_prev):
                s.wait_ge(s_pe.s, qd - 1)
                if wdp:
                    s.wait_ge(s_dve.s, wdp)
                s.activation(pr_p[:], ps_r[:], AF.Exp, bias=consts[:, 0:1]).then_inc(s_act.s, 1)
            block.scalar(_r_a)
            s_act.v += 1
            prp_done_prev = s_act.v

            def _r_v(v, k=k, pv=s_act.v):
                v.wait_ge(s_act.s, pv)
                v.max(out=pr_m8[:], in_=pr_p[:]).then_inc(s_dve.s, 1)
                v.wait_ge(s_dve.s, s_dve.v + 1)
                v.match_replace(out=pr_zap[:], in_to_replace=pr_m8[:], in_values=pr_p[:], imm_value=0.0).then_inc(s_dve.s, 1)
                v.wait_ge(s_dve.s, s_dve.v + 2)
                v.tensor_sub(pr_zap[:], pr_p[:], pr_zap[:]).then_inc(s_dve.s, 1)
                v.wait_ge(s_dve.s, s_dve.v + 3)
                v.reduce_sum(pr_s[:, 0:1], pr_zap[:], axis=AX.X).then_inc(s_dve.s, 1)
                v.wait_ge(s_dve.s, s_dve.v + 4)
                v.reciprocal(pr_s[:, 1:2], pr_s[:, 0:1]).then_inc(s_dve.s, 1)
                v.wait_ge(s_dve.s, s_dve.v + 5)
                v.tensor_scalar(w_sb[:, k * 32:(k + 1) * 32], pr_zap[:], pr_s[:, 1:2], None,
                                op0=ALU.mult).then_inc(s_dve.s, 1)
            block.vector(_r_v)
            s_dve.v += 6
            w_done_prev = s_dve.v

        def _ri_v(v, wd=w_done_prev):
            v.wait_ge(s_dve.s, wd)
            v.tensor_scalar(a_bf[:], w_sb[:], 0.0, None, op0=ALU.is_gt).then_inc(s_dve.s, 1)
            v.wait_ge(s_dve.s, s_dve.v + 1)
            v.tensor_scalar(a_f[:], w_sb[:], 0.0, None, op0=ALU.is_gt).then_inc(s_dve.s, 1)
        block.vector(_ri_v)
        s_dve.v += 2
        a_done = s_dve.v

        def _rk_pe(t, ad=a_done):
            t.wait_ge(s_dve.s, ad)
            last = None
            for k in range(NOWN):
                for kp in range(k + 1):
                    last = t.matmul(ps_rank[:, k * 32:(k + 1) * 32],
                                    tri_sb[:] if kp == k else onesq_sb[:],
                                    a_bf[:, kp * 32:(kp + 1) * 32],
                                    start=(kp == 0), stop=(kp == k))
            last.then_inc(s_pe.s, 1)
        block.tensor(_rk_pe)
        s_pe.v += 1
        rank_done = s_pe.v

        def _pos_v(v, rd=rank_done):
            base = s_dve.v
            cnt = [0]
            def stepped(ins):
                v.wait_ge(s_dve.s, base + cnt[0])
                ins().then_inc(s_dve.s, 1)
                cnt[0] += 1
            v.wait_ge(s_pe.s, rd)
            v.wait_ge(s_g.s, gA_ready)
            stepped(lambda: v.tensor_tensor(tmp512[:], ps_rank[:], ecap_sb[:], ALU.add))
            stepped(lambda: v.tensor_copy(cA[:], a_f[:]))
            src, dst = cA, cB
            for sh in (1, 2, 4, 8, 16):
                s3 = src[:].rearrange("p (k e) -> p k e", e=32)
                d3 = dst[:].rearrange("p (k e) -> p k e", e=32)
                v.wait_ge(s_dve.s, base + cnt[0])
                v.tensor_copy(d3[:, :, 0:sh], s3[:, :, 0:sh])
                v.tensor_tensor(d3[:, :, sh:32], s3[:, :, sh:32],
                                s3[:, :, 0:32 - sh], ALU.add).then_inc(s_dve.s, 1)
                cnt[0] += 1
                src, dst = dst, src
            pos3 = pos8_f[:].rearrange("p (k j) -> p k j", j=8)
            pay4 = pay8[:].rearrange("p (k j w) -> p k j w", j=8, w=2)
            for j in range(8):
                stepped(lambda j=j: v.tensor_scalar(mjt[:], src[:], float(j + 1), None, op0=ALU.is_equal))
                stepped(lambda: v.tensor_tensor(mjt[:], mjt[:], a_f[:], ALU.mult))
                stepped(lambda: v.tensor_tensor(mj2[:], mjt[:], tmp512[:], ALU.mult))
                stepped(lambda j=j: v.reduce_sum(pos3[:, :, j:j + 1],
                                                 mj2[:].rearrange("p (k e) -> p k e", e=32), axis=AX.X))
                stepped(lambda: v.tensor_tensor(mj2[:], mjt[:], w_sb[:], ALU.mult))
                stepped(lambda j=j: v.reduce_sum(pay4[:, :, j, 1:2],
                                                 mj2[:].rearrange("p (k e) -> p k e", e=32), axis=AX.X))
            tid = tokid_sb[:].rearrange("p (a b) -> p a b", b=1)
            for k in range(NOWN):
                stepped(lambda k=k: v.tensor_copy(pay4[:, k, :, 0:1],
                                                  tid[:, k:k + 1, :].to_broadcast([128, 8, 1])))
            stepped(lambda: v.tensor_copy(pos8_u[:], pos8_f[:]))
        block.vector(_pos_v)
        s_dve.v += 2 + 5 + 48 + 16 + 1
        pay_done = s_dve.v

        def emit_scatters():
            def _sc_g(g, pd=pay_done, tid=ti_done):
                g.wait_ge(s_ti.s, tid)
                g.wait_ge(s_dve.s, pd)
                for col in range(NOWN * 8):
                    g.indirect_dma_start(
                        toks2, IOoA(pos8_u[:, col:col + 1], 0),
                        pay8[:, col * 2:col * 2 + 2], None,
                    ).then_inc(s_sc.s, 16)
            block.gpsimd(_sc_g)
            s_sc.v += NOWN * 8 * 16
            return s_sc.v

        barrier()
        ctxA.close()

    ctxBC = ExitStack()
    if phases >= 2:
        KT_sb = [ctxBC.enter_context(sb(f"KT{g}", [128, S], BF16)) for g in range(G)]
        V_sb = ctxBC.enter_context(sb("V_sb", [128, G * NT * VST], BF16))
        ctxB = ExitStack()
        EB = ctxB.enter_context
        wk_sb = EB(sb("wk_sb", [128, G * NC_ * 128], BF16, side="right"))
        wv_sb = EB(sb("wv_sb", [128, NC_ * 512], BF16, side="right"))
        bkT_sb = EB(sb("bkT_sb", [128, G], F32, side="right"))
        bv_sb = EB(sb("bv_sb", [1, 512], BF16, side="right"))
        xnT4 = EB(sb("xnT4", [128, NC_ * 512], BF16, side="right"))
        ps_tr = [EB(nc.psum_tensor(f"ps_trB{i}", [128, 128], BF16)) for i in range(2)]
        ps_kt = [EB(nc.psum_tensor(f"ps_kt{i}", [128, 512], F32)) for i in range(2)]
        ps_v = [EB(nc.psum_tensor(f"ps_v{i}", [128, 512], F32)) for i in range(2)]
        tr_parity[0] = 0

        def _pro_vb(v):
            vv = V_sb[:].rearrange("p (t c) -> p t c", c=VST)
            v.memset(vv[:, :, 128:129], 1.0).then_inc(s_dve.s, 1)
        block.vector(_pro_vb)
        s_dve.v += 1
        vones_ready = s_dve.v

        def _b_g(g):
            g.dma_start(wk_sb[:], wk_s).then_inc(s_g.s, 16)
            g.dma_start(wv_sb[:], wv_s).then_inc(s_g.s, 16)
            g.dma_start(bv_sb[:], bv_s).then_inc(s_g.s, 16)
        block.gpsimd(_b_g)
        s_g.v += 48
        gB_ready = s_g.v

        sc_total = emit_scatters()

        def _b_s(sy):
            sy.dma_start(bkT_sb[:], bkT_s).then_inc(s_x.s, 16)
            sy.dma_start(xbuf[0][:], x_full[0]).then_inc(s_x.s, 16)
        block.sync(_b_s)
        s_x.v += 32
        bkT_val = s_x.v

        kt_ev = []
        v_ev = []
        for t4 in range(NT // 4):
            for ti in range(4):
                tt = 4 * t4 + ti
                ln_done = emit_ln(s_x.v, xbuf[tt % 2], xn_hi=xn_hi)
                if tt + 1 < NT:
                    def _nx(sy, tt=tt, need=ln_done):
                        sy.wait_ge(s_dve.s, need)
                        sy.dma_start(xbuf[(tt + 1) % 2][:], x_full[tt + 1]).then_inc(s_x.s, 16)
                    block.sync(_nx)
                    s_x.v += 16
                pairs = [(xn_hi[:, c * 128:(c + 1) * 128],
                          xnT4[:, c * 512 + ti * 128: c * 512 + ti * 128 + 128]) for c in range(NC_)]
                emit_transposes(pairs, pre_wait=[(s_dve.s, ln_done)])
            x4_ready = s_tev.v

            for g in range(G):
                def _kt_pe(t, g=g, t4=t4, x4=x4_ready):
                    t.wait_ge(s_tev.s, x4)
                    if t4 == 0 and g == 0:
                        t.wait_ge(s_g.s, gB_ready)
                    if len(kt_ev) >= 2:
                        t.wait_ge(s_dve.s, kt_ev[-2])
                    last = None
                    for c in range(NC_):
                        last = t.matmul(ps_kt[g % 2][:],
                                        wk_sb[:, (g * NC_ + c) * 128:(g * NC_ + c) * 128 + 128],
                                        xnT4[:, c * 512:(c + 1) * 512],
                                        start=(c == 0), stop=(c == NC_ - 1))
                    last.then_inc(s_pe.s, 1)
                block.tensor(_kt_pe)
                s_pe.v += 1
                kt_done = s_pe.v

                def _kt_v(v, g=g, t4=t4, kd=kt_done, bv=bkT_val):
                    v.wait_ge(s_pe.s, kd)
                    if t4 == 0 and g == 0:
                        v.wait_ge(s_x.s, bv)
                    v.tensor_scalar(KT_sb[g][:, t4 * 512:(t4 + 1) * 512], ps_kt[g % 2][:],
                                    bkT_sb[:, g:g + 1], None, op0=ALU.add).then_inc(s_dve.s, 1)
                block.vector(_kt_v)
                s_dve.v += 1
                kt_ev.append(s_dve.v)

            for ti in range(4):
                tt = 4 * t4 + ti
                def _v_pe(t, ti=ti, t4=t4, x4=x4_ready):
                    t.wait_ge(s_tev.s, x4)
                    if t4 == 0 and ti == 0:
                        t.wait_ge(s_g.s, gB_ready)
                        t.wait_ge(s_dve.s, pro_dve)
                    if len(v_ev) >= 2:
                        t.wait_ge(s_dve.s, v_ev[-2])
                    for c in range(NC_):
                        t.matmul(ps_v[ti % 2][:],
                                 xnT4[:, c * 512 + ti * 128: c * 512 + ti * 128 + 128],
                                 wv_sb[:, c * 512:(c + 1) * 512],
                                 start=(c == 0), stop=False)
                    t.matmul(ps_v[ti % 2][:], ones_col[0:1, :], bv_sb[:],
                             start=False, stop=True).then_inc(s_pe.s, 1)
                block.tensor(_v_pe)
                s_pe.v += 1
                v_done = s_pe.v

                def _v_v(v, tt=tt, ti=ti, vd=v_done, vo=vones_ready):
                    v.wait_ge(s_pe.s, vd)
                    if tt == 0:
                        v.wait_ge(s_dve.s, vo)
                    dst = V_sb[:].rearrange("p (gt c) -> p gt c", c=VST)
                    v.tensor_copy(dst[:, tt::NT, 0:128],
                                  ps_v[ti % 2][:].rearrange("p (g d) -> p g d", g=G)).then_inc(s_dve.s, 1)
                block.vector(_v_v)
                s_dve.v += 1
                v_ev.append(s_dve.v)

        barrier()
        ctxB.close()
        ctxAB.close()
    else:
        ctxAB.close()

    if phases >= 3:
        ctxC = ExitStack()
        EC = ctxC.enter_context
        qt4 = [EC(sb(f"qt4_{i}", [128, 2048], BF16, side="right")) for i in range(2)]
        mask_sb = EC(sb("mask_sb", [128, 4 * 128], BF16, side="right"))
        pt_sb = [EC(sb(f"pt_sb{i}", [128, 512], BF16, side="right")) for i in range(2)]
        attn_sb = EC(sb("attn_sb", [128, 2048], F32, side="right"))
        xob = [EC(sb(f"xob{i}", [128, E], F32, side="right")) for i in range(2)]
        base_sb = [EC(sb(f"base_sb{i}", [128, E], F32, side="right")) for i in range(2)]
        linv_sb = EC(sb("linv_sb", [128, 4], F32, side="right"))
        xq_pg = [EC(sb(f"xq_pg{i}", [128, E], BF16, side="right")) for i in range(3)]
        tkp = [EC(sb(f"tkp{i}", [128, CAP_T * 2], F32, side="right")) for i in range(2)]
        gip = [EC(sb(f"gip{i}", [128, CAP_T], U32, side="right")) for i in range(2)]
        ps_st = [EC(nc.psum_tensor(f"ps_st{i}", [128, 512], F32)) for i in range(2)]
        ps_o = [EC(nc.psum_tensor(f"ps_o{i}", [128, 258], F32)) for i in range(2)]

        s_tkp2 = [Sem(ctx, nc, f"s_tkp{i}") for i in range(2)]
        s_icp = Sem(ctx, nc, "s_icp")
        s_gxp3 = [Sem(ctx, nc, f"s_gxp{i}") for i in range(3)]
        s_wxp3 = [Sem(ctx, nc, f"s_wxp{i}") for i in range(3)]
        ALLSEMS.extend(s_tkp2 + s_gxp3 + s_wxp3 + [s_icp])
        pg_gx = {}
        pg_wx = {}

        def emit_pg_write(i):
            def _wx_g(g, i=i, gv=pg_gx[i]):
                g.wait_ge(gv[0], gv[1])
                g.dma_start(xg_scr[i * 128:(i + 1) * 128, :],
                            xq_pg[i % 3][:]).then_inc(s_wxp3[i % 3].s, 16)
            block.gpsimd(_wx_g)
            s_wxp3[i % 3].v += 16
            pg_wx[i] = (s_wxp3[i % 3].s, s_wxp3[i % 3].v)

        def emit_pg(e):
            def _tk_s(sy, e=e, icn=(5 * (e - 1) if e >= 2 else 0)):
                if e == 0:
                    sy.wait_ge(s_sc.s, sc_total)
                if icn:
                    sy.wait_ge(s_icp.s, icn)
                sy.dma_start(
                    tkp[e % 2][:].rearrange("p (c w) -> p c w", w=2),
                    toks2[e * CAP:(e + 1) * CAP, :].rearrange("(c p) w -> p c w", p=128),
                ).then_inc(s_tkp2[e % 2].s, 16)
            block.sync(_tk_s)
            s_tkp2[e % 2].v += 16
            tkv = s_tkp2[e % 2].v

            def _ic_v(v, e=e, tkv=tkv, gwait=(pg_gx.get(5 * (e - 2) + 4) if e >= 2 else None)):
                v.wait_ge(s_tkp2[e % 2].s, tkv)
                if gwait:
                    v.wait_ge(gwait[0], gwait[1])
                t2 = tkp[e % 2][:].rearrange("p (c w) -> p c w", w=2)
                for c in range(CAP_T):
                    v.tensor_copy(gip[e % 2][:, c:c + 1], t2[:, c, 0:1]).then_inc(s_icp.s, 1)
            block.vector(_ic_v)
            s_icp.v += CAP_T

            for c in range(CAP_T):
                i = e * CAP_T + c
                def _gx_g(g, i=i, e=e, c=c, wprev=pg_wx.get(i - 3)):
                    g.wait_ge(s_icp.s, 5 * e + c + 1)
                    if wprev:
                        g.wait_ge(wprev[0], wprev[1])
                    g.indirect_dma_start(
                        xq_pg[i % 3][:], None, xn_scr,
                        IOoA(gip[e % 2][:, c:c + 1], 0),
                    ).then_inc(s_gxp3[i % 3].s, 16)
                block.gpsimd(_gx_g)
                s_gxp3[i % 3].v += 16
                pg_gx[i] = (s_gxp3[i % 3].s, s_gxp3[i % 3].v)
                if i >= 2:
                    emit_pg_write(i - 2)

        def _c_pre(sy):
            sy.dma_start(qt4[0][:], qT_scr[0]).then_inc(s_x.s, 16)
            sy.dma_start(xob[0][:], x_own[0]).then_inc(s_x.s, 16)
            sy.dma_start(mask_sb[:], masks[:, 0:512]).then_inc(s_x.s, 16)
        block.sync(_c_pre)
        s_x.v += 48

        c_st0 = s_pe.v
        c_exp0 = s_act.v
        c_av0 = s_pe2.v
        c_ev0 = s_ev.v
        n_unit = 0
        n_grp = 0
        base_w_vals = []

        for k in range(c_slots):
            J = 4 * k + 4
            qt_ready = s_x.v

            for g in range(G):
                u0 = n_unit
                m = n_grp

                mul_base = s_dve.v

                def _t_grp(t, k=k, g=g, J=J, u0=u0, m=m, qt_ready=qt_ready, mul_base=mul_base):
                    def emit_st(j):
                        u = u0 + j
                        if u >= 2:
                            t.wait_ge(s_act.s, c_exp0 + u - 1)
                        t.matmul(ps_st[u % 2][:], KT_sb[g][:, j * 128:(j + 1) * 128],
                                 qt4[k % 2][:, g * 512:(g + 1) * 512],
                                 start=True, stop=True).then_inc(s_pe.s, 1)

                    def emit_av(j):
                        if j == 0 and m >= 1:
                            t.wait_ge(s_ev.s, c_ev0 + 8 * m)
                        if j >= J - 4:
                            t.wait_ge(s_dve.s, mul_base + (j - (J - 4)) + 1)
                        else:
                            t.wait_ge(s_act.s, c_exp0 + u0 + j + 1)
                        for h4 in range(4):
                            t.matmul(ps_o[h4 // 2][:, (h4 % 2) * 129:(h4 % 2) * 129 + 129],
                                     pt_sb[(u0 + j) % 2][:, h4 * 128:(h4 + 1) * 128],
                                     V_sb[:, (g * NT + j) * VST:(g * NT + j) * VST + 129],
                                     start=(j == 0 and h4 % 2 == 0),
                                     stop=(j == J - 1 and h4 % 2 == 1)).then_inc(s_pe2.s, 1)

                    if g == 0:
                        t.wait_ge(s_x.s, qt_ready)
                    emit_st(0)
                    if J > 1:
                        emit_st(1)
                    for j in range(J):
                        if j + 2 < J:
                            emit_st(j + 2)
                        emit_av(j)
                block.tensor(_t_grp)
                s_pe.v += J
                s_pe2.v += 4 * J
                av_all = s_pe2.v

                def _exp_a(s, J=J, u0=u0):
                    for j in range(J):
                        u = u0 + j
                        s.wait_ge(s_pe.s, c_st0 + u + 1)
                        if u >= 2:
                            s.wait_ge(s_pe2.s, c_av0 + 4 * (u - 1))
                        s.activation(pt_sb[u % 2][:], ps_st[u % 2][:], AF.Exp,
                                     bias=consts[:, 0:1]).then_inc(s_act.s, 1)
                block.scalar(_exp_a)
                s_act.v += J

                def _mul_v(v, J=J, u0=u0, g=g, qt_ready=qt_ready):
                    if g == 0:
                        v.wait_ge(s_x.s, qt_ready)
                    for j in range(J - 4, J):
                        u = u0 + j
                        v.wait_ge(s_act.s, c_exp0 + u + 1)
                        m_b = mask_sb[:, (j - (J - 4)) * 128:(j - (J - 4) + 1) * 128].unsqueeze(1).to_broadcast([128, 4, 128])
                        p_v = pt_sb[u % 2][:].rearrange("p (r n) -> p r n", r=4)
                        v.tensor_tensor(p_v, p_v, m_b, ALU.mult).then_inc(s_dve.s, 1)
                block.vector(_mul_v)
                s_dve.v += 4

                def _ev_v(v, g=g, av=av_all, ev_base=s_ev.v):
                    v.wait_ge(s_pe2.s, av)
                    for h4 in range(4):
                        v.reciprocal(linv_sb[:, h4:h4 + 1],
                                     ps_o[h4 // 2][:, (h4 % 2) * 129 + 128:(h4 % 2) * 129 + 129]).then_inc(s_ev.s, 1)
                        v.wait_ge(s_ev.s, ev_base + 2 * h4 + 1)
                        v.tensor_scalar(attn_sb[:, (4 * g + h4) * 128:(4 * g + h4 + 1) * 128],
                                        ps_o[h4 // 2][:, (h4 % 2) * 129:(h4 % 2) * 129 + 128],
                                        linv_sb[:, h4:h4 + 1], None, op0=ALU.mult).then_inc(s_ev.s, 1)
                block.vector(_ev_v)
                s_ev.v += 8

                n_unit += J
                n_grp += 1

            def _base_v(v, k=k, ev=s_ev.v, qr=qt_ready,
                        bw=(base_w_vals[k - 1] if k >= 1 else 0)):
                v.wait_ge(s_ev.s, ev)
                v.wait_ge(s_x.s, qr)
                if bw:
                    v.wait_ge(s_out.s, bw)
                v.tensor_add(base_sb[k % 2][:], attn_sb[:], xob[k % 2][:]).then_inc(s_dve.s, 1)
            block.vector(_base_v)
            s_dve.v += 1
            base_done = s_dve.v

            def _base_w(g, k=k, bd=base_done):
                g.wait_ge(s_dve.s, bd)
                g.dma_start(y[k * 128:(k + 1) * 128, :], base_sb[k % 2][:]).then_inc(s_out.s, 16)
            block.scalar(_base_w)
            s_out.v += 16
            base_w_vals.append(s_out.v)

            emit_pg(2 * k)
            emit_pg(2 * k + 1)
            if k == NOWN - 1:
                emit_pg_write(NTILE - 2)
                emit_pg_write(NTILE - 1)

            if k + 1 < NOWN:
                def _c_nx(sy, k=k, stv=s_pe.v, mulv=s_dve.v):
                    sy.wait_ge(s_pe.s, stv)
                    sy.wait_ge(s_dve.s, mulv)
                    sy.dma_start(qt4[(k + 1) % 2][:], qT_scr[k + 1]).then_inc(s_x.s, 16)
                    sy.dma_start(xob[(k + 1) % 2][:], x_own[k + 1]).then_inc(s_x.s, 16)
                    sy.dma_start(mask_sb[:],
                                 masks[:, (k + 1) * 512:(k + 2) * 512]).then_inc(s_x.s, 16)
                block.sync(_c_nx)
                s_x.v += 48

        barrier()
        ctxC.close()
    ctxBC.close()

    if phases >= 4:
        base_total = s_out.v
        ctxD = ExitStack()
        ED = ctxD.enter_context
        wgu_sb = [ED(sb(f"wgu{i}", [128, 16 * 1536], BF16)) for i in range(2)]
        wd_sb = [ED(sb(f"wdd{i}", [128, NIB * 2048], BF16)) for i in range(2)]
        xg = [ED(sb(f"xg{i}", [128, E], BF16)) for i in range(2)]
        xgT = [ED(sb(f"xgT{i}", [128, E], BF16)) for i in range(2)]
        Hg = ED(sb("Hg", [128, I], BF16))
        H1 = ED(sb("H1", [128, I], BF16))
        H1T = [ED(sb(f"H1T{i}", [128, NIB * 128], BF16)) for i in range(2)]
        ye = [ED(sb(f"ye{i}", [128, E], F32)) for i in range(2)]
        tk_sb = [ED(sb(f"tk{i}", [128, CAP_T * 2], F32)) for i in range(2)]
        gi_u = [ED(sb(f"gi{i}", [128, CAP_T], U32)) for i in range(2)]
        stag = [ED(sb(f"stag{i}", [128, 1024], F32)) for i in range(4)]
        ps_gate = ED(nc.psum_tensor("ps_gate", [128, I], F32))
        ps_up = ED(nc.psum_tensor("ps_up", [128, I], F32))
        ps_d = [ED(nc.psum_tensor(f"ps_d{i}", [128, 512], F32)) for i in range(2)]
        ps_tr = [ED(nc.psum_tensor(f"ps_trD{i}", [128, 128], BF16)) for i in range(2)]

        s_st = [Sem(ctx, nc, f"s_st{i}") for i in range(4)]
        s_cw = Sem(ctx, nc, "s_cw")
        s_tk = Sem(ctx, nc, "s_tk")
        s_ic = Sem(ctx, nc, "s_ic")
        s_gx2 = [Sem(ctx, nc, f"s_gx{i}") for i in range(2)]
        s_sa2 = [Sem(ctx, nc, f"s_sa{i}") for i in range(2)]
        s_sil = Sem(ctx, nc, "s_sil")
        s_h1 = Sem(ctx, nc, "s_h1")
        s_ye = Sem(ctx, nc, "s_ye")
        ALLSEMS.extend(s_st + s_gx2 + s_sa2 + [s_cw, s_tk, s_ic, s_sil, s_h1, s_ye])

        ye_base = s_ye.v
        pe2_base = s_pe2.v

        up_pe = {}
        gate_pe = {}
        dn_pe2 = {}
        sil_v = {}
        h1_v = {}
        htr_ev = {}
        xgtr_ev = {}
        xgtr_tr = {}
        gx_v = {}
        sa_v = {}
        ic_v = {}
        tk_v = {}
        gu_rdy = {}
        dn_rdy = {}
        wq_cnt = [0]

        def load_toks(e, extra=None):
            def _t_s(sy, e=e, extra=extra):
                if extra:
                    for sm, vv in extra:
                        if vv > 0:
                            sy.wait_ge(sm, vv)
                sy.dma_start(
                    tk_sb[e % 2][:].rearrange("p (c w) -> p c w", w=2),
                    toks2[e * CAP:(e + 1) * CAP, :].rearrange("(c p) w -> p c w", p=128),
                ).then_inc(s_tk.s, 16)
            block.sync(_t_s)
            s_tk.v += 16
            tk_v[e] = s_tk.v

        def cast_idx(e, extra=None):
            def _c_v(v, e=e, extra=extra):
                v.wait_ge(s_tk.s, tk_v[e])
                if extra:
                    for sm, vv in extra:
                        if vv > 0:
                            v.wait_ge(sm, vv)
                tk2 = tk_sb[e % 2][:].rearrange("p (c w) -> p c w", w=2)
                for c in range(CAP_T):
                    v.tensor_copy(gi_u[e % 2][:, c:c + 1], tk2[:, c, 0:1]).then_inc(s_ic.s, 1)
            block.vector(_c_v)
            s_ic.v += CAP_T
            ic_v[e] = s_ic.v

        def emit_wchunks(e, c0, c1):
            for c in range(c0, c1):
                q = wq_cnt[0]
                slot = q % 4
                src = (wgu_in[e][:, c * 1024:(c + 1) * 1024] if c < 24
                       else wd_in[e][:, (c - 24) * 1024:(c - 23) * 1024])
                need = q - 3
                def _ld(eng, slot=slot, src=src, need=need):
                    if need > 0:
                        eng.wait_ge(s_cw.s, need)
                    eng.dma_start(stag[slot][:], src).then_inc(s_st[slot].s, 16)
                (block.sync if q % 2 == 0 else block.scalar)(_ld)
                s_st[slot].v += 16
                stv = s_st[slot].v
                dst = (wgu_sb[e % 2][:, c * 1024:(c + 1) * 1024] if c < 24
                       else wd_sb[e % 2][:, (c - 24) * 1024:(c - 23) * 1024])
                def _ca(a, slot=slot, stv=stv, dst=dst, e=e, c=c):
                    a.wait_ge(s_st[slot].s, stv)
                    if c == 0 and e >= 2:
                        a.wait_ge(s_pe.s, up_pe[5 * (e - 2) + 4])
                    if c == 24 and e >= 2:
                        a.wait_ge(s_pe2.s, dn_pe2[5 * (e - 2) + 4])
                    a.tensor_copy(dst, stag[slot][:]).then_inc(s_cw.s, 1)
                block.vector(_ca)
                wq_cnt[0] += 1
                if c == 23:
                    gu_rdy[e] = wq_cnt[0]
                if c == 35:
                    dn_rdy[e] = wq_cnt[0]

        def gather(i, first=False):
            def _gx_a(a, i=i):
                if i >= 2:
                    a.wait_ge(s_tr.s, xgtr_tr[i - 2])
                a.dma_start(xg[i % 2][:],
                            xg_scr[i * 128:(i + 1) * 128, :]).then_inc(s_gx2[i % 2].s, 16)
            block.sync(_gx_a)
            s_gx2[i % 2].v += 16
            gx_v[i] = s_gx2[i % 2].v

        def sa(i):
            e, c = i // CAP_T, i % CAP_T
            w0, w1 = s_sa2[0].v, s_sa2[1].v
            def _sa_g(g, i=i, e=e, c=c, w0=w0, w1=w1):
                g.wait_ge(s_ye.s, ye_base + 4 * (i + 1))
                if i == 0:
                    g.wait_ge(s_out.s, base_total)
                elif c == 0:
                    if w0:
                        g.wait_ge(s_sa2[0].s, w0)
                    if w1:
                        g.wait_ge(s_sa2[1].s, w1)
                g.indirect_dma_start(
                    y, IOoA(gi_u[e % 2][:, c:c + 1], 0),
                    ye[i % 2][:], None,
                    compute_op=ALU.add,
                ).then_inc(s_sa2[i % 2].s, 16)
            block.gpsimd(_sa_g)
            s_sa2[i % 2].v += 16
            sa_v[i] = s_sa2[i % 2].v

        def emit_xgtr(i):
            pw = [(s_gx2[i % 2].s, gx_v[i])]
            epw = []
            if i >= 2:
                epw.append((s_pe.s, up_pe[i - 2]))
            pairs = [(xg[i % 2][:, c * 128:(c + 1) * 128], xgT[i % 2][:, c * 128:(c + 1) * 128])
                     for c in range(NC_)]
            xgtr_tr[i] = s_tr.v + NC_
            xgtr_ev[i] = emit_transposes(pairs, pre_wait=pw, ev_pre_wait=epw or None)

        load_toks(0, extra=[(s_sc.s, sc_total)])
        cast_idx(0)
        load_toks(1)
        cast_idx(1)
        emit_wchunks(0, 0, 36)
        emit_wchunks(1, 0, 36)
        gather(0, first=True)
        gather(1)
        emit_xgtr(0)

        for i in range(NTILE):
            e, c = i // CAP_T, i % CAP_T
            es = e % 2

            if c == 0 and e >= 1 and e + 1 < NEXP:
                load_toks(e + 1, extra=[(s_ye.s, ye_base + 4 * (5 * e)),
                                        (s_ic.s, ic_v[e])])
            if c == 1 and e >= 1 and e + 1 < NEXP:
                cast_idx(e + 1, extra=[(s_sa2[(5 * e - 1) % 2].s, sa_v[5 * e - 1])])

            def _gu_pe(t, i=i, es=es, e=e):
                t.wait_ge(s_tev.s, xgtr_ev[i])
                t.wait_ge(s_cw.s, gu_rdy[e])
                if i >= 1:
                    t.wait_ge(s_sil.s, sil_v[i - 1])
                last = None
                for c16 in range(NC_):
                    lhs = xgT[i % 2][:, c16 * 128:(c16 + 1) * 128]
                    b = c16 * 1536
                    t.matmul(ps_gate[:, 0:512], lhs, wgu_sb[es][:, b:b + 512],
                             start=(c16 == 0), stop=False)
                    last = t.matmul(ps_gate[:, 512:768], lhs, wgu_sb[es][:, b + 512:b + 768],
                                    start=(c16 == 0), stop=(c16 == NC_ - 1))
                last.then_inc(s_pe.s, 1)
                if i >= 1:
                    t.wait_ge(s_h1.s, h1_v[i - 1])
                last = None
                for c16 in range(NC_):
                    lhs = xgT[i % 2][:, c16 * 128:(c16 + 1) * 128]
                    b = c16 * 1536 + 768
                    t.matmul(ps_up[:, 0:512], lhs, wgu_sb[es][:, b:b + 512],
                             start=(c16 == 0), stop=False)
                    last = t.matmul(ps_up[:, 512:768], lhs, wgu_sb[es][:, b + 512:b + 768],
                                    start=(c16 == 0), stop=(c16 == NC_ - 1))
                last.then_inc(s_pe.s, 1)
            block.tensor(_gu_pe)
            s_pe.v += 2
            gate_pe[i] = s_pe.v - 1
            up_pe[i] = s_pe.v

            def _sil_a(s, i=i):
                s.wait_ge(s_pe.s, gate_pe[i])
                if i >= 1:
                    s.wait_ge(s_h1.s, h1_v[i - 1])
                s.activation(Hg[:], ps_gate[:], AF.Silu).then_inc(s_sil.s, 1)
            block.scalar(_sil_a)
            s_sil.v += 1
            sil_v[i] = s_sil.v

            if i >= 1:
                ip = i - 1
                ep, cp = ip // CAP_T, ip % CAP_T
                def _ye_v(v, ip=ip, ep=ep, cp=cp):
                    if ip >= 2:
                        v.wait_ge(s_sa2[ip % 2].s, sa_v[ip - 2])
                    tk2 = tk_sb[ep % 2][:].rearrange("p (c w) -> p c w", w=2)
                    for j in range(4):
                        v.wait_ge(s_pe2.s, pe2_base + 4 * ip + j + 1)
                        v.tensor_scalar(ye[ip % 2][:, j * 512:(j + 1) * 512],
                                        ps_d[j % 2][:], tk2[:, cp, 1:2], None,
                                        op0=ALU.mult).then_inc(s_ye.s, 1)
                block.vector(_ye_v)
                s_ye.v += 4

            def _h1_v(v, i=i):
                v.wait_ge(s_sil.s, sil_v[i])
                v.wait_ge(s_pe.s, up_pe[i])
                v.tensor_tensor(H1[:], ps_up[:], Hg[:], ALU.mult).then_inc(s_h1.s, 1)
            block.vector(_h1_v)
            s_h1.v += 1
            h1_v[i] = s_h1.v

            if i + 1 < NTILE:
                emit_xgtr(i + 1)

            epw = []
            if i >= 2:
                epw.append((s_pe2.s, dn_pe2[i - 2]))
            pairs = [(H1[:, c2 * 128:(c2 + 1) * 128], H1T[i % 2][:, c2 * 128:(c2 + 1) * 128])
                     for c2 in range(NIB)]
            htr_ev[i] = emit_transposes(
                pairs, pre_wait=[(s_h1.s, h1_v[i])], ev_pre_wait=epw or None)

            def _dn_pe(t, i=i, es=es, e=e):
                t.wait_ge(s_tev.s, htr_ev[i])
                t.wait_ge(s_cw.s, dn_rdy[e])
                for j in range(4):
                    gidx = 4 * i + j
                    if gidx >= 2:
                        t.wait_ge(s_ye.s, ye_base + gidx - 1)
                    last = None
                    for ib in range(NIB):
                        last = t.matmul(ps_d[j % 2][:],
                                        H1T[i % 2][:, ib * 128:(ib + 1) * 128],
                                        wd_sb[es][:, ib * 2048 + j * 512: ib * 2048 + (j + 1) * 512],
                                        start=(ib == 0), stop=(ib == NIB - 1))
                    last.then_inc(s_pe2.s, 1)
            block.tensor(_dn_pe)
            s_pe2.v += 4
            dn_pe2[i] = s_pe2.v

            if e >= 1 and e + 1 < NEXP:
                rngs = [(0, 8), (8, 16), (16, 24), (24, 32), (32, 36)]
                emit_wchunks(e + 1, *rngs[c])

            if i + 2 < NTILE:
                gather(i + 2)
            if i >= 1:
                sa(i - 1)

        ip = NTILE - 1
        ep, cp = ip // CAP_T, ip % CAP_T
        def _ye_last(v, ip=ip, ep=ep, cp=cp):
            v.wait_ge(s_sa2[ip % 2].s, sa_v[ip - 2])
            tk2 = tk_sb[ep % 2][:].rearrange("p (c w) -> p c w", w=2)
            for j in range(4):
                v.wait_ge(s_pe2.s, pe2_base + 4 * ip + j + 1)
                v.tensor_scalar(ye[ip % 2][:, j * 512:(j + 1) * 512],
                                ps_d[j % 2][:], tk2[:, cp, 1:2], None,
                                op0=ALU.mult).then_inc(s_ye.s, 1)
        block.vector(_ye_last)
        s_ye.v += 4
        sa(NTILE - 1)

        def _fin(g, sv0=s_sa2[0].v, sv1=s_sa2[1].v, outv=s_out.v):
            g.wait_ge(s_sa2[0].s, sv0)
            g.wait_ge(s_sa2[1].s, sv1)
            g.wait_ge(s_out.s, outv)
        block.gpsimd(_fin)
        ctxD.close()

    ctx.close()
    return nc


def _prep_inputs(L0_attn_x, L0_attn_Wq, L0_attn_bq, L0_attn_Wk, L0_attn_bk,
                 L0_attn_Wv, L0_attn_bv, L0_ffn_router_w, L0_ffn_w_gate,
                 L0_ffn_w_up, L0_ffn_w_down):
    f = np.ascontiguousarray
    x = np.asarray(L0_attn_x, np.float32)
    wq = np.asarray(L0_attn_Wq, np.float32)
    bq = np.asarray(L0_attn_bq, np.float32)
    wk = np.asarray(L0_attn_Wk, np.float32)
    bk = np.asarray(L0_attn_bk, np.float32)
    wv = np.asarray(L0_attn_Wv, np.float32)
    bv = np.asarray(L0_attn_bv, np.float32)
    rw = np.asarray(L0_ffn_router_w, np.float32)
    wg = np.asarray(L0_ffn_w_gate, np.float32)
    wu = np.asarray(L0_ffn_w_up, np.float32)
    wd = np.asarray(L0_ffn_w_down, np.float32)

    wq_s = f(wq.transpose(1, 0, 2).reshape(2048, 2048).reshape(NC_, 128, 2048).transpose(1, 0, 2))
    wqb_s = f(bq.reshape(1, 2048))
    wk_s = f(wk.reshape(G, NC_, 128, 128).transpose(2, 0, 1, 3).reshape(128, G * NC_ * 128))
    bkT_s = f(bk.T)
    wv_s = f(wv.transpose(1, 0, 2).reshape(2048, 512).reshape(NC_, 128, 512).transpose(1, 0, 2).reshape(128, NC_ * 512))
    bv_s = f(bv.reshape(1, 512))
    rw_pad = np.zeros((2048, 128), np.float32)
    rw_pad[:, :32] = rw.T
    rw_s = f(rw_pad.reshape(NC_, 128, 128).transpose(1, 0, 2).reshape(128, NC_ * 128))
    ident = np.eye(128, dtype=ml_dtypes.bfloat16)

    wgt = wg.reshape(NEXP, 16, 128, I).transpose(0, 2, 1, 3)
    wut = wu.reshape(NEXP, 16, 128, I).transpose(0, 2, 1, 3)
    wgu_s = f(np.concatenate([wgt, wut], axis=3).reshape(NEXP, 128, 16 * 1536))
    wd_s = f(wd.reshape(NEXP, NIB, 128, 2048).transpose(0, 2, 1, 3).reshape(NEXP, 128, NIB * 2048))

    tri_c = f(np.triu(np.ones((128, 128), np.float32)).astype(ml_dtypes.bfloat16))
    ones_c = np.ones((128, 128), ml_dtypes.bfloat16)
    ecap_c = f(np.broadcast_to(np.tile(np.arange(NEXP, dtype=np.float32) * CAP - 1.0, NOWN),
                               (128, NOWN * 32)).copy())
    tokid_c = f((np.arange(NOWN, dtype=np.float32)[None, :] * 128
                 + np.arange(128, dtype=np.float32)[:, None]))

    kk_ar = np.arange(128)
    masks_by_r = []
    for r in range(4):
        m = np.zeros((NOWN * 4, 128, 128), np.float32)
        for k in range(NOWN):
            qpos = (4 * k + r) * 128 + kk_ar
            for jj in range(4):
                kpos = (4 * k + jj) * 128 + kk_ar
                m[k * 4 + jj] = (kpos[:, None] <= qpos[None, :])
        masks_by_r.append(f(m.astype(ml_dtypes.bfloat16).transpose(1, 0, 2).reshape(128, NOWN * 4 * 128)))

    shared = dict(wq_s=wq_s, wqb_s=wqb_s, wk_s=wk_s, bkT_s=bkT_s, wv_s=wv_s,
                  bv_s=bv_s, rw_s=rw_s, wgu_in=wgu_s, wd_in=wd_s, ident=ident,
                  tri_c=tri_c, ones_c=ones_c, ecap_c=ecap_c, tokid_c=tokid_c)
    in_maps = []
    for c in range(8):
        b, r = c // 4, c % 4
        xb = x[b].reshape(NT, 128, 2048)
        m = dict(shared)
        m["x_own"] = f(xb[r::4])
        m["x_full"] = f(xb)
        m["masks"] = masks_by_r[r]
        in_maps.append(m)
    return in_maps


_NC_CACHE = {}


def kernel(**inputs):
    key = (5, False)
    if key not in _NC_CACHE:
        _NC_CACHE[key] = build(phases=4, dbg=False)
    nc = _NC_CACHE[key]
    in_maps = _prep_inputs(**inputs)
    res = run_bass_kernel_spmd(nc, in_maps, core_ids=list(range(8)))
    out = np.empty((B, S, E), np.float32)
    for c in range(8):
        b, r = c // 4, c % 4
        out[b].reshape(NT, 128, 2048)[r::4] = res.results[c]["y"][:NOWN * 128].reshape(NOWN, 128, E)
    return out


# version 11
# speedup vs baseline: 1.1743x; 1.1018x over previous
import math
import numpy as np
import ml_dtypes
from contextlib import ExitStack

import concourse.bass as bass
import concourse.mybir as mybir
from concourse.bass_utils import run_bass_kernel_spmd

F32 = mybir.dt.float32
BF16 = mybir.dt.bfloat16
U32 = mybir.dt.uint32
AF = mybir.ActivationFunctionType
ALU = mybir.AluOpType
AX = mybir.AxisListType
IOoA = bass.IndirectOffsetOnAxis

B, S, E = 2, 8192, 2048
G, D = 4, 128
NEXP, I = 32, 768
NT = S // 128
NOWN = 16
NC_ = E // 128
NIB = I // 128
CSH = 20.0
ALPHA = 1.0 / math.sqrt(D)
VST = 132
MOFF = [sum(4 * kk + 4 for kk in range(k)) for k in range(NOWN + 1)]
CAP_T = 5
CAP = CAP_T * 128
NSLOT = NEXP * CAP
NTILE = NEXP * CAP_T
TOKS_ROWS = NSLOT + 128
DUMMY_TOK = 2048.0
YROWS = NOWN * 128 + 128


class Sem:
    def __init__(self, ctx, nc, name):
        self.s = ctx.enter_context(nc.semaphore(name))
        self.v = 0


def build(phases=4, dbg=False, c_slots=NOWN):
    nc = bass.Bass("TRN2", target_bir_lowering=False, debug=False)

    def din(name, shape, dt):
        return nc.dram_tensor(name, shape, dt, kind="ExternalInput").ap()

    def dscr(name, shape, dt):
        kind = "ExternalOutput" if dbg else "Internal"
        return nc.dram_tensor(name, shape, dt, kind=kind).ap()

    x_own = din("x_own", [NOWN, 128, E], F32)
    x_full = din("x_full", [NT, 128, E], F32)
    wq_s = din("wq_s", [128, NC_, 2048], F32)
    wqb_s = din("wqb_s", [1, 2048], F32)
    wk_s = din("wk_s", [128, G * NC_ * 128], F32)
    bkT_s = din("bkT_s", [128, G], F32)
    wv_s = din("wv_s", [128, NC_ * 512], F32)
    bv_s = din("bv_s", [1, 512], F32)
    rw_s = din("rw_s", [128, NC_ * 128], F32)
    wgu_in = din("wgu_in", [NEXP, 128, 16 * 1536], F32)
    wd_in = din("wd_in", [NEXP, 128, NIB * 2048], F32)
    masks = din("masks", [128, NOWN * 4 * 128], BF16)
    ident = din("ident", [128, 128], BF16)
    tri_c = din("tri_c", [128, 128], BF16)
    ones_c = din("ones_c", [128, 128], BF16)
    ecap_c = din("ecap_c", [128, NOWN * 32], F32)
    tokid_c = din("tokid_c", [128, NOWN], F32)

    qT_scr = dscr("qT_scr", [NOWN, 128, 2048], BF16)
    xn_scr = dscr("xn_scr", [YROWS, E], BF16)
    toks2 = dscr("toks2", [TOKS_ROWS, 2], F32)
    xg_scr = dscr("xg_scr", [NTILE * 128, E], BF16)
    y = nc.dram_tensor("y", [YROWS, E], F32, kind="ExternalOutput").ap()

    ctx = ExitStack()
    EE = ctx.enter_context

    def sb(name, shape, dt, side="left"):
        return nc.sbuf_tensor(name, shape, dt, side=side)

    consts = EE(sb("consts", [128, 8], F32))
    ones_col = EE(sb("ones_col", [1, 128], BF16))
    id_sb = EE(sb("id_sb", [128, 128], BF16))
    w_sb = EE(sb("w_sb", [128, NOWN * 32], F32))
    stat6 = EE(sb("stat6", [128, 24], F32))
    mv = EE(sb("mv", [128, 8], F32))
    pay8 = EE(sb("pay8", [128, NOWN * 8 * 2], F32))
    pos8_u = EE(sb("pos8_u", [128, NOWN * 8], U32))

    s_x = Sem(ctx, nc, "s_x")
    s_g = Sem(ctx, nc, "s_g")
    s_out = Sem(ctx, nc, "s_out")
    s_pe = Sem(ctx, nc, "s_pe")
    s_tr = Sem(ctx, nc, "s_tr")
    s_pe2 = Sem(ctx, nc, "s_pe2")
    s_dve = Sem(ctx, nc, "s_dve")
    s_ev = Sem(ctx, nc, "s_ev")
    s_act = Sem(ctx, nc, "s_act")
    s_ti = Sem(ctx, nc, "s_ti")
    s_tev = Sem(ctx, nc, "s_tev")
    s_xw = Sem(ctx, nc, "s_xw")
    s_sc = Sem(ctx, nc, "s_sc")

    block = EE(nc.Block())
    ALLSEMS = [s_x, s_g, s_out, s_pe, s_tr, s_pe2, s_dve, s_ev, s_act, s_ti, s_tev, s_xw]

    def barrier():
        vals = [(sm.s, sm.v) for sm in ALLSEMS]
        for name in ("sync", "tensor", "vector", "scalar", "gpsimd"):
            def _f(eng, vals=vals):
                for s, v in vals:
                    if v > 0:
                        eng.wait_ge(s, v)
            getattr(block, name)(_f)

    def emit_ln(x_ready_val, xb, xn_f=None, xn_lo=None, xn_hi=None, extra_wait=None):
        def _v1(v, xv=x_ready_val):
            v.wait_ge(s_x.s, xv)
            for i in range(3):
                v.bn_stats(stat6[:, i * 6:(i + 1) * 6], xb[:, i * 512:(i + 1) * 512])
            v.bn_stats(stat6[:, 18:24], xb[:, 1536:2048]).then_inc(s_dve.s, 1)
            v.wait_ge(s_dve.s, s_dve.v + 1)
            v.bn_aggr(mv[:, 0:2], stat6[:]).then_inc(s_dve.s, 1)
        block.vector(_v1)
        s_dve.v += 2
        mv_ready = s_dve.v

        def _a1(s, mvv=mv_ready):
            s.wait_ge(s_dve.s, mvv)
            s.activation(mv[:, 2:3], mv[:, 1:2], AF.Ln, bias=consts[:, 2:3]).then_inc(s_act.s, 1)
            s.wait_ge(s_act.s, s_act.v + 1)
            s.activation(mv[:, 3:4], mv[:, 2:3], AF.Exp, scale=consts[:, 3:4]).then_inc(s_act.s, 1)
        block.scalar(_a1)
        s_act.v += 2
        rstd_ready = s_act.v

        def _v2(v, rv=rstd_ready, ew=extra_wait):
            v.wait_ge(s_act.s, rv)
            if ew:
                for sm, vv in ew:
                    if vv > 0:
                        v.wait_ge(sm, vv)
            if xn_lo is not None:
                v.tensor_scalar(xn_hi[:], xb[:], mv[:, 0:1], mv[:, 3:4], op0=ALU.subtract, op1=ALU.mult)
                v.tensor_scalar(xn_f[:], xb[:], mv[:, 0:1], mv[:, 3:4], op0=ALU.subtract, op1=ALU.mult).then_inc(s_dve.s, 1)
                v.wait_ge(s_dve.s, s_dve.v + 1)
                v.tensor_sub(xn_lo[:], xn_f[:], xn_hi[:]).then_inc(s_dve.s, 1)
            else:
                v.tensor_scalar(xn_hi[:], xb[:], mv[:, 0:1], mv[:, 3:4], op0=ALU.subtract, op1=ALU.mult).then_inc(s_dve.s, 1)
        block.vector(_v2)
        s_dve.v += 2 if xn_lo is not None else 1
        return s_dve.v

    tr_parity = [0]
    ps_tr = None

    def emit_transposes(pairs, pre_wait=None, ev_pre_wait=None):
        n = len(pairs)
        tr0 = s_tr.v

        def _pe(t, pre_wait=pre_wait, tr0=tr0):
            if pre_wait:
                for sm, vv in pre_wait:
                    if vv > 0:
                        t.wait_ge(sm, vv)
            for i, (src, dst) in enumerate(pairs):
                gi = tr0 + i
                if gi >= 2:
                    t.wait_ge(s_tev.s, gi - 1)
                t.transpose(ps_tr[(tr_parity[0] + i) % 2][:], src, id_sb[:]).then_inc(s_tr.s, 1)
        block.tensor(_pe)
        s_tr.v += n

        def _ve(v, tr0=tr0, par=tr_parity[0], epw=ev_pre_wait):
            if epw:
                for sm, vv in epw:
                    if vv > 0:
                        v.wait_ge(sm, vv)
            for i, (src, dst) in enumerate(pairs):
                v.wait_ge(s_tr.s, tr0 + i + 1)
                v.tensor_copy(dst, ps_tr[(par + i) % 2][:]).then_inc(s_tev.s, 1)
        block.vector(_ve)
        s_tev.v += n
        tr_parity[0] = (tr_parity[0] + n) % 2
        return s_tev.v

    def _pro_v(v):
        v.memset(consts[:, 0:1], -CSH)
        v.memset(consts[:, 1:2], 1.0 / 2048.0)
        v.memset(consts[:, 2:3], 1e-5)
        v.memset(consts[:, 3:4], -0.5)
        v.memset(consts[:, 4:5], DUMMY_TOK)
        v.memset(ones_col[:], 1.0).then_inc(s_dve.s, 1)
    block.vector(_pro_v)
    s_dve.v += 1
    pro_dve = s_dve.v

    ctxAB = ExitStack()
    if phases >= 1:
        ctxA = ExitStack()
        EA = ctxA.enter_context
        xbuf = [ctxAB.enter_context(sb(f"xbuf{i}", [128, E], F32, side="right")) for i in range(2)]
        xn_hi = ctxAB.enter_context(sb("xn_hi", [128, E], BF16, side="right"))
        wq_sb = EA(sb("wq_sb", [128, NC_ * 2048], BF16))
        wqb_sb = EA(sb("wqb_sb", [1, 2048], BF16))
        rw_f = EA(sb("rw_f", [128, NC_ * 128], F32))
        rw_hi = EA(sb("rw_hi", [128, NC_ * 128], BF16))
        rw_lo = EA(sb("rw_lo", [128, NC_ * 128], BF16))
        xn_f = EA(sb("xn_f", [128, E], F32))
        xn_lo = EA(sb("xn_lo", [128, E], BF16))
        xnT_sb = EA(sb("xnT_sb", [128, E], BF16))
        xnT_lo_sb = EA(sb("xnT_lo_sb", [128, E], BF16))
        qbf = EA(sb("qbf", [128, 2048], BF16))
        qt_sb = EA(sb("qt_sb", [128, 2048], BF16))
        pr_p = EA(sb("pr_p", [128, 32], F32))
        pr_m8 = EA(sb("pr_m8", [128, 8], F32))
        pr_zap = EA(sb("pr_zap", [128, 32], F32))
        pr_s = EA(sb("pr_s", [128, 2], F32))
        tri_sb = EA(sb("tri_sb", [128, 128], BF16))
        onesq_sb = EA(sb("onesq_sb", [128, 128], BF16))
        ecap_sb = EA(sb("ecap_sb", [128, NOWN * 32], F32))
        tokid_sb = EA(sb("tokid_sb", [128, NOWN], F32))
        init_sb = EA(sb("init_sb", [128, (TOKS_ROWS // 128) * 2], F32))
        a_bf = EA(sb("a_bf", [128, NOWN * 32], BF16))
        a_f = EA(sb("a_f", [128, NOWN * 32], F32))
        tmp512 = EA(sb("tmp512", [128, NOWN * 32], F32))
        cA = EA(sb("cA", [128, NOWN * 32], F32))
        cB = EA(sb("cB", [128, NOWN * 32], F32))
        mjt = EA(sb("mjt", [128, NOWN * 32], F32))
        mj2 = EA(sb("mj2", [128, NOWN * 32], F32))
        pos8_f = EA(sb("pos8_f", [128, NOWN * 8], F32))
        ps_tr = [EA(nc.psum_tensor(f"ps_trA{i}", [128, 128], BF16)) for i in range(2)]
        ps_q = EA(nc.psum_tensor("ps_q", [128, 2048], F32))
        ps_r = EA(nc.psum_tensor("ps_r", [128, 32], F32))
        ps_rank = EA(nc.psum_tensor("ps_rank", [128, NOWN * 32], F32))

        def _a_g(g):
            g.dma_start(wq_sb[:].rearrange("p (c n) -> p c n", c=NC_), wq_s).then_inc(s_g.s, 16)
            g.dma_start(wqb_sb[:], wqb_s).then_inc(s_g.s, 16)
            g.dma_start(id_sb[:], ident).then_inc(s_g.s, 16)
            g.dma_start(tri_sb[:], tri_c).then_inc(s_g.s, 16)
            g.dma_start(onesq_sb[:], ones_c).then_inc(s_g.s, 16)
            g.dma_start(ecap_sb[:], ecap_c).then_inc(s_g.s, 16)
            g.dma_start(tokid_sb[:], tokid_c).then_inc(s_g.s, 16)
        block.gpsimd(_a_g)
        s_g.v += 112
        gA_ready = s_g.v
        id_ready = s_g.v

        def _a_s(sy):
            sy.dma_start(rw_f[:], rw_s).then_inc(s_x.s, 16)
            sy.dma_start(xbuf[0][:], x_own[0]).then_inc(s_x.s, 16)
        block.sync(_a_s)
        s_x.v += 32

        def _a_v0(v):
            v.memset(qbf[:], 0.0).then_inc(s_dve.s, 1)
            v.memset(init_sb[:], 0.0).then_inc(s_dve.s, 1)
            v.wait_ge(s_dve.s, s_dve.v + 2)
            iv = init_sb[:].rearrange("p (r w) -> p r w", w=2)
            v.tensor_copy(iv[:, :, 0:1],
                          consts[:, 4:5].rearrange("p (a b) -> p a b", b=1)
                          .to_broadcast([128, TOKS_ROWS // 128, 1])).then_inc(s_dve.s, 1)
        block.vector(_a_v0)
        s_dve.v += 3
        init_done = s_dve.v

        def _a_g0(g, idn=init_done):
            g.wait_ge(s_dve.s, idn)
            g.dma_start(toks2.rearrange("(p r) w -> p (r w)", p=128), init_sb[:]).then_inc(s_ti.s, 16)
            g.dma_start(xn_scr[NOWN * 128:YROWS, :], qbf[:]).then_inc(s_ti.s, 16)
        block.gpsimd(_a_g0)
        s_ti.v += 32
        ti_done = s_ti.v

        def _a_v(v, xv=s_x.v):
            v.wait_ge(s_x.s, xv)
            v.tensor_copy(rw_hi[:], rw_f[:]).then_inc(s_dve.s, 1)
            v.wait_ge(s_dve.s, s_dve.v + 1)
            v.tensor_sub(rw_lo[:], rw_f[:], rw_hi[:]).then_inc(s_dve.s, 1)
        block.vector(_a_v)
        s_dve.v += 2
        rw_ready = s_dve.v

        prp_done_prev = 0
        w_done_prev = 0
        qt_out_prev = 0
        q_done_prev = 0
        xnw_prev = 0
        for k in range(NOWN):
            ew = [(s_xw.s, xnw_prev)] if xnw_prev else None
            ln_done = emit_ln(s_x.v, xbuf[k % 2], xn_f=xn_f, xn_lo=xn_lo, xn_hi=xn_hi,
                              extra_wait=ew)

            if k + 1 < NOWN:
                def _nx(sy, k=k, need=ln_done):
                    sy.wait_ge(s_dve.s, need)
                    sy.dma_start(xbuf[(k + 1) % 2][:], x_own[k + 1]).then_inc(s_x.s, 16)
                block.sync(_nx)
                s_x.v += 16

            def _xnw(g, k=k, need=ln_done):
                g.wait_ge(s_dve.s, need)
                g.dma_start(xn_scr[k * 128:(k + 1) * 128, :], xn_hi[:]).then_inc(s_xw.s, 16)
            block.gpsimd(_xnw)
            s_xw.v += 16
            xnw_prev = s_xw.v

            pw = [(s_dve.s, ln_done), (s_g.s, id_ready)]
            if qt_out_prev:
                pw.append((s_out.s, qt_out_prev))
            if q_done_prev:
                pw.append((s_pe.s, q_done_prev))
            pairs = [(xn_hi[:, c * 128:(c + 1) * 128], xnT_sb[:, c * 128:(c + 1) * 128]) for c in range(NC_)]
            pairs += [(xn_lo[:, c * 128:(c + 1) * 128], xnT_lo_sb[:, c * 128:(c + 1) * 128]) for c in range(NC_)]
            ev_hi_lo = emit_transposes(pairs, pre_wait=pw)

            def _q_pe(t, k=k, ev=ev_hi_lo, pd=prp_done_prev):
                t.wait_ge(s_tev.s, ev)
                if k == 0:
                    t.wait_ge(s_g.s, gA_ready)
                    t.wait_ge(s_dve.s, rw_ready)
                if pd:
                    t.wait_ge(s_act.s, pd)
                for c in range(NC_):
                    lhs = xnT_sb[:, c * 128:(c + 1) * 128]
                    for n4 in range(4):
                        t.matmul(ps_q[:, n4 * 512:(n4 + 1) * 512], lhs,
                                 wq_sb[:, c * 2048 + n4 * 512: c * 2048 + (n4 + 1) * 512],
                                 start=(c == 0), stop=False)
                    t.matmul(ps_r[:], lhs, rw_hi[:, c * 128:c * 128 + 32], start=(c == 0), stop=False)
                    t.matmul(ps_r[:], lhs, rw_lo[:, c * 128:c * 128 + 32], start=False, stop=False)
                    mm = t.matmul(ps_r[:], xnT_lo_sb[:, c * 128:(c + 1) * 128],
                                  rw_hi[:, c * 128:c * 128 + 32], start=False, stop=(c == NC_ - 1))
                    if c == NC_ - 1:
                        mm.then_inc(s_pe.s, 1)
                last = None
                for n4 in range(4):
                    last = t.matmul(ps_q[:, n4 * 512:(n4 + 1) * 512], ones_col[0:1, :],
                                    wqb_sb[0:1, n4 * 512:(n4 + 1) * 512], start=False, stop=True)
                last.then_inc(s_pe.s, 1)
            block.tensor(_q_pe)
            s_pe.v += 2
            q_done = s_pe.v
            q_done_prev = q_done

            def _q_v(v, k=k, qd=q_done):
                v.wait_ge(s_pe.s, qd)
                if k == 0:
                    v.wait_ge(s_ti.s, ti_done)
                v.tensor_scalar(qbf[:], ps_q[:], ALPHA, None, op0=ALU.mult).then_inc(s_dve.s, 1)
            block.vector(_q_v)
            s_dve.v += 1
            qbf_done = s_dve.v

            pairs = [(qbf[:, c * 128:(c + 1) * 128], qt_sb[:, c * 128:(c + 1) * 128]) for c in range(NC_)]
            ev_qt = emit_transposes(pairs, pre_wait=[(s_dve.s, qbf_done)])

            def _qtw(g, k=k, ev=ev_qt):
                g.wait_ge(s_tev.s, ev)
                g.dma_start(qT_scr[k], qt_sb[:]).then_inc(s_out.s, 16)
            block.gpsimd(_qtw)
            s_out.v += 16
            qt_out_prev = s_out.v

            def _r_a(s, qd=q_done, wdp=w_done_prev):
                s.wait_ge(s_pe.s, qd - 1)
                if wdp:
                    s.wait_ge(s_dve.s, wdp)
                s.activation(pr_p[:], ps_r[:], AF.Exp, bias=consts[:, 0:1]).then_inc(s_act.s, 1)
            block.scalar(_r_a)
            s_act.v += 1
            prp_done_prev = s_act.v

            def _r_v(v, k=k, pv=s_act.v):
                v.wait_ge(s_act.s, pv)
                v.max(out=pr_m8[:], in_=pr_p[:]).then_inc(s_dve.s, 1)
                v.wait_ge(s_dve.s, s_dve.v + 1)
                v.match_replace(out=pr_zap[:], in_to_replace=pr_m8[:], in_values=pr_p[:], imm_value=0.0).then_inc(s_dve.s, 1)
                v.wait_ge(s_dve.s, s_dve.v + 2)
                v.tensor_sub(pr_zap[:], pr_p[:], pr_zap[:]).then_inc(s_dve.s, 1)
                v.wait_ge(s_dve.s, s_dve.v + 3)
                v.reduce_sum(pr_s[:, 0:1], pr_zap[:], axis=AX.X).then_inc(s_dve.s, 1)
                v.wait_ge(s_dve.s, s_dve.v + 4)
                v.reciprocal(pr_s[:, 1:2], pr_s[:, 0:1]).then_inc(s_dve.s, 1)
                v.wait_ge(s_dve.s, s_dve.v + 5)
                v.tensor_scalar(w_sb[:, k * 32:(k + 1) * 32], pr_zap[:], pr_s[:, 1:2], None,
                                op0=ALU.mult).then_inc(s_dve.s, 1)
            block.vector(_r_v)
            s_dve.v += 6
            w_done_prev = s_dve.v

        def _ri_v(v, wd=w_done_prev):
            v.wait_ge(s_dve.s, wd)
            v.tensor_scalar(a_bf[:], w_sb[:], 0.0, None, op0=ALU.is_gt).then_inc(s_dve.s, 1)
            v.wait_ge(s_dve.s, s_dve.v + 1)
            v.tensor_scalar(a_f[:], w_sb[:], 0.0, None, op0=ALU.is_gt).then_inc(s_dve.s, 1)
        block.vector(_ri_v)
        s_dve.v += 2
        a_done = s_dve.v

        def _rk_pe(t, ad=a_done):
            t.wait_ge(s_dve.s, ad)
            last = None
            for k in range(NOWN):
                for kp in range(k + 1):
                    last = t.matmul(ps_rank[:, k * 32:(k + 1) * 32],
                                    tri_sb[:] if kp == k else onesq_sb[:],
                                    a_bf[:, kp * 32:(kp + 1) * 32],
                                    start=(kp == 0), stop=(kp == k))
            last.then_inc(s_pe.s, 1)
        block.tensor(_rk_pe)
        s_pe.v += 1
        rank_done = s_pe.v

        def _pos_v(v, rd=rank_done):
            base = s_dve.v
            cnt = [0]
            def stepped(ins):
                v.wait_ge(s_dve.s, base + cnt[0])
                ins().then_inc(s_dve.s, 1)
                cnt[0] += 1
            v.wait_ge(s_pe.s, rd)
            v.wait_ge(s_g.s, gA_ready)
            stepped(lambda: v.tensor_tensor(tmp512[:], ps_rank[:], ecap_sb[:], ALU.add))
            stepped(lambda: v.tensor_copy(cA[:], a_f[:]))
            src, dst = cA, cB
            for sh in (1, 2, 4, 8, 16):
                s3 = src[:].rearrange("p (k e) -> p k e", e=32)
                d3 = dst[:].rearrange("p (k e) -> p k e", e=32)
                v.wait_ge(s_dve.s, base + cnt[0])
                v.tensor_copy(d3[:, :, 0:sh], s3[:, :, 0:sh])
                v.tensor_tensor(d3[:, :, sh:32], s3[:, :, sh:32],
                                s3[:, :, 0:32 - sh], ALU.add).then_inc(s_dve.s, 1)
                cnt[0] += 1
                src, dst = dst, src
            pos3 = pos8_f[:].rearrange("p (k j) -> p k j", j=8)
            pay4 = pay8[:].rearrange("p (k j w) -> p k j w", j=8, w=2)
            for j in range(8):
                stepped(lambda j=j: v.tensor_scalar(mjt[:], src[:], float(j + 1), None, op0=ALU.is_equal))
                stepped(lambda: v.tensor_tensor(mjt[:], mjt[:], a_f[:], ALU.mult))
                stepped(lambda: v.tensor_tensor(mj2[:], mjt[:], tmp512[:], ALU.mult))
                stepped(lambda j=j: v.reduce_sum(pos3[:, :, j:j + 1],
                                                 mj2[:].rearrange("p (k e) -> p k e", e=32), axis=AX.X))
                stepped(lambda: v.tensor_tensor(mj2[:], mjt[:], w_sb[:], ALU.mult))
                stepped(lambda j=j: v.reduce_sum(pay4[:, :, j, 1:2],
                                                 mj2[:].rearrange("p (k e) -> p k e", e=32), axis=AX.X))
            tid = tokid_sb[:].rearrange("p (a b) -> p a b", b=1)
            for k in range(NOWN):
                stepped(lambda k=k: v.tensor_copy(pay4[:, k, :, 0:1],
                                                  tid[:, k:k + 1, :].to_broadcast([128, 8, 1])))
            stepped(lambda: v.tensor_copy(pos8_u[:], pos8_f[:]))
        block.vector(_pos_v)
        s_dve.v += 2 + 5 + 48 + 16 + 1
        pay_done = s_dve.v

        def emit_scatters():
            def _sc_g(g, pd=pay_done, tid=ti_done):
                g.wait_ge(s_ti.s, tid)
                g.wait_ge(s_dve.s, pd)
                for col in range(NOWN * 8):
                    g.indirect_dma_start(
                        toks2, IOoA(pos8_u[:, col:col + 1], 0),
                        pay8[:, col * 2:col * 2 + 2], None,
                    ).then_inc(s_sc.s, 16)
            block.gpsimd(_sc_g)
            s_sc.v += NOWN * 8 * 16
            return s_sc.v

        barrier()
        ctxA.close()

    ctxBC = ExitStack()
    if phases >= 2:
        KT_sb = [ctxBC.enter_context(sb(f"KT{g}", [128, S], BF16)) for g in range(G)]
        V_sb = ctxBC.enter_context(sb("V_sb", [128, G * NT * VST], BF16))
        ctxB = ExitStack()
        EB = ctxB.enter_context
        wk_sb = EB(sb("wk_sb", [128, G * NC_ * 128], BF16, side="right"))
        wv_sb = EB(sb("wv_sb", [128, NC_ * 512], BF16, side="right"))
        bkT_sb = EB(sb("bkT_sb", [128, G], F32, side="right"))
        bv_sb = EB(sb("bv_sb", [1, 512], BF16, side="right"))
        xnT4 = EB(sb("xnT4", [128, NC_ * 512], BF16, side="right"))
        ps_tr = [EB(nc.psum_tensor(f"ps_trB{i}", [128, 128], BF16)) for i in range(2)]
        ps_kt = [EB(nc.psum_tensor(f"ps_kt{i}", [128, 512], F32)) for i in range(2)]
        ps_v = [EB(nc.psum_tensor(f"ps_v{i}", [128, 512], F32)) for i in range(2)]
        tr_parity[0] = 0

        def _pro_vb(v):
            vv = V_sb[:].rearrange("p (t c) -> p t c", c=VST)
            v.memset(vv[:, :, 128:129], 1.0).then_inc(s_dve.s, 1)
        block.vector(_pro_vb)
        s_dve.v += 1
        vones_ready = s_dve.v

        def _b_g(g):
            g.dma_start(wk_sb[:], wk_s).then_inc(s_g.s, 16)
            g.dma_start(wv_sb[:], wv_s).then_inc(s_g.s, 16)
            g.dma_start(bv_sb[:], bv_s).then_inc(s_g.s, 16)
        block.gpsimd(_b_g)
        s_g.v += 48
        gB_ready = s_g.v

        sc_total = emit_scatters()

        def _b_s(sy):
            sy.dma_start(bkT_sb[:], bkT_s).then_inc(s_x.s, 16)
            sy.dma_start(xbuf[0][:], x_full[0]).then_inc(s_x.s, 16)
        block.sync(_b_s)
        s_x.v += 32
        bkT_val = s_x.v

        kt_ev = []
        v_ev = []
        for t4 in range(NT // 4):
            for ti in range(4):
                tt = 4 * t4 + ti
                ln_done = emit_ln(s_x.v, xbuf[tt % 2], xn_hi=xn_hi)
                if tt + 1 < NT:
                    def _nx(sy, tt=tt, need=ln_done):
                        sy.wait_ge(s_dve.s, need)
                        sy.dma_start(xbuf[(tt + 1) % 2][:], x_full[tt + 1]).then_inc(s_x.s, 16)
                    block.sync(_nx)
                    s_x.v += 16
                pairs = [(xn_hi[:, c * 128:(c + 1) * 128],
                          xnT4[:, c * 512 + ti * 128: c * 512 + ti * 128 + 128]) for c in range(NC_)]
                emit_transposes(pairs, pre_wait=[(s_dve.s, ln_done)])
            x4_ready = s_tev.v

            for g in range(G):
                def _kt_pe(t, g=g, t4=t4, x4=x4_ready):
                    t.wait_ge(s_tev.s, x4)
                    if t4 == 0 and g == 0:
                        t.wait_ge(s_g.s, gB_ready)
                    if len(kt_ev) >= 2:
                        t.wait_ge(s_dve.s, kt_ev[-2])
                    last = None
                    for c in range(NC_):
                        last = t.matmul(ps_kt[g % 2][:],
                                        wk_sb[:, (g * NC_ + c) * 128:(g * NC_ + c) * 128 + 128],
                                        xnT4[:, c * 512:(c + 1) * 512],
                                        start=(c == 0), stop=(c == NC_ - 1))
                    last.then_inc(s_pe.s, 1)
                block.tensor(_kt_pe)
                s_pe.v += 1
                kt_done = s_pe.v

                def _kt_v(v, g=g, t4=t4, kd=kt_done, bv=bkT_val):
                    v.wait_ge(s_pe.s, kd)
                    if t4 == 0 and g == 0:
                        v.wait_ge(s_x.s, bv)
                    v.tensor_scalar(KT_sb[g][:, t4 * 512:(t4 + 1) * 512], ps_kt[g % 2][:],
                                    bkT_sb[:, g:g + 1], None, op0=ALU.add).then_inc(s_dve.s, 1)
                block.vector(_kt_v)
                s_dve.v += 1
                kt_ev.append(s_dve.v)

            for ti in range(4):
                tt = 4 * t4 + ti
                def _v_pe(t, ti=ti, t4=t4, x4=x4_ready):
                    t.wait_ge(s_tev.s, x4)
                    if t4 == 0 and ti == 0:
                        t.wait_ge(s_g.s, gB_ready)
                        t.wait_ge(s_dve.s, pro_dve)
                    if len(v_ev) >= 2:
                        t.wait_ge(s_dve.s, v_ev[-2])
                    for c in range(NC_):
                        t.matmul(ps_v[ti % 2][:],
                                 xnT4[:, c * 512 + ti * 128: c * 512 + ti * 128 + 128],
                                 wv_sb[:, c * 512:(c + 1) * 512],
                                 start=(c == 0), stop=False)
                    t.matmul(ps_v[ti % 2][:], ones_col[0:1, :], bv_sb[:],
                             start=False, stop=True).then_inc(s_pe.s, 1)
                block.tensor(_v_pe)
                s_pe.v += 1
                v_done = s_pe.v

                def _v_v(v, tt=tt, ti=ti, vd=v_done, vo=vones_ready):
                    v.wait_ge(s_pe.s, vd)
                    if tt == 0:
                        v.wait_ge(s_dve.s, vo)
                    dst = V_sb[:].rearrange("p (gt c) -> p gt c", c=VST)
                    v.tensor_copy(dst[:, tt::NT, 0:128],
                                  ps_v[ti % 2][:].rearrange("p (g d) -> p g d", g=G)).then_inc(s_dve.s, 1)
                block.vector(_v_v)
                s_dve.v += 1
                v_ev.append(s_dve.v)

        barrier()
        ctxB.close()
        ctxAB.close()
    else:
        ctxAB.close()

    if phases >= 3:
        ctxC = ExitStack()
        EC = ctxC.enter_context
        qt4 = [EC(sb(f"qt4_{i}", [128, 2048], BF16, side="right")) for i in range(2)]
        mask_sb = EC(sb("mask_sb", [128, 4 * 128], BF16, side="right"))
        pt_sb = [EC(sb(f"pt_sb{i}", [128, 512], BF16, side="right")) for i in range(2)]
        attn_sb = EC(sb("attn_sb", [128, 2048], F32, side="right"))
        xob = [EC(sb(f"xob{i}", [128, E], F32, side="right")) for i in range(2)]
        base_sb = [EC(sb(f"base_sb{i}", [128, E], F32, side="right")) for i in range(2)]
        linv_sb = EC(sb("linv_sb", [128, 4], F32, side="right"))
        xq_pg = [EC(sb(f"xq_pg{i}", [128, E], BF16, side="right")) for i in range(3)]
        tkp = [EC(sb(f"tkp{i}", [128, CAP_T * 2], F32, side="right")) for i in range(2)]
        gip = [EC(sb(f"gip{i}", [128, CAP_T], U32, side="right")) for i in range(2)]
        ps_st = [EC(nc.psum_tensor(f"ps_st{i}", [128, 512], F32)) for i in range(2)]
        ps_o = [EC(nc.psum_tensor(f"ps_o{i}", [128, 258], F32)) for i in range(2)]

        s_tkp2 = [Sem(ctx, nc, f"s_tkp{i}") for i in range(2)]
        s_icp = Sem(ctx, nc, "s_icp")
        s_gxp3 = [Sem(ctx, nc, f"s_gxp{i}") for i in range(3)]
        s_wxp3 = [Sem(ctx, nc, f"s_wxp{i}") for i in range(3)]
        ALLSEMS.extend(s_tkp2 + s_gxp3 + s_wxp3 + [s_icp])
        pg_gx = {}
        pg_wx = {}

        def emit_pg_write(i):
            def _wx_g(g, i=i, gv=pg_gx[i]):
                g.wait_ge(gv[0], gv[1])
                g.dma_start(xg_scr[i * 128:(i + 1) * 128, :],
                            xq_pg[i % 3][:]).then_inc(s_wxp3[i % 3].s, 16)
            block.gpsimd(_wx_g)
            s_wxp3[i % 3].v += 16
            pg_wx[i] = (s_wxp3[i % 3].s, s_wxp3[i % 3].v)

        def emit_pg(e):
            def _tk_s(sy, e=e, icn=(5 * (e - 1) if e >= 2 else 0)):
                if e == 0:
                    sy.wait_ge(s_sc.s, sc_total)
                if icn:
                    sy.wait_ge(s_icp.s, icn)
                sy.dma_start(
                    tkp[e % 2][:].rearrange("p (c w) -> p c w", w=2),
                    toks2[e * CAP:(e + 1) * CAP, :].rearrange("(c p) w -> p c w", p=128),
                ).then_inc(s_tkp2[e % 2].s, 16)
            block.sync(_tk_s)
            s_tkp2[e % 2].v += 16
            tkv = s_tkp2[e % 2].v

            def _ic_v(v, e=e, tkv=tkv, gwait=(pg_gx.get(5 * (e - 2) + 4) if e >= 2 else None)):
                v.wait_ge(s_tkp2[e % 2].s, tkv)
                if gwait:
                    v.wait_ge(gwait[0], gwait[1])
                t2 = tkp[e % 2][:].rearrange("p (c w) -> p c w", w=2)
                for c in range(CAP_T):
                    v.tensor_copy(gip[e % 2][:, c:c + 1], t2[:, c, 0:1]).then_inc(s_icp.s, 1)
            block.vector(_ic_v)
            s_icp.v += CAP_T

            for c in range(CAP_T):
                i = e * CAP_T + c
                def _gx_g(g, i=i, e=e, c=c, wprev=pg_wx.get(i - 3)):
                    g.wait_ge(s_icp.s, 5 * e + c + 1)
                    if wprev:
                        g.wait_ge(wprev[0], wprev[1])
                    g.indirect_dma_start(
                        xq_pg[i % 3][:], None, xn_scr,
                        IOoA(gip[e % 2][:, c:c + 1], 0),
                    ).then_inc(s_gxp3[i % 3].s, 16)
                block.gpsimd(_gx_g)
                s_gxp3[i % 3].v += 16
                pg_gx[i] = (s_gxp3[i % 3].s, s_gxp3[i % 3].v)
                if i >= 2:
                    emit_pg_write(i - 2)

        def _c_pre(sy):
            sy.dma_start(qt4[0][:], qT_scr[0]).then_inc(s_x.s, 16)
            sy.dma_start(xob[0][:], x_own[0]).then_inc(s_x.s, 16)
            sy.dma_start(mask_sb[:], masks[:, 0:512]).then_inc(s_x.s, 16)
        block.sync(_c_pre)
        s_x.v += 48

        c_st0 = s_pe.v
        c_exp0 = s_act.v
        c_av0 = s_pe2.v
        c_ev0 = s_ev.v
        n_unit = 0
        n_grp = 0
        base_w_vals = []

        for k in range(c_slots):
            J = 4 * k + 4
            qt_ready = s_x.v

            for g in range(G):
                u0 = n_unit
                m = n_grp

                mul_base = s_dve.v

                def _t_grp(t, k=k, g=g, J=J, u0=u0, m=m, qt_ready=qt_ready, mul_base=mul_base):
                    def emit_st(j):
                        u = u0 + j
                        if u >= 2:
                            t.wait_ge(s_act.s, c_exp0 + u - 1)
                        t.matmul(ps_st[u % 2][:], KT_sb[g][:, j * 128:(j + 1) * 128],
                                 qt4[k % 2][:, g * 512:(g + 1) * 512],
                                 start=True, stop=True).then_inc(s_pe.s, 1)

                    def emit_av(j):
                        if j == 0 and m >= 1:
                            t.wait_ge(s_ev.s, c_ev0 + 8 * m)
                        if j >= J - 4:
                            t.wait_ge(s_dve.s, mul_base + (j - (J - 4)) + 1)
                        else:
                            t.wait_ge(s_act.s, c_exp0 + u0 + j + 1)
                        for h4 in range(4):
                            t.matmul(ps_o[h4 // 2][:, (h4 % 2) * 129:(h4 % 2) * 129 + 129],
                                     pt_sb[(u0 + j) % 2][:, h4 * 128:(h4 + 1) * 128],
                                     V_sb[:, (g * NT + j) * VST:(g * NT + j) * VST + 129],
                                     start=(j == 0 and h4 % 2 == 0),
                                     stop=(j == J - 1 and h4 % 2 == 1)).then_inc(s_pe2.s, 1)

                    if g == 0:
                        t.wait_ge(s_x.s, qt_ready)
                    emit_st(0)
                    if J > 1:
                        emit_st(1)
                    for j in range(J):
                        if j + 2 < J:
                            emit_st(j + 2)
                        emit_av(j)
                block.tensor(_t_grp)
                s_pe.v += J
                s_pe2.v += 4 * J
                av_all = s_pe2.v

                def _exp_a(s, J=J, u0=u0):
                    for j in range(J):
                        u = u0 + j
                        s.wait_ge(s_pe.s, c_st0 + u + 1)
                        if u >= 2:
                            s.wait_ge(s_pe2.s, c_av0 + 4 * (u - 1))
                        s.activation(pt_sb[u % 2][:], ps_st[u % 2][:], AF.Exp,
                                     bias=consts[:, 0:1]).then_inc(s_act.s, 1)
                block.scalar(_exp_a)
                s_act.v += J

                def _mul_v(v, J=J, u0=u0, g=g, qt_ready=qt_ready):
                    if g == 0:
                        v.wait_ge(s_x.s, qt_ready)
                    for j in range(J - 4, J):
                        u = u0 + j
                        v.wait_ge(s_act.s, c_exp0 + u + 1)
                        m_b = mask_sb[:, (j - (J - 4)) * 128:(j - (J - 4) + 1) * 128].unsqueeze(1).to_broadcast([128, 4, 128])
                        p_v = pt_sb[u % 2][:].rearrange("p (r n) -> p r n", r=4)
                        v.tensor_tensor(p_v, p_v, m_b, ALU.mult).then_inc(s_dve.s, 1)
                block.vector(_mul_v)
                s_dve.v += 4

                def _ev_v(v, g=g, av=av_all, ev_base=s_ev.v):
                    v.wait_ge(s_pe2.s, av)
                    for h4 in range(4):
                        v.reciprocal(linv_sb[:, h4:h4 + 1],
                                     ps_o[h4 // 2][:, (h4 % 2) * 129 + 128:(h4 % 2) * 129 + 129]).then_inc(s_ev.s, 1)
                        v.wait_ge(s_ev.s, ev_base + 2 * h4 + 1)
                        v.tensor_scalar(attn_sb[:, (4 * g + h4) * 128:(4 * g + h4 + 1) * 128],
                                        ps_o[h4 // 2][:, (h4 % 2) * 129:(h4 % 2) * 129 + 128],
                                        linv_sb[:, h4:h4 + 1], None, op0=ALU.mult).then_inc(s_ev.s, 1)
                block.vector(_ev_v)
                s_ev.v += 8

                n_unit += J
                n_grp += 1

            def _base_v(v, k=k, ev=s_ev.v, qr=qt_ready,
                        bw=(base_w_vals[k - 1] if k >= 1 else 0)):
                v.wait_ge(s_ev.s, ev)
                v.wait_ge(s_x.s, qr)
                if bw:
                    v.wait_ge(s_out.s, bw)
                v.tensor_add(base_sb[k % 2][:], attn_sb[:], xob[k % 2][:]).then_inc(s_dve.s, 1)
            block.vector(_base_v)
            s_dve.v += 1
            base_done = s_dve.v

            def _base_w(g, k=k, bd=base_done):
                g.wait_ge(s_dve.s, bd)
                g.dma_start(y[k * 128:(k + 1) * 128, :], base_sb[k % 2][:]).then_inc(s_out.s, 16)
            block.scalar(_base_w)
            s_out.v += 16
            base_w_vals.append(s_out.v)

            emit_pg(2 * k)
            emit_pg(2 * k + 1)
            if k == NOWN - 1:
                emit_pg_write(NTILE - 2)
                emit_pg_write(NTILE - 1)

            if k + 1 < NOWN:
                def _c_nx(sy, k=k, stv=s_pe.v, mulv=s_dve.v):
                    sy.wait_ge(s_pe.s, stv)
                    sy.wait_ge(s_dve.s, mulv)
                    sy.dma_start(qt4[(k + 1) % 2][:], qT_scr[k + 1]).then_inc(s_x.s, 16)
                    sy.dma_start(xob[(k + 1) % 2][:], x_own[k + 1]).then_inc(s_x.s, 16)
                    sy.dma_start(mask_sb[:],
                                 masks[:, (k + 1) * 512:(k + 2) * 512]).then_inc(s_x.s, 16)
                block.sync(_c_nx)
                s_x.v += 48

        barrier()
        ctxC.close()
    ctxBC.close()

    if phases >= 4:
        base_total = s_out.v
        ctxD = ExitStack()
        ED = ctxD.enter_context
        wgu_sb = [ED(sb(f"wgu{i}", [128, 16 * 1536], BF16)) for i in range(2)]
        wd_sb = [ED(sb(f"wdd{i}", [128, NIB * 2048], BF16)) for i in range(2)]
        xg = [ED(sb(f"xg{i}", [128, E], BF16)) for i in range(2)]
        xgT = [ED(sb(f"xgT{i}", [128, E], BF16)) for i in range(2)]
        Hg = ED(sb("Hg", [128, I], BF16))
        H1 = ED(sb("H1", [128, I], BF16))
        H1T = [ED(sb(f"H1T{i}", [128, NIB * 128], BF16)) for i in range(2)]
        ye = [ED(sb(f"ye{i}", [128, E], F32)) for i in range(2)]
        tk_sb = [ED(sb(f"tk{i}", [128, CAP_T * 2], F32)) for i in range(2)]
        gi_u = [ED(sb(f"gi{i}", [128, CAP_T], U32)) for i in range(2)]
        stag = [ED(sb(f"stag{i}", [128, 1024], F32)) for i in range(4)]
        ps_gate = ED(nc.psum_tensor("ps_gate", [128, I], F32))
        ps_up = ED(nc.psum_tensor("ps_up", [128, I], F32))
        ps_d = [ED(nc.psum_tensor(f"ps_d{i}", [128, 512], F32)) for i in range(2)]
        ps_tr = [ED(nc.psum_tensor(f"ps_trD{i}", [128, 128], BF16)) for i in range(2)]

        s_st = [Sem(ctx, nc, f"s_st{i}") for i in range(4)]
        s_cw = Sem(ctx, nc, "s_cw")
        s_tk = Sem(ctx, nc, "s_tk")
        s_ic = Sem(ctx, nc, "s_ic")
        s_gx2 = [Sem(ctx, nc, f"s_gx{i}") for i in range(2)]
        s_sa2 = [Sem(ctx, nc, f"s_sa{i}") for i in range(2)]
        s_sil = Sem(ctx, nc, "s_sil")
        s_h1 = Sem(ctx, nc, "s_h1")
        s_ye = Sem(ctx, nc, "s_ye")
        ALLSEMS.extend(s_st + s_gx2 + s_sa2 + [s_cw, s_tk, s_ic, s_sil, s_h1, s_ye])

        ye_base = s_ye.v
        pe2_base = s_pe2.v

        up_pe = {}
        gate_pe = {}
        dn_pe2 = {}
        sil_v = {}
        h1_v = {}
        htr_ev = {}
        xgtr_ev = {}
        xgtr_tr = {}
        gx_v = {}
        sa_v = {}
        ic_v = {}
        tk_v = {}
        gu_rdy = {}
        dn_rdy = {}
        wq_cnt = [0]

        def load_toks(e, extra=None):
            def _t_s(sy, e=e, extra=extra):
                if extra:
                    for sm, vv in extra:
                        if vv > 0:
                            sy.wait_ge(sm, vv)
                sy.dma_start(
                    tk_sb[e % 2][:].rearrange("p (c w) -> p c w", w=2),
                    toks2[e * CAP:(e + 1) * CAP, :].rearrange("(c p) w -> p c w", p=128),
                ).then_inc(s_tk.s, 16)
            block.sync(_t_s)
            s_tk.v += 16
            tk_v[e] = s_tk.v

        def cast_idx(e, extra=None):
            def _c_v(v, e=e, extra=extra):
                v.wait_ge(s_tk.s, tk_v[e])
                if extra:
                    for sm, vv in extra:
                        if vv > 0:
                            v.wait_ge(sm, vv)
                tk2 = tk_sb[e % 2][:].rearrange("p (c w) -> p c w", w=2)
                for c in range(CAP_T):
                    v.tensor_copy(gi_u[e % 2][:, c:c + 1], tk2[:, c, 0:1]).then_inc(s_ic.s, 1)
            block.vector(_c_v)
            s_ic.v += CAP_T
            ic_v[e] = s_ic.v

        def emit_wchunks(e, c0, c1):
            for c in range(c0, c1):
                q = wq_cnt[0]
                slot = q % 4
                src = (wgu_in[e][:, c * 1024:(c + 1) * 1024] if c < 24
                       else wd_in[e][:, (c - 24) * 1024:(c - 23) * 1024])
                need = q - 3
                def _ld(eng, slot=slot, src=src, need=need):
                    if need > 0:
                        eng.wait_ge(s_cw.s, need)
                    eng.dma_start(stag[slot][:], src).then_inc(s_st[slot].s, 16)
                (block.sync if q % 2 == 0 else block.scalar)(_ld)
                s_st[slot].v += 16
                stv = s_st[slot].v
                dst = (wgu_sb[e % 2][:, c * 1024:(c + 1) * 1024] if c < 24
                       else wd_sb[e % 2][:, (c - 24) * 1024:(c - 23) * 1024])
                def _ca(a, slot=slot, stv=stv, dst=dst, e=e, c=c):
                    a.wait_ge(s_st[slot].s, stv)
                    if c == 0 and e >= 2:
                        a.wait_ge(s_pe.s, up_pe[5 * (e - 2) + 4])
                    if c == 24 and e >= 2:
                        a.wait_ge(s_pe2.s, dn_pe2[5 * (e - 2) + 4])
                    a.activation(dst, stag[slot][:], AF.Copy).then_inc(s_cw.s, 1)
                block.scalar(_ca)
                wq_cnt[0] += 1
                if c == 23:
                    gu_rdy[e] = wq_cnt[0]
                if c == 35:
                    dn_rdy[e] = wq_cnt[0]

        def gather(i, first=False):
            def _gx_a(a, i=i):
                if i >= 2:
                    a.wait_ge(s_tr.s, xgtr_tr[i - 2])
                a.dma_start(xg[i % 2][:],
                            xg_scr[i * 128:(i + 1) * 128, :]).then_inc(s_gx2[i % 2].s, 16)
            block.sync(_gx_a)
            s_gx2[i % 2].v += 16
            gx_v[i] = s_gx2[i % 2].v

        def sa(i):
            e, c = i // CAP_T, i % CAP_T
            w0, w1 = s_sa2[0].v, s_sa2[1].v
            def _sa_g(g, i=i, e=e, c=c, w0=w0, w1=w1):
                g.wait_ge(s_ye.s, ye_base + 4 * (i + 1))
                if i == 0:
                    g.wait_ge(s_out.s, base_total)
                elif c == 0:
                    if w0:
                        g.wait_ge(s_sa2[0].s, w0)
                    if w1:
                        g.wait_ge(s_sa2[1].s, w1)
                g.indirect_dma_start(
                    y, IOoA(gi_u[e % 2][:, c:c + 1], 0),
                    ye[i % 2][:], None,
                    compute_op=ALU.add,
                ).then_inc(s_sa2[i % 2].s, 16)
            block.gpsimd(_sa_g)
            s_sa2[i % 2].v += 16
            sa_v[i] = s_sa2[i % 2].v

        def emit_xgtr(i):
            pw = [(s_gx2[i % 2].s, gx_v[i])]
            epw = []
            if i >= 2:
                epw.append((s_pe.s, up_pe[i - 2]))
            pairs = [(xg[i % 2][:, c * 128:(c + 1) * 128], xgT[i % 2][:, c * 128:(c + 1) * 128])
                     for c in range(NC_)]
            xgtr_tr[i] = s_tr.v + NC_
            xgtr_ev[i] = emit_transposes(pairs, pre_wait=pw, ev_pre_wait=epw or None)

        load_toks(0, extra=[(s_sc.s, sc_total)])
        cast_idx(0)
        load_toks(1)
        cast_idx(1)
        emit_wchunks(0, 0, 36)
        emit_wchunks(1, 0, 36)
        gather(0, first=True)
        gather(1)
        emit_xgtr(0)

        for i in range(NTILE):
            e, c = i // CAP_T, i % CAP_T
            es = e % 2

            if c == 0 and e >= 1 and e + 1 < NEXP:
                load_toks(e + 1, extra=[(s_ye.s, ye_base + 4 * (5 * e)),
                                        (s_ic.s, ic_v[e])])
            if c == 1 and e >= 1 and e + 1 < NEXP:
                cast_idx(e + 1, extra=[(s_sa2[(5 * e - 1) % 2].s, sa_v[5 * e - 1])])

            def _gu_pe(t, i=i, es=es, e=e):
                t.wait_ge(s_tev.s, xgtr_ev[i])
                t.wait_ge(s_cw.s, gu_rdy[e])
                if i >= 1:
                    t.wait_ge(s_sil.s, sil_v[i - 1])
                last = None
                for c16 in range(NC_):
                    lhs = xgT[i % 2][:, c16 * 128:(c16 + 1) * 128]
                    b = c16 * 1536
                    t.matmul(ps_gate[:, 0:512], lhs, wgu_sb[es][:, b:b + 512],
                             start=(c16 == 0), stop=False)
                    last = t.matmul(ps_gate[:, 512:768], lhs, wgu_sb[es][:, b + 512:b + 768],
                                    start=(c16 == 0), stop=(c16 == NC_ - 1))
                last.then_inc(s_pe.s, 1)
                if i >= 1:
                    t.wait_ge(s_h1.s, h1_v[i - 1])
                last = None
                for c16 in range(NC_):
                    lhs = xgT[i % 2][:, c16 * 128:(c16 + 1) * 128]
                    b = c16 * 1536 + 768
                    t.matmul(ps_up[:, 0:512], lhs, wgu_sb[es][:, b:b + 512],
                             start=(c16 == 0), stop=False)
                    last = t.matmul(ps_up[:, 512:768], lhs, wgu_sb[es][:, b + 512:b + 768],
                                    start=(c16 == 0), stop=(c16 == NC_ - 1))
                last.then_inc(s_pe.s, 1)
            block.tensor(_gu_pe)
            s_pe.v += 2
            gate_pe[i] = s_pe.v - 1
            up_pe[i] = s_pe.v

            def _sil_a(s, i=i):
                s.wait_ge(s_pe.s, gate_pe[i])
                if i >= 1:
                    s.wait_ge(s_h1.s, h1_v[i - 1])
                s.activation(Hg[:], ps_gate[:], AF.Silu).then_inc(s_sil.s, 1)
            block.scalar(_sil_a)
            s_sil.v += 1
            sil_v[i] = s_sil.v

            if i >= 1:
                ip = i - 1
                ep, cp = ip // CAP_T, ip % CAP_T
                def _ye_v(v, ip=ip, ep=ep, cp=cp):
                    if ip >= 2:
                        v.wait_ge(s_sa2[ip % 2].s, sa_v[ip - 2])
                    tk2 = tk_sb[ep % 2][:].rearrange("p (c w) -> p c w", w=2)
                    for j in range(4):
                        v.wait_ge(s_pe2.s, pe2_base + 4 * ip + j + 1)
                        v.tensor_scalar(ye[ip % 2][:, j * 512:(j + 1) * 512],
                                        ps_d[j % 2][:], tk2[:, cp, 1:2], None,
                                        op0=ALU.mult).then_inc(s_ye.s, 1)
                block.vector(_ye_v)
                s_ye.v += 4

            def _h1_v(v, i=i):
                v.wait_ge(s_sil.s, sil_v[i])
                v.wait_ge(s_pe.s, up_pe[i])
                v.tensor_tensor(H1[:], ps_up[:], Hg[:], ALU.mult).then_inc(s_h1.s, 1)
            block.vector(_h1_v)
            s_h1.v += 1
            h1_v[i] = s_h1.v

            epw = []
            if i >= 2:
                epw.append((s_pe2.s, dn_pe2[i - 2]))
            pairs = [(H1[:, c2 * 128:(c2 + 1) * 128], H1T[i % 2][:, c2 * 128:(c2 + 1) * 128])
                     for c2 in range(NIB)]
            htr_ev[i] = emit_transposes(
                pairs, pre_wait=[(s_h1.s, h1_v[i])], ev_pre_wait=epw or None)

            if i + 1 < NTILE:
                emit_xgtr(i + 1)

            def _dn_pe(t, i=i, es=es, e=e):
                t.wait_ge(s_tev.s, htr_ev[i])
                t.wait_ge(s_cw.s, dn_rdy[e])
                for j in range(4):
                    gidx = 4 * i + j
                    if gidx >= 2:
                        t.wait_ge(s_ye.s, ye_base + gidx - 1)
                    last = None
                    for ib in range(NIB):
                        last = t.matmul(ps_d[j % 2][:],
                                        H1T[i % 2][:, ib * 128:(ib + 1) * 128],
                                        wd_sb[es][:, ib * 2048 + j * 512: ib * 2048 + (j + 1) * 512],
                                        start=(ib == 0), stop=(ib == NIB - 1))
                    last.then_inc(s_pe2.s, 1)
            block.tensor(_dn_pe)
            s_pe2.v += 4
            dn_pe2[i] = s_pe2.v

            if e >= 1 and e + 1 < NEXP:
                rngs = [(0, 8), (8, 16), (16, 24), (24, 32), (32, 36)]
                emit_wchunks(e + 1, *rngs[c])

            if i + 2 < NTILE:
                gather(i + 2)
            if i >= 1:
                sa(i - 1)

        ip = NTILE - 1
        ep, cp = ip // CAP_T, ip % CAP_T
        def _ye_last(v, ip=ip, ep=ep, cp=cp):
            v.wait_ge(s_sa2[ip % 2].s, sa_v[ip - 2])
            tk2 = tk_sb[ep % 2][:].rearrange("p (c w) -> p c w", w=2)
            for j in range(4):
                v.wait_ge(s_pe2.s, pe2_base + 4 * ip + j + 1)
                v.tensor_scalar(ye[ip % 2][:, j * 512:(j + 1) * 512],
                                ps_d[j % 2][:], tk2[:, cp, 1:2], None,
                                op0=ALU.mult).then_inc(s_ye.s, 1)
        block.vector(_ye_last)
        s_ye.v += 4
        sa(NTILE - 1)

        def _fin(g, sv0=s_sa2[0].v, sv1=s_sa2[1].v, outv=s_out.v):
            g.wait_ge(s_sa2[0].s, sv0)
            g.wait_ge(s_sa2[1].s, sv1)
            g.wait_ge(s_out.s, outv)
        block.gpsimd(_fin)
        ctxD.close()

    ctx.close()
    return nc


def _prep_inputs(L0_attn_x, L0_attn_Wq, L0_attn_bq, L0_attn_Wk, L0_attn_bk,
                 L0_attn_Wv, L0_attn_bv, L0_ffn_router_w, L0_ffn_w_gate,
                 L0_ffn_w_up, L0_ffn_w_down):
    f = np.ascontiguousarray
    x = np.asarray(L0_attn_x, np.float32)
    wq = np.asarray(L0_attn_Wq, np.float32)
    bq = np.asarray(L0_attn_bq, np.float32)
    wk = np.asarray(L0_attn_Wk, np.float32)
    bk = np.asarray(L0_attn_bk, np.float32)
    wv = np.asarray(L0_attn_Wv, np.float32)
    bv = np.asarray(L0_attn_bv, np.float32)
    rw = np.asarray(L0_ffn_router_w, np.float32)
    wg = np.asarray(L0_ffn_w_gate, np.float32)
    wu = np.asarray(L0_ffn_w_up, np.float32)
    wd = np.asarray(L0_ffn_w_down, np.float32)

    wq_s = f(wq.transpose(1, 0, 2).reshape(2048, 2048).reshape(NC_, 128, 2048).transpose(1, 0, 2))
    wqb_s = f(bq.reshape(1, 2048))
    wk_s = f(wk.reshape(G, NC_, 128, 128).transpose(2, 0, 1, 3).reshape(128, G * NC_ * 128))
    bkT_s = f(bk.T)
    wv_s = f(wv.transpose(1, 0, 2).reshape(2048, 512).reshape(NC_, 128, 512).transpose(1, 0, 2).reshape(128, NC_ * 512))
    bv_s = f(bv.reshape(1, 512))
    rw_pad = np.zeros((2048, 128), np.float32)
    rw_pad[:, :32] = rw.T
    rw_s = f(rw_pad.reshape(NC_, 128, 128).transpose(1, 0, 2).reshape(128, NC_ * 128))
    ident = np.eye(128, dtype=ml_dtypes.bfloat16)

    wgt = wg.reshape(NEXP, 16, 128, I).transpose(0, 2, 1, 3)
    wut = wu.reshape(NEXP, 16, 128, I).transpose(0, 2, 1, 3)
    wgu_s = f(np.concatenate([wgt, wut], axis=3).reshape(NEXP, 128, 16 * 1536))
    wd_s = f(wd.reshape(NEXP, NIB, 128, 2048).transpose(0, 2, 1, 3).reshape(NEXP, 128, NIB * 2048))

    tri_c = f(np.triu(np.ones((128, 128), np.float32)).astype(ml_dtypes.bfloat16))
    ones_c = np.ones((128, 128), ml_dtypes.bfloat16)
    ecap_c = f(np.broadcast_to(np.tile(np.arange(NEXP, dtype=np.float32) * CAP - 1.0, NOWN),
                               (128, NOWN * 32)).copy())
    tokid_c = f((np.arange(NOWN, dtype=np.float32)[None, :] * 128
                 + np.arange(128, dtype=np.float32)[:, None]))

    kk_ar = np.arange(128)
    masks_by_r = []
    for r in range(4):
        m = np.zeros((NOWN * 4, 128, 128), np.float32)
        for k in range(NOWN):
            qpos = (4 * k + r) * 128 + kk_ar
            for jj in range(4):
                kpos = (4 * k + jj) * 128 + kk_ar
                m[k * 4 + jj] = (kpos[:, None] <= qpos[None, :])
        masks_by_r.append(f(m.astype(ml_dtypes.bfloat16).transpose(1, 0, 2).reshape(128, NOWN * 4 * 128)))

    shared = dict(wq_s=wq_s, wqb_s=wqb_s, wk_s=wk_s, bkT_s=bkT_s, wv_s=wv_s,
                  bv_s=bv_s, rw_s=rw_s, wgu_in=wgu_s, wd_in=wd_s, ident=ident,
                  tri_c=tri_c, ones_c=ones_c, ecap_c=ecap_c, tokid_c=tokid_c)
    in_maps = []
    for c in range(8):
        b, r = c // 4, c % 4
        xb = x[b].reshape(NT, 128, 2048)
        m = dict(shared)
        m["x_own"] = f(xb[r::4])
        m["x_full"] = f(xb)
        m["masks"] = masks_by_r[r]
        in_maps.append(m)
    return in_maps


_NC_CACHE = {}


def kernel(**inputs):
    key = (5, False)
    if key not in _NC_CACHE:
        _NC_CACHE[key] = build(phases=4, dbg=False)
    nc = _NC_CACHE[key]
    in_maps = _prep_inputs(**inputs)
    res = run_bass_kernel_spmd(nc, in_maps, core_ids=list(range(8)))
    out = np.empty((B, S, E), np.float32)
    for c in range(8):
        b, r = c // 4, c % 4
        out[b].reshape(NT, 128, 2048)[r::4] = res.results[c]["y"][:NOWN * 128].reshape(NOWN, 128, E)
    return out


# version 12 (final)
# speedup vs baseline: 1.1930x; 1.0159x over previous
import math
import numpy as np
import ml_dtypes
from contextlib import ExitStack

import concourse.bass as bass
import concourse.mybir as mybir
from concourse.bass_utils import run_bass_kernel_spmd

F32 = mybir.dt.float32
BF16 = mybir.dt.bfloat16
U32 = mybir.dt.uint32
AF = mybir.ActivationFunctionType
ALU = mybir.AluOpType
AX = mybir.AxisListType
IOoA = bass.IndirectOffsetOnAxis

B, S, E = 2, 8192, 2048
G, D = 4, 128
NEXP, I = 32, 768
NT = S // 128
NOWN = 16
NC_ = E // 128
NIB = I // 128
CSH = 20.0
ALPHA = 1.0 / math.sqrt(D)
VST = 132
MOFF = [sum(4 * kk + 4 for kk in range(k)) for k in range(NOWN + 1)]
CAP_T = 5
CAP = CAP_T * 128
NSLOT = NEXP * CAP
NTILE = NEXP * CAP_T
TOKS_ROWS = NSLOT + 128
DUMMY_TOK = 2048.0
YROWS = NOWN * 128 + 128


class Sem:
    def __init__(self, ctx, nc, name):
        self.s = ctx.enter_context(nc.semaphore(name))
        self.v = 0


def build(phases=4, dbg=False, c_slots=NOWN):
    nc = bass.Bass("TRN2", target_bir_lowering=False, debug=False)

    def din(name, shape, dt):
        return nc.dram_tensor(name, shape, dt, kind="ExternalInput").ap()

    def dscr(name, shape, dt):
        kind = "ExternalOutput" if dbg else "Internal"
        return nc.dram_tensor(name, shape, dt, kind=kind).ap()

    x_own = din("x_own", [NOWN, 128, E], F32)
    x_full = din("x_full", [NT, 128, E], F32)
    wq_s = din("wq_s", [128, NC_, 2048], F32)
    wqb_s = din("wqb_s", [1, 2048], F32)
    wk_s = din("wk_s", [128, G * NC_ * 128], F32)
    bkT_s = din("bkT_s", [128, G], F32)
    wv_s = din("wv_s", [128, NC_ * 512], F32)
    bv_s = din("bv_s", [1, 512], F32)
    rw_s = din("rw_s", [128, NC_ * 128], F32)
    wgu_in = din("wgu_in", [NEXP, 128, 16 * 1536], F32)
    wd_in = din("wd_in", [NEXP, 128, NIB * 2048], F32)
    masks = din("masks", [128, NOWN * 4 * 128], BF16)
    ident = din("ident", [128, 128], BF16)
    tri_c = din("tri_c", [128, 128], BF16)
    ones_c = din("ones_c", [128, 128], BF16)
    ecap_c = din("ecap_c", [128, NOWN * 32], F32)
    tokid_c = din("tokid_c", [128, NOWN], F32)

    qT_scr = dscr("qT_scr", [NOWN, 128, 2048], BF16)
    xn_scr = dscr("xn_scr", [YROWS, E], BF16)
    toks2 = dscr("toks2", [TOKS_ROWS, 2], F32)
    xg_scr = dscr("xg_scr", [NTILE * 128, E], BF16)
    y = nc.dram_tensor("y", [YROWS, E], F32, kind="ExternalOutput").ap()

    ctx = ExitStack()
    EE = ctx.enter_context

    def sb(name, shape, dt, side="left"):
        return nc.sbuf_tensor(name, shape, dt, side=side)

    consts = EE(sb("consts", [128, 8], F32))
    ones_col = EE(sb("ones_col", [1, 128], BF16))
    id_sb = EE(sb("id_sb", [128, 128], BF16))
    w_sb = EE(sb("w_sb", [128, NOWN * 32], F32))
    stat6 = EE(sb("stat6", [128, 24], F32))
    mv = EE(sb("mv", [128, 8], F32))
    pay8 = EE(sb("pay8", [128, NOWN * 8 * 2], F32))
    pos8_u = EE(sb("pos8_u", [128, NOWN * 8], U32))

    s_x = Sem(ctx, nc, "s_x")
    s_g = Sem(ctx, nc, "s_g")
    s_out = Sem(ctx, nc, "s_out")
    s_pe = Sem(ctx, nc, "s_pe")
    s_tr = Sem(ctx, nc, "s_tr")
    s_pe2 = Sem(ctx, nc, "s_pe2")
    s_dve = Sem(ctx, nc, "s_dve")
    s_ev = Sem(ctx, nc, "s_ev")
    s_act = Sem(ctx, nc, "s_act")
    s_ti = Sem(ctx, nc, "s_ti")
    s_tev = Sem(ctx, nc, "s_tev")
    s_xw = Sem(ctx, nc, "s_xw")
    s_sc = Sem(ctx, nc, "s_sc")

    block = EE(nc.Block())
    ALLSEMS = [s_x, s_g, s_out, s_pe, s_tr, s_pe2, s_dve, s_ev, s_act, s_ti, s_tev, s_xw]

    def barrier():
        vals = [(sm.s, sm.v) for sm in ALLSEMS]
        for name in ("sync", "tensor", "vector", "scalar", "gpsimd"):
            def _f(eng, vals=vals):
                for s, v in vals:
                    if v > 0:
                        eng.wait_ge(s, v)
            getattr(block, name)(_f)

    def emit_ln(x_ready_val, xb, xn_f=None, xn_lo=None, xn_hi=None, extra_wait=None):
        def _v1(v, xv=x_ready_val):
            v.wait_ge(s_x.s, xv)
            for i in range(3):
                v.bn_stats(stat6[:, i * 6:(i + 1) * 6], xb[:, i * 512:(i + 1) * 512])
            v.bn_stats(stat6[:, 18:24], xb[:, 1536:2048]).then_inc(s_dve.s, 1)
            v.wait_ge(s_dve.s, s_dve.v + 1)
            v.bn_aggr(mv[:, 0:2], stat6[:]).then_inc(s_dve.s, 1)
        block.vector(_v1)
        s_dve.v += 2
        mv_ready = s_dve.v

        def _a1(s, mvv=mv_ready):
            s.wait_ge(s_dve.s, mvv)
            s.activation(mv[:, 2:3], mv[:, 1:2], AF.Ln, bias=consts[:, 2:3]).then_inc(s_act.s, 1)
            s.wait_ge(s_act.s, s_act.v + 1)
            s.activation(mv[:, 3:4], mv[:, 2:3], AF.Exp, scale=consts[:, 3:4]).then_inc(s_act.s, 1)
        block.scalar(_a1)
        s_act.v += 2
        rstd_ready = s_act.v

        def _v2(v, rv=rstd_ready, ew=extra_wait):
            v.wait_ge(s_act.s, rv)
            if ew:
                for sm, vv in ew:
                    if vv > 0:
                        v.wait_ge(sm, vv)
            if xn_lo is not None:
                v.tensor_scalar(xn_hi[:], xb[:], mv[:, 0:1], mv[:, 3:4], op0=ALU.subtract, op1=ALU.mult)
                v.tensor_scalar(xn_f[:], xb[:], mv[:, 0:1], mv[:, 3:4], op0=ALU.subtract, op1=ALU.mult).then_inc(s_dve.s, 1)
                v.wait_ge(s_dve.s, s_dve.v + 1)
                v.tensor_sub(xn_lo[:], xn_f[:], xn_hi[:]).then_inc(s_dve.s, 1)
            else:
                v.tensor_scalar(xn_hi[:], xb[:], mv[:, 0:1], mv[:, 3:4], op0=ALU.subtract, op1=ALU.mult).then_inc(s_dve.s, 1)
        block.vector(_v2)
        s_dve.v += 2 if xn_lo is not None else 1
        return s_dve.v

    tr_parity = [0]
    ps_tr = None

    def emit_transposes(pairs, pre_wait=None, ev_pre_wait=None):
        n = len(pairs)
        tr0 = s_tr.v

        def _pe(t, pre_wait=pre_wait, tr0=tr0):
            if pre_wait:
                for sm, vv in pre_wait:
                    if vv > 0:
                        t.wait_ge(sm, vv)
            for i, (src, dst) in enumerate(pairs):
                gi = tr0 + i
                if gi >= 2:
                    t.wait_ge(s_tev.s, gi - 1)
                t.transpose(ps_tr[(tr_parity[0] + i) % 2][:], src, id_sb[:]).then_inc(s_tr.s, 1)
        block.tensor(_pe)
        s_tr.v += n

        def _ve(v, tr0=tr0, par=tr_parity[0], epw=ev_pre_wait):
            if epw:
                for sm, vv in epw:
                    if vv > 0:
                        v.wait_ge(sm, vv)
            for i, (src, dst) in enumerate(pairs):
                v.wait_ge(s_tr.s, tr0 + i + 1)
                v.tensor_copy(dst, ps_tr[(par + i) % 2][:]).then_inc(s_tev.s, 1)
        block.vector(_ve)
        s_tev.v += n
        tr_parity[0] = (tr_parity[0] + n) % 2
        return s_tev.v

    def _pro_v(v):
        v.memset(consts[:, 0:1], -CSH)
        v.memset(consts[:, 1:2], 1.0 / 2048.0)
        v.memset(consts[:, 2:3], 1e-5)
        v.memset(consts[:, 3:4], -0.5)
        v.memset(consts[:, 4:5], DUMMY_TOK)
        v.memset(ones_col[:], 1.0).then_inc(s_dve.s, 1)
    block.vector(_pro_v)
    s_dve.v += 1
    pro_dve = s_dve.v

    ctxAB = ExitStack()
    if phases >= 1:
        ctxA = ExitStack()
        EA = ctxA.enter_context
        xbuf = [ctxAB.enter_context(sb(f"xbuf{i}", [128, E], F32, side="right")) for i in range(2)]
        xn_hi = ctxAB.enter_context(sb("xn_hi", [128, E], BF16, side="right"))
        wq_sb = EA(sb("wq_sb", [128, NC_ * 2048], BF16))
        wqb_sb = EA(sb("wqb_sb", [1, 2048], BF16))
        rw_f = EA(sb("rw_f", [128, NC_ * 128], F32))
        rw_hi = EA(sb("rw_hi", [128, NC_ * 128], BF16))
        rw_lo = EA(sb("rw_lo", [128, NC_ * 128], BF16))
        xn_f = EA(sb("xn_f", [128, E], F32))
        xn_lo = EA(sb("xn_lo", [128, E], BF16))
        xnT_sb = EA(sb("xnT_sb", [128, E], BF16))
        xnT_lo_sb = EA(sb("xnT_lo_sb", [128, E], BF16))
        qbf = EA(sb("qbf", [128, 2048], BF16))
        qt_sb = EA(sb("qt_sb", [128, 2048], BF16))
        pr_p = EA(sb("pr_p", [128, 32], F32))
        pr_m8 = EA(sb("pr_m8", [128, 8], F32))
        pr_zap = EA(sb("pr_zap", [128, 32], F32))
        pr_s = EA(sb("pr_s", [128, 2], F32))
        tri_sb = EA(sb("tri_sb", [128, 128], BF16))
        onesq_sb = EA(sb("onesq_sb", [128, 128], BF16))
        ecap_sb = EA(sb("ecap_sb", [128, NOWN * 32], F32))
        tokid_sb = EA(sb("tokid_sb", [128, NOWN], F32))
        init_sb = EA(sb("init_sb", [128, (TOKS_ROWS // 128) * 2], F32))
        a_bf = EA(sb("a_bf", [128, NOWN * 32], BF16))
        a_f = EA(sb("a_f", [128, NOWN * 32], F32))
        tmp512 = EA(sb("tmp512", [128, NOWN * 32], F32))
        cA = EA(sb("cA", [128, NOWN * 32], F32))
        cB = EA(sb("cB", [128, NOWN * 32], F32))
        mjt = EA(sb("mjt", [128, NOWN * 32], F32))
        mj2 = EA(sb("mj2", [128, NOWN * 32], F32))
        pos8_f = EA(sb("pos8_f", [128, NOWN * 8], F32))
        ps_tr = [EA(nc.psum_tensor(f"ps_trA{i}", [128, 128], BF16)) for i in range(2)]
        ps_q = EA(nc.psum_tensor("ps_q", [128, 2048], F32))
        ps_r = EA(nc.psum_tensor("ps_r", [128, 32], F32))
        ps_rank = EA(nc.psum_tensor("ps_rank", [128, NOWN * 32], F32))

        def _a_g(g):
            g.dma_start(wq_sb[:].rearrange("p (c n) -> p c n", c=NC_), wq_s).then_inc(s_g.s, 16)
            g.dma_start(wqb_sb[:], wqb_s).then_inc(s_g.s, 16)
            g.dma_start(id_sb[:], ident).then_inc(s_g.s, 16)
            g.dma_start(tri_sb[:], tri_c).then_inc(s_g.s, 16)
            g.dma_start(onesq_sb[:], ones_c).then_inc(s_g.s, 16)
            g.dma_start(ecap_sb[:], ecap_c).then_inc(s_g.s, 16)
            g.dma_start(tokid_sb[:], tokid_c).then_inc(s_g.s, 16)
        block.gpsimd(_a_g)
        s_g.v += 112
        gA_ready = s_g.v
        id_ready = s_g.v

        def _a_s(sy):
            sy.dma_start(rw_f[:], rw_s).then_inc(s_x.s, 16)
            sy.dma_start(xbuf[0][:], x_own[0]).then_inc(s_x.s, 16)
        block.sync(_a_s)
        s_x.v += 32

        def _a_v0(v):
            v.memset(qbf[:], 0.0).then_inc(s_dve.s, 1)
            v.memset(init_sb[:], 0.0).then_inc(s_dve.s, 1)
            v.wait_ge(s_dve.s, s_dve.v + 2)
            iv = init_sb[:].rearrange("p (r w) -> p r w", w=2)
            v.tensor_copy(iv[:, :, 0:1],
                          consts[:, 4:5].rearrange("p (a b) -> p a b", b=1)
                          .to_broadcast([128, TOKS_ROWS // 128, 1])).then_inc(s_dve.s, 1)
        block.vector(_a_v0)
        s_dve.v += 3
        init_done = s_dve.v

        def _a_g0(g, idn=init_done):
            g.wait_ge(s_dve.s, idn)
            g.dma_start(toks2.rearrange("(p r) w -> p (r w)", p=128), init_sb[:]).then_inc(s_ti.s, 16)
            g.dma_start(xn_scr[NOWN * 128:YROWS, :], qbf[:]).then_inc(s_ti.s, 16)
        block.gpsimd(_a_g0)
        s_ti.v += 32
        ti_done = s_ti.v

        def _a_v(v, xv=s_x.v):
            v.wait_ge(s_x.s, xv)
            v.tensor_copy(rw_hi[:], rw_f[:]).then_inc(s_dve.s, 1)
            v.wait_ge(s_dve.s, s_dve.v + 1)
            v.tensor_sub(rw_lo[:], rw_f[:], rw_hi[:]).then_inc(s_dve.s, 1)
        block.vector(_a_v)
        s_dve.v += 2
        rw_ready = s_dve.v

        prp_done_prev = 0
        w_done_prev = 0
        qt_out_prev = 0
        q_done_prev = 0
        xnw_prev = 0
        for k in range(NOWN):
            ew = [(s_xw.s, xnw_prev)] if xnw_prev else None
            ln_done = emit_ln(s_x.v, xbuf[k % 2], xn_f=xn_f, xn_lo=xn_lo, xn_hi=xn_hi,
                              extra_wait=ew)

            if k + 1 < NOWN:
                def _nx(sy, k=k, need=ln_done):
                    sy.wait_ge(s_dve.s, need)
                    sy.dma_start(xbuf[(k + 1) % 2][:], x_own[k + 1]).then_inc(s_x.s, 16)
                block.sync(_nx)
                s_x.v += 16

            def _xnw(g, k=k, need=ln_done):
                g.wait_ge(s_dve.s, need)
                g.dma_start(xn_scr[k * 128:(k + 1) * 128, :], xn_hi[:]).then_inc(s_xw.s, 16)
            block.gpsimd(_xnw)
            s_xw.v += 16
            xnw_prev = s_xw.v

            pw = [(s_dve.s, ln_done), (s_g.s, id_ready)]
            if qt_out_prev:
                pw.append((s_out.s, qt_out_prev))
            if q_done_prev:
                pw.append((s_pe.s, q_done_prev))
            pairs = [(xn_hi[:, c * 128:(c + 1) * 128], xnT_sb[:, c * 128:(c + 1) * 128]) for c in range(NC_)]
            pairs += [(xn_lo[:, c * 128:(c + 1) * 128], xnT_lo_sb[:, c * 128:(c + 1) * 128]) for c in range(NC_)]
            ev_hi_lo = emit_transposes(pairs, pre_wait=pw)

            def _q_pe(t, k=k, ev=ev_hi_lo, pd=prp_done_prev):
                t.wait_ge(s_tev.s, ev)
                if k == 0:
                    t.wait_ge(s_g.s, gA_ready)
                    t.wait_ge(s_dve.s, rw_ready)
                if pd:
                    t.wait_ge(s_act.s, pd)
                for c in range(NC_):
                    lhs = xnT_sb[:, c * 128:(c + 1) * 128]
                    for n4 in range(4):
                        t.matmul(ps_q[:, n4 * 512:(n4 + 1) * 512], lhs,
                                 wq_sb[:, c * 2048 + n4 * 512: c * 2048 + (n4 + 1) * 512],
                                 start=(c == 0), stop=False)
                    t.matmul(ps_r[:], lhs, rw_hi[:, c * 128:c * 128 + 32], start=(c == 0), stop=False)
                    t.matmul(ps_r[:], lhs, rw_lo[:, c * 128:c * 128 + 32], start=False, stop=False)
                    mm = t.matmul(ps_r[:], xnT_lo_sb[:, c * 128:(c + 1) * 128],
                                  rw_hi[:, c * 128:c * 128 + 32], start=False, stop=(c == NC_ - 1))
                    if c == NC_ - 1:
                        mm.then_inc(s_pe.s, 1)
                last = None
                for n4 in range(4):
                    last = t.matmul(ps_q[:, n4 * 512:(n4 + 1) * 512], ones_col[0:1, :],
                                    wqb_sb[0:1, n4 * 512:(n4 + 1) * 512], start=False, stop=True)
                last.then_inc(s_pe.s, 1)
            block.tensor(_q_pe)
            s_pe.v += 2
            q_done = s_pe.v
            q_done_prev = q_done

            def _q_v(v, k=k, qd=q_done):
                v.wait_ge(s_pe.s, qd)
                if k == 0:
                    v.wait_ge(s_ti.s, ti_done)
                v.tensor_scalar(qbf[:], ps_q[:], ALPHA, None, op0=ALU.mult).then_inc(s_dve.s, 1)
            block.vector(_q_v)
            s_dve.v += 1
            qbf_done = s_dve.v

            pairs = [(qbf[:, c * 128:(c + 1) * 128], qt_sb[:, c * 128:(c + 1) * 128]) for c in range(NC_)]
            ev_qt = emit_transposes(pairs, pre_wait=[(s_dve.s, qbf_done)])

            def _qtw(g, k=k, ev=ev_qt):
                g.wait_ge(s_tev.s, ev)
                g.dma_start(qT_scr[k], qt_sb[:]).then_inc(s_out.s, 16)
            block.gpsimd(_qtw)
            s_out.v += 16
            qt_out_prev = s_out.v

            def _r_a(s, qd=q_done, wdp=w_done_prev):
                s.wait_ge(s_pe.s, qd - 1)
                if wdp:
                    s.wait_ge(s_dve.s, wdp)
                s.activation(pr_p[:], ps_r[:], AF.Exp, bias=consts[:, 0:1]).then_inc(s_act.s, 1)
            block.scalar(_r_a)
            s_act.v += 1
            prp_done_prev = s_act.v

            def _r_v(v, k=k, pv=s_act.v):
                v.wait_ge(s_act.s, pv)
                v.max(out=pr_m8[:], in_=pr_p[:]).then_inc(s_dve.s, 1)
                v.wait_ge(s_dve.s, s_dve.v + 1)
                v.match_replace(out=pr_zap[:], in_to_replace=pr_m8[:], in_values=pr_p[:], imm_value=0.0).then_inc(s_dve.s, 1)
                v.wait_ge(s_dve.s, s_dve.v + 2)
                v.tensor_sub(pr_zap[:], pr_p[:], pr_zap[:]).then_inc(s_dve.s, 1)
                v.wait_ge(s_dve.s, s_dve.v + 3)
                v.reduce_sum(pr_s[:, 0:1], pr_zap[:], axis=AX.X).then_inc(s_dve.s, 1)
                v.wait_ge(s_dve.s, s_dve.v + 4)
                v.reciprocal(pr_s[:, 1:2], pr_s[:, 0:1]).then_inc(s_dve.s, 1)
                v.wait_ge(s_dve.s, s_dve.v + 5)
                v.tensor_scalar(w_sb[:, k * 32:(k + 1) * 32], pr_zap[:], pr_s[:, 1:2], None,
                                op0=ALU.mult).then_inc(s_dve.s, 1)
            block.vector(_r_v)
            s_dve.v += 6
            w_done_prev = s_dve.v

        def _ri_v(v, wd=w_done_prev):
            v.wait_ge(s_dve.s, wd)
            v.tensor_scalar(a_bf[:], w_sb[:], 0.0, None, op0=ALU.is_gt).then_inc(s_dve.s, 1)
            v.wait_ge(s_dve.s, s_dve.v + 1)
            v.tensor_scalar(a_f[:], w_sb[:], 0.0, None, op0=ALU.is_gt).then_inc(s_dve.s, 1)
        block.vector(_ri_v)
        s_dve.v += 2
        a_done = s_dve.v

        def _rk_pe(t, ad=a_done):
            t.wait_ge(s_dve.s, ad)
            last = None
            for k in range(NOWN):
                for kp in range(k + 1):
                    last = t.matmul(ps_rank[:, k * 32:(k + 1) * 32],
                                    tri_sb[:] if kp == k else onesq_sb[:],
                                    a_bf[:, kp * 32:(kp + 1) * 32],
                                    start=(kp == 0), stop=(kp == k))
            last.then_inc(s_pe.s, 1)
        block.tensor(_rk_pe)
        s_pe.v += 1
        rank_done = s_pe.v

        def _pos_v(v, rd=rank_done):
            base = s_dve.v
            cnt = [0]
            def stepped(ins):
                v.wait_ge(s_dve.s, base + cnt[0])
                ins().then_inc(s_dve.s, 1)
                cnt[0] += 1
            v.wait_ge(s_pe.s, rd)
            v.wait_ge(s_g.s, gA_ready)
            stepped(lambda: v.tensor_tensor(tmp512[:], ps_rank[:], ecap_sb[:], ALU.add))
            stepped(lambda: v.tensor_copy(cA[:], a_f[:]))
            src, dst = cA, cB
            for sh in (1, 2, 4, 8, 16):
                s3 = src[:].rearrange("p (k e) -> p k e", e=32)
                d3 = dst[:].rearrange("p (k e) -> p k e", e=32)
                v.wait_ge(s_dve.s, base + cnt[0])
                v.tensor_copy(d3[:, :, 0:sh], s3[:, :, 0:sh])
                v.tensor_tensor(d3[:, :, sh:32], s3[:, :, sh:32],
                                s3[:, :, 0:32 - sh], ALU.add).then_inc(s_dve.s, 1)
                cnt[0] += 1
                src, dst = dst, src
            pos3 = pos8_f[:].rearrange("p (k j) -> p k j", j=8)
            pay4 = pay8[:].rearrange("p (k j w) -> p k j w", j=8, w=2)
            for j in range(8):
                stepped(lambda j=j: v.tensor_scalar(mjt[:], src[:], float(j + 1), None, op0=ALU.is_equal))
                stepped(lambda: v.tensor_tensor(mjt[:], mjt[:], a_f[:], ALU.mult))
                stepped(lambda: v.tensor_tensor(mj2[:], mjt[:], tmp512[:], ALU.mult))
                stepped(lambda j=j: v.reduce_sum(pos3[:, :, j:j + 1],
                                                 mj2[:].rearrange("p (k e) -> p k e", e=32), axis=AX.X))
                stepped(lambda: v.tensor_tensor(mj2[:], mjt[:], w_sb[:], ALU.mult))
                stepped(lambda j=j: v.reduce_sum(pay4[:, :, j, 1:2],
                                                 mj2[:].rearrange("p (k e) -> p k e", e=32), axis=AX.X))
            tid = tokid_sb[:].rearrange("p (a b) -> p a b", b=1)
            for k in range(NOWN):
                stepped(lambda k=k: v.tensor_copy(pay4[:, k, :, 0:1],
                                                  tid[:, k:k + 1, :].to_broadcast([128, 8, 1])))
            stepped(lambda: v.tensor_copy(pos8_u[:], pos8_f[:]))
        block.vector(_pos_v)
        s_dve.v += 2 + 5 + 48 + 16 + 1
        pay_done = s_dve.v

        def emit_scatters():
            def _sc_g(g, pd=pay_done, tid=ti_done):
                g.wait_ge(s_ti.s, tid)
                g.wait_ge(s_dve.s, pd)
                for col in range(NOWN * 8):
                    g.indirect_dma_start(
                        toks2, IOoA(pos8_u[:, col:col + 1], 0),
                        pay8[:, col * 2:col * 2 + 2], None,
                    ).then_inc(s_sc.s, 16)
            block.gpsimd(_sc_g)
            s_sc.v += NOWN * 8 * 16
            return s_sc.v

        barrier()
        ctxA.close()

    ctxBC = ExitStack()
    if phases >= 2:
        KT_sb = [ctxBC.enter_context(sb(f"KT{g}", [128, S], BF16)) for g in range(G)]
        V_sb = ctxBC.enter_context(sb("V_sb", [128, G * NT * VST], BF16))
        ctxB = ExitStack()
        EB = ctxB.enter_context
        wk_sb = EB(sb("wk_sb", [128, G * NC_ * 128], BF16, side="right"))
        wv_sb = EB(sb("wv_sb", [128, NC_ * 512], BF16, side="right"))
        bkT_sb = EB(sb("bkT_sb", [128, G], F32, side="right"))
        bv_sb = EB(sb("bv_sb", [1, 512], BF16, side="right"))
        xnT4 = EB(sb("xnT4", [128, NC_ * 512], BF16, side="right"))
        ps_tr = [EB(nc.psum_tensor(f"ps_trB{i}", [128, 128], BF16)) for i in range(2)]
        ps_kt = [EB(nc.psum_tensor(f"ps_kt{i}", [128, 512], F32)) for i in range(2)]
        ps_v = [EB(nc.psum_tensor(f"ps_v{i}", [128, 512], F32)) for i in range(2)]
        tr_parity[0] = 0

        def _pro_vb(v):
            vv = V_sb[:].rearrange("p (t c) -> p t c", c=VST)
            v.memset(vv[:, :, 128:129], 1.0).then_inc(s_dve.s, 1)
        block.vector(_pro_vb)
        s_dve.v += 1
        vones_ready = s_dve.v

        def _b_g(g):
            g.dma_start(wk_sb[:], wk_s).then_inc(s_g.s, 16)
            g.dma_start(wv_sb[:], wv_s).then_inc(s_g.s, 16)
            g.dma_start(bv_sb[:], bv_s).then_inc(s_g.s, 16)
        block.gpsimd(_b_g)
        s_g.v += 48
        gB_ready = s_g.v

        sc_total = emit_scatters()

        def _b_s(sy):
            sy.dma_start(bkT_sb[:], bkT_s).then_inc(s_x.s, 16)
            sy.dma_start(xbuf[0][:], x_full[0]).then_inc(s_x.s, 16)
        block.sync(_b_s)
        s_x.v += 32
        bkT_val = s_x.v

        kt_ev = []
        v_ev = []
        for t4 in range(NT // 4):
            for ti in range(4):
                tt = 4 * t4 + ti
                ln_done = emit_ln(s_x.v, xbuf[tt % 2], xn_hi=xn_hi)
                if tt + 1 < NT:
                    def _nx(sy, tt=tt, need=ln_done):
                        sy.wait_ge(s_dve.s, need)
                        sy.dma_start(xbuf[(tt + 1) % 2][:], x_full[tt + 1]).then_inc(s_x.s, 16)
                    block.sync(_nx)
                    s_x.v += 16
                pairs = [(xn_hi[:, c * 128:(c + 1) * 128],
                          xnT4[:, c * 512 + ti * 128: c * 512 + ti * 128 + 128]) for c in range(NC_)]
                emit_transposes(pairs, pre_wait=[(s_dve.s, ln_done)])
            x4_ready = s_tev.v

            for g in range(G):
                def _kt_pe(t, g=g, t4=t4, x4=x4_ready):
                    t.wait_ge(s_tev.s, x4)
                    if t4 == 0 and g == 0:
                        t.wait_ge(s_g.s, gB_ready)
                    if len(kt_ev) >= 2:
                        t.wait_ge(s_dve.s, kt_ev[-2])
                    last = None
                    for c in range(NC_):
                        last = t.matmul(ps_kt[g % 2][:],
                                        wk_sb[:, (g * NC_ + c) * 128:(g * NC_ + c) * 128 + 128],
                                        xnT4[:, c * 512:(c + 1) * 512],
                                        start=(c == 0), stop=(c == NC_ - 1))
                    last.then_inc(s_pe.s, 1)
                block.tensor(_kt_pe)
                s_pe.v += 1
                kt_done = s_pe.v

                def _kt_v(v, g=g, t4=t4, kd=kt_done, bv=bkT_val):
                    v.wait_ge(s_pe.s, kd)
                    if t4 == 0 and g == 0:
                        v.wait_ge(s_x.s, bv)
                    v.tensor_scalar(KT_sb[g][:, t4 * 512:(t4 + 1) * 512], ps_kt[g % 2][:],
                                    bkT_sb[:, g:g + 1], None, op0=ALU.add).then_inc(s_dve.s, 1)
                block.vector(_kt_v)
                s_dve.v += 1
                kt_ev.append(s_dve.v)

            for ti in range(4):
                tt = 4 * t4 + ti
                def _v_pe(t, ti=ti, t4=t4, x4=x4_ready):
                    t.wait_ge(s_tev.s, x4)
                    if t4 == 0 and ti == 0:
                        t.wait_ge(s_g.s, gB_ready)
                        t.wait_ge(s_dve.s, pro_dve)
                    if len(v_ev) >= 2:
                        t.wait_ge(s_dve.s, v_ev[-2])
                    for c in range(NC_):
                        t.matmul(ps_v[ti % 2][:],
                                 xnT4[:, c * 512 + ti * 128: c * 512 + ti * 128 + 128],
                                 wv_sb[:, c * 512:(c + 1) * 512],
                                 start=(c == 0), stop=False)
                    t.matmul(ps_v[ti % 2][:], ones_col[0:1, :], bv_sb[:],
                             start=False, stop=True).then_inc(s_pe.s, 1)
                block.tensor(_v_pe)
                s_pe.v += 1
                v_done = s_pe.v

                def _v_v(v, tt=tt, ti=ti, vd=v_done, vo=vones_ready):
                    v.wait_ge(s_pe.s, vd)
                    if tt == 0:
                        v.wait_ge(s_dve.s, vo)
                    dst = V_sb[:].rearrange("p (gt c) -> p gt c", c=VST)
                    v.tensor_copy(dst[:, tt::NT, 0:128],
                                  ps_v[ti % 2][:].rearrange("p (g d) -> p g d", g=G)).then_inc(s_dve.s, 1)
                block.vector(_v_v)
                s_dve.v += 1
                v_ev.append(s_dve.v)

        barrier()
        ctxB.close()
        ctxAB.close()
    else:
        ctxAB.close()

    if phases >= 3:
        ctxC = ExitStack()
        EC = ctxC.enter_context
        qt4 = [EC(sb(f"qt4_{i}", [128, 2048], BF16, side="right")) for i in range(2)]
        mask_sb = EC(sb("mask_sb", [128, 4 * 128], BF16, side="right"))
        pt_sb = [EC(sb(f"pt_sb{i}", [128, 512], BF16, side="right")) for i in range(2)]
        attn_sb = EC(sb("attn_sb", [128, 2048], F32, side="right"))
        xob = [EC(sb(f"xob{i}", [128, E], F32, side="right")) for i in range(2)]
        base_sb = [EC(sb(f"base_sb{i}", [128, E], F32, side="right")) for i in range(2)]
        linv_sb = EC(sb("linv_sb", [128, 4], F32, side="right"))
        xq_pg = [EC(sb(f"xq_pg{i}", [128, E], BF16, side="right")) for i in range(3)]
        tkp = [EC(sb(f"tkp{i}", [128, CAP_T * 2], F32, side="right")) for i in range(2)]
        gip = [EC(sb(f"gip{i}", [128, CAP_T], U32, side="right")) for i in range(2)]
        ps_st = [EC(nc.psum_tensor(f"ps_st{i}", [128, 512], F32)) for i in range(2)]
        ps_o = [EC(nc.psum_tensor(f"ps_o{i}", [128, 258], F32)) for i in range(4)]

        s_tkp2 = [Sem(ctx, nc, f"s_tkp{i}") for i in range(2)]
        s_icp = Sem(ctx, nc, "s_icp")
        s_gxp3 = [Sem(ctx, nc, f"s_gxp{i}") for i in range(3)]
        s_wxp3 = [Sem(ctx, nc, f"s_wxp{i}") for i in range(3)]
        ALLSEMS.extend(s_tkp2 + s_gxp3 + s_wxp3 + [s_icp])
        pg_gx = {}
        pg_wx = {}

        def emit_pg_write(i):
            def _wx_g(g, i=i, gv=pg_gx[i]):
                g.wait_ge(gv[0], gv[1])
                g.dma_start(xg_scr[i * 128:(i + 1) * 128, :],
                            xq_pg[i % 3][:]).then_inc(s_wxp3[i % 3].s, 16)
            block.gpsimd(_wx_g)
            s_wxp3[i % 3].v += 16
            pg_wx[i] = (s_wxp3[i % 3].s, s_wxp3[i % 3].v)

        def emit_pg(e):
            def _tk_s(sy, e=e, icn=(5 * (e - 1) if e >= 2 else 0)):
                if e == 0:
                    sy.wait_ge(s_sc.s, sc_total)
                if icn:
                    sy.wait_ge(s_icp.s, icn)
                sy.dma_start(
                    tkp[e % 2][:].rearrange("p (c w) -> p c w", w=2),
                    toks2[e * CAP:(e + 1) * CAP, :].rearrange("(c p) w -> p c w", p=128),
                ).then_inc(s_tkp2[e % 2].s, 16)
            block.sync(_tk_s)
            s_tkp2[e % 2].v += 16
            tkv = s_tkp2[e % 2].v

            def _ic_v(v, e=e, tkv=tkv, gwait=(pg_gx.get(5 * (e - 2) + 4) if e >= 2 else None)):
                v.wait_ge(s_tkp2[e % 2].s, tkv)
                if gwait:
                    v.wait_ge(gwait[0], gwait[1])
                t2 = tkp[e % 2][:].rearrange("p (c w) -> p c w", w=2)
                for c in range(CAP_T):
                    v.tensor_copy(gip[e % 2][:, c:c + 1], t2[:, c, 0:1]).then_inc(s_icp.s, 1)
            block.vector(_ic_v)
            s_icp.v += CAP_T

            for c in range(CAP_T):
                i = e * CAP_T + c
                def _gx_g(g, i=i, e=e, c=c, wprev=pg_wx.get(i - 3)):
                    g.wait_ge(s_icp.s, 5 * e + c + 1)
                    if wprev:
                        g.wait_ge(wprev[0], wprev[1])
                    g.indirect_dma_start(
                        xq_pg[i % 3][:], None, xn_scr,
                        IOoA(gip[e % 2][:, c:c + 1], 0),
                    ).then_inc(s_gxp3[i % 3].s, 16)
                block.gpsimd(_gx_g)
                s_gxp3[i % 3].v += 16
                pg_gx[i] = (s_gxp3[i % 3].s, s_gxp3[i % 3].v)
                if i >= 2:
                    emit_pg_write(i - 2)

        def _c_pre(sy):
            sy.dma_start(qt4[0][:], qT_scr[0]).then_inc(s_x.s, 16)
            sy.dma_start(xob[0][:], x_own[0]).then_inc(s_x.s, 16)
            sy.dma_start(mask_sb[:], masks[:, 0:512]).then_inc(s_x.s, 16)
        block.sync(_c_pre)
        s_x.v += 48

        c_st0 = s_pe.v
        c_exp0 = s_act.v
        c_av0 = s_pe2.v
        c_ev0 = s_ev.v
        n_unit = 0
        n_grp = 0
        base_w_vals = []

        for k in range(c_slots):
            J = 4 * k + 4
            qt_ready = s_x.v

            for g in range(G):
                u0 = n_unit
                m = n_grp

                mul_base = s_dve.v

                def _t_grp(t, k=k, g=g, J=J, u0=u0, m=m, qt_ready=qt_ready, mul_base=mul_base):
                    def emit_st(j):
                        u = u0 + j
                        if u >= 2:
                            t.wait_ge(s_act.s, c_exp0 + u - 1)
                        t.matmul(ps_st[u % 2][:], KT_sb[g][:, j * 128:(j + 1) * 128],
                                 qt4[k % 2][:, g * 512:(g + 1) * 512],
                                 start=True, stop=True).then_inc(s_pe.s, 1)

                    po = 2 * (m % 2)
                    def emit_av(j):
                        if j == 0 and m >= 2:
                            t.wait_ge(s_ev.s, c_ev0 + 8 * (m - 1))
                        if j >= J - 4:
                            t.wait_ge(s_dve.s, mul_base + (j - (J - 4)) + 1)
                        else:
                            t.wait_ge(s_act.s, c_exp0 + u0 + j + 1)
                        for h4 in range(4):
                            t.matmul(ps_o[po + h4 // 2][:, (h4 % 2) * 129:(h4 % 2) * 129 + 129],
                                     pt_sb[(u0 + j) % 2][:, h4 * 128:(h4 + 1) * 128],
                                     V_sb[:, (g * NT + j) * VST:(g * NT + j) * VST + 129],
                                     start=(j == 0 and h4 % 2 == 0),
                                     stop=(j == J - 1 and h4 % 2 == 1)).then_inc(s_pe2.s, 1)

                    if g == 0:
                        t.wait_ge(s_x.s, qt_ready)
                    emit_st(0)
                    if J > 1:
                        emit_st(1)
                    for j in range(J):
                        if j + 2 < J:
                            emit_st(j + 2)
                        emit_av(j)
                block.tensor(_t_grp)
                s_pe.v += J
                s_pe2.v += 4 * J
                av_all = s_pe2.v

                def _exp_a(s, J=J, u0=u0):
                    for j in range(J):
                        u = u0 + j
                        s.wait_ge(s_pe.s, c_st0 + u + 1)
                        if u >= 2:
                            s.wait_ge(s_pe2.s, c_av0 + 4 * (u - 1))
                        s.activation(pt_sb[u % 2][:], ps_st[u % 2][:], AF.Exp,
                                     bias=consts[:, 0:1]).then_inc(s_act.s, 1)
                block.scalar(_exp_a)
                s_act.v += J

                def _mul_v(v, J=J, u0=u0, g=g, qt_ready=qt_ready):
                    if g == 0:
                        v.wait_ge(s_x.s, qt_ready)
                    for j in range(J - 4, J):
                        u = u0 + j
                        v.wait_ge(s_act.s, c_exp0 + u + 1)
                        m_b = mask_sb[:, (j - (J - 4)) * 128:(j - (J - 4) + 1) * 128].unsqueeze(1).to_broadcast([128, 4, 128])
                        p_v = pt_sb[u % 2][:].rearrange("p (r n) -> p r n", r=4)
                        v.tensor_tensor(p_v, p_v, m_b, ALU.mult).then_inc(s_dve.s, 1)
                block.vector(_mul_v)
                s_dve.v += 4

                def _ev_v(v, g=g, av=av_all, ev_base=s_ev.v, po=2 * (n_grp % 2)):
                    v.wait_ge(s_pe2.s, av)
                    for h4 in range(4):
                        v.reciprocal(linv_sb[:, h4:h4 + 1],
                                     ps_o[po + h4 // 2][:, (h4 % 2) * 129 + 128:(h4 % 2) * 129 + 129]).then_inc(s_ev.s, 1)
                        v.wait_ge(s_ev.s, ev_base + 2 * h4 + 1)
                        v.tensor_scalar(attn_sb[:, (4 * g + h4) * 128:(4 * g + h4 + 1) * 128],
                                        ps_o[po + h4 // 2][:, (h4 % 2) * 129:(h4 % 2) * 129 + 128],
                                        linv_sb[:, h4:h4 + 1], None, op0=ALU.mult).then_inc(s_ev.s, 1)
                block.vector(_ev_v)
                s_ev.v += 8

                n_unit += J
                n_grp += 1

            def _base_v(v, k=k, ev=s_ev.v, qr=qt_ready,
                        bw=(base_w_vals[k - 1] if k >= 1 else 0)):
                v.wait_ge(s_ev.s, ev)
                v.wait_ge(s_x.s, qr)
                if bw:
                    v.wait_ge(s_out.s, bw)
                v.tensor_add(base_sb[k % 2][:], attn_sb[:], xob[k % 2][:]).then_inc(s_dve.s, 1)
            block.vector(_base_v)
            s_dve.v += 1
            base_done = s_dve.v

            def _base_w(g, k=k, bd=base_done):
                g.wait_ge(s_dve.s, bd)
                g.dma_start(y[k * 128:(k + 1) * 128, :], base_sb[k % 2][:]).then_inc(s_out.s, 16)
            block.scalar(_base_w)
            s_out.v += 16
            base_w_vals.append(s_out.v)

            emit_pg(2 * k)
            emit_pg(2 * k + 1)
            if k == NOWN - 1:
                emit_pg_write(NTILE - 2)
                emit_pg_write(NTILE - 1)

            if k + 1 < NOWN:
                def _c_nx(sy, k=k, stv=s_pe.v, mulv=s_dve.v):
                    sy.wait_ge(s_pe.s, stv)
                    sy.wait_ge(s_dve.s, mulv)
                    sy.dma_start(qt4[(k + 1) % 2][:], qT_scr[k + 1]).then_inc(s_x.s, 16)
                    sy.dma_start(xob[(k + 1) % 2][:], x_own[k + 1]).then_inc(s_x.s, 16)
                    sy.dma_start(mask_sb[:],
                                 masks[:, (k + 1) * 512:(k + 2) * 512]).then_inc(s_x.s, 16)
                block.sync(_c_nx)
                s_x.v += 48

        barrier()
        ctxC.close()
    ctxBC.close()

    if phases >= 4:
        base_total = s_out.v
        ctxD = ExitStack()
        ED = ctxD.enter_context
        wgu_sb = [ED(sb(f"wgu{i}", [128, 16 * 1536], BF16)) for i in range(2)]
        wd_sb = [ED(sb(f"wdd{i}", [128, NIB * 2048], BF16)) for i in range(2)]
        xg = [ED(sb(f"xg{i}", [128, E], BF16)) for i in range(2)]
        xgT = [ED(sb(f"xgT{i}", [128, E], BF16)) for i in range(2)]
        Hg = ED(sb("Hg", [128, I], BF16))
        H1 = ED(sb("H1", [128, I], BF16))
        H1T = [ED(sb(f"H1T{i}", [128, NIB * 128], BF16)) for i in range(2)]
        ye = [ED(sb(f"ye{i}", [128, E], F32)) for i in range(2)]
        tk_sb = [ED(sb(f"tk{i}", [128, CAP_T * 2], F32)) for i in range(2)]
        gi_u = [ED(sb(f"gi{i}", [128, CAP_T], U32)) for i in range(2)]
        stag = [ED(sb(f"stag{i}", [128, 1024], F32)) for i in range(4)]
        ps_gate = ED(nc.psum_tensor("ps_gate", [128, I], F32))
        ps_up = ED(nc.psum_tensor("ps_up", [128, I], F32))
        ps_d = [ED(nc.psum_tensor(f"ps_d{i}", [128, 512], F32)) for i in range(2)]
        ps_tr = [ED(nc.psum_tensor(f"ps_trD{i}", [128, 128], BF16)) for i in range(2)]

        s_st = [Sem(ctx, nc, f"s_st{i}") for i in range(4)]
        s_cw = Sem(ctx, nc, "s_cw")
        s_tk = Sem(ctx, nc, "s_tk")
        s_ic = Sem(ctx, nc, "s_ic")
        s_gx2 = [Sem(ctx, nc, f"s_gx{i}") for i in range(2)]
        s_sa2 = [Sem(ctx, nc, f"s_sa{i}") for i in range(2)]
        s_sil = Sem(ctx, nc, "s_sil")
        s_h1 = Sem(ctx, nc, "s_h1")
        s_ye = Sem(ctx, nc, "s_ye")
        ALLSEMS.extend(s_st + s_gx2 + s_sa2 + [s_cw, s_tk, s_ic, s_sil, s_h1, s_ye])

        ye_base = s_ye.v
        pe2_base = s_pe2.v

        up_pe = {}
        gate_pe = {}
        dn_pe2 = {}
        sil_v = {}
        h1_v = {}
        htr_ev = {}
        xgtr_ev = {}
        xgtr_tr = {}
        gx_v = {}
        sa_v = {}
        ic_v = {}
        tk_v = {}
        gu_rdy = {}
        dn_rdy = {}
        wq_cnt = [0]

        def load_toks(e, extra=None):
            def _t_s(sy, e=e, extra=extra):
                if extra:
                    for sm, vv in extra:
                        if vv > 0:
                            sy.wait_ge(sm, vv)
                sy.dma_start(
                    tk_sb[e % 2][:].rearrange("p (c w) -> p c w", w=2),
                    toks2[e * CAP:(e + 1) * CAP, :].rearrange("(c p) w -> p c w", p=128),
                ).then_inc(s_tk.s, 16)
            block.sync(_t_s)
            s_tk.v += 16
            tk_v[e] = s_tk.v

        def cast_idx(e, extra=None):
            def _c_v(v, e=e, extra=extra):
                v.wait_ge(s_tk.s, tk_v[e])
                if extra:
                    for sm, vv in extra:
                        if vv > 0:
                            v.wait_ge(sm, vv)
                tk2 = tk_sb[e % 2][:].rearrange("p (c w) -> p c w", w=2)
                for c in range(CAP_T):
                    v.tensor_copy(gi_u[e % 2][:, c:c + 1], tk2[:, c, 0:1]).then_inc(s_ic.s, 1)
            block.vector(_c_v)
            s_ic.v += CAP_T
            ic_v[e] = s_ic.v

        def emit_wchunks(e, c0, c1):
            for c in range(c0, c1):
                q = wq_cnt[0]
                slot = q % 4
                src = (wgu_in[e][:, c * 1024:(c + 1) * 1024] if c < 24
                       else wd_in[e][:, (c - 24) * 1024:(c - 23) * 1024])
                need = q - 3
                def _ld(eng, slot=slot, src=src, need=need):
                    if need > 0:
                        eng.wait_ge(s_cw.s, need)
                    eng.dma_start(stag[slot][:], src).then_inc(s_st[slot].s, 16)
                (block.sync if q % 2 == 0 else block.scalar)(_ld)
                s_st[slot].v += 16
                stv = s_st[slot].v
                dst = (wgu_sb[e % 2][:, c * 1024:(c + 1) * 1024] if c < 24
                       else wd_sb[e % 2][:, (c - 24) * 1024:(c - 23) * 1024])
                def _ca(a, slot=slot, stv=stv, dst=dst, e=e, c=c):
                    a.wait_ge(s_st[slot].s, stv)
                    if c == 0 and e >= 2:
                        a.wait_ge(s_pe.s, up_pe[5 * (e - 2) + 4])
                    if c == 24 and e >= 2:
                        a.wait_ge(s_pe2.s, dn_pe2[5 * (e - 2) + 4])
                    a.activation(dst, stag[slot][:], AF.Copy).then_inc(s_cw.s, 1)
                block.scalar(_ca)
                wq_cnt[0] += 1
                if c == 23:
                    gu_rdy[e] = wq_cnt[0]
                if c == 35:
                    dn_rdy[e] = wq_cnt[0]

        def gather(i, first=False):
            def _gx_a(a, i=i):
                if i >= 2:
                    a.wait_ge(s_tr.s, xgtr_tr[i - 2])
                a.dma_start(xg[i % 2][:],
                            xg_scr[i * 128:(i + 1) * 128, :]).then_inc(s_gx2[i % 2].s, 16)
            block.sync(_gx_a)
            s_gx2[i % 2].v += 16
            gx_v[i] = s_gx2[i % 2].v

        def sa(i):
            e, c = i // CAP_T, i % CAP_T
            w0, w1 = s_sa2[0].v, s_sa2[1].v
            def _sa_g(g, i=i, e=e, c=c, w0=w0, w1=w1):
                g.wait_ge(s_ye.s, ye_base + 4 * (i + 1))
                if i == 0:
                    g.wait_ge(s_out.s, base_total)
                elif c == 0:
                    if w0:
                        g.wait_ge(s_sa2[0].s, w0)
                    if w1:
                        g.wait_ge(s_sa2[1].s, w1)
                g.indirect_dma_start(
                    y, IOoA(gi_u[e % 2][:, c:c + 1], 0),
                    ye[i % 2][:], None,
                    compute_op=ALU.add,
                ).then_inc(s_sa2[i % 2].s, 16)
            block.gpsimd(_sa_g)
            s_sa2[i % 2].v += 16
            sa_v[i] = s_sa2[i % 2].v

        def emit_xgtr(i):
            pw = [(s_gx2[i % 2].s, gx_v[i])]
            epw = []
            if i >= 2:
                epw.append((s_pe.s, up_pe[i - 2]))
            pairs = [(xg[i % 2][:, c * 128:(c + 1) * 128], xgT[i % 2][:, c * 128:(c + 1) * 128])
                     for c in range(NC_)]
            xgtr_tr[i] = s_tr.v + NC_
            xgtr_ev[i] = emit_transposes(pairs, pre_wait=pw, ev_pre_wait=epw or None)

        load_toks(0, extra=[(s_sc.s, sc_total)])
        cast_idx(0)
        load_toks(1)
        cast_idx(1)
        emit_wchunks(0, 0, 36)
        emit_wchunks(1, 0, 36)
        gather(0, first=True)
        gather(1)
        emit_xgtr(0)

        for i in range(NTILE):
            e, c = i // CAP_T, i % CAP_T
            es = e % 2

            if c == 0 and e >= 1 and e + 1 < NEXP:
                load_toks(e + 1, extra=[(s_ye.s, ye_base + 4 * (5 * e)),
                                        (s_ic.s, ic_v[e])])
            if c == 1 and e >= 1 and e + 1 < NEXP:
                cast_idx(e + 1, extra=[(s_sa2[(5 * e - 1) % 2].s, sa_v[5 * e - 1])])

            def _gu_pe(t, i=i, es=es, e=e):
                t.wait_ge(s_tev.s, xgtr_ev[i])
                t.wait_ge(s_cw.s, gu_rdy[e])
                if i >= 1:
                    t.wait_ge(s_sil.s, sil_v[i - 1])
                last = None
                for c16 in range(NC_):
                    lhs = xgT[i % 2][:, c16 * 128:(c16 + 1) * 128]
                    b = c16 * 1536
                    t.matmul(ps_gate[:, 0:512], lhs, wgu_sb[es][:, b:b + 512],
                             start=(c16 == 0), stop=False)
                    last = t.matmul(ps_gate[:, 512:768], lhs, wgu_sb[es][:, b + 512:b + 768],
                                    start=(c16 == 0), stop=(c16 == NC_ - 1))
                last.then_inc(s_pe.s, 1)
                if i >= 1:
                    t.wait_ge(s_h1.s, h1_v[i - 1])
                last = None
                for c16 in range(NC_):
                    lhs = xgT[i % 2][:, c16 * 128:(c16 + 1) * 128]
                    b = c16 * 1536 + 768
                    t.matmul(ps_up[:, 0:512], lhs, wgu_sb[es][:, b:b + 512],
                             start=(c16 == 0), stop=False)
                    last = t.matmul(ps_up[:, 512:768], lhs, wgu_sb[es][:, b + 512:b + 768],
                                    start=(c16 == 0), stop=(c16 == NC_ - 1))
                last.then_inc(s_pe.s, 1)
            block.tensor(_gu_pe)
            s_pe.v += 2
            gate_pe[i] = s_pe.v - 1
            up_pe[i] = s_pe.v

            def _sil_a(s, i=i):
                s.wait_ge(s_pe.s, gate_pe[i])
                if i >= 1:
                    s.wait_ge(s_h1.s, h1_v[i - 1])
                s.activation(Hg[:], ps_gate[:], AF.Silu).then_inc(s_sil.s, 1)
            block.scalar(_sil_a)
            s_sil.v += 1
            sil_v[i] = s_sil.v

            if i >= 1:
                ip = i - 1
                ep, cp = ip // CAP_T, ip % CAP_T
                def _ye_v(v, ip=ip, ep=ep, cp=cp):
                    if ip >= 2:
                        v.wait_ge(s_sa2[ip % 2].s, sa_v[ip - 2])
                    tk2 = tk_sb[ep % 2][:].rearrange("p (c w) -> p c w", w=2)
                    for j in range(4):
                        v.wait_ge(s_pe2.s, pe2_base + 4 * ip + j + 1)
                        v.tensor_scalar(ye[ip % 2][:, j * 512:(j + 1) * 512],
                                        ps_d[j % 2][:], tk2[:, cp, 1:2], None,
                                        op0=ALU.mult).then_inc(s_ye.s, 1)
                block.vector(_ye_v)
                s_ye.v += 4

            def _h1_v(v, i=i):
                v.wait_ge(s_sil.s, sil_v[i])
                v.wait_ge(s_pe.s, up_pe[i])
                v.tensor_tensor(H1[:], ps_up[:], Hg[:], ALU.mult).then_inc(s_h1.s, 1)
            block.vector(_h1_v)
            s_h1.v += 1
            h1_v[i] = s_h1.v

            if i + 1 < NTILE:
                emit_xgtr(i + 1)

            epw = []
            if i >= 2:
                epw.append((s_pe2.s, dn_pe2[i - 2]))
            pairs = [(H1[:, c2 * 128:(c2 + 1) * 128], H1T[i % 2][:, c2 * 128:(c2 + 1) * 128])
                     for c2 in range(NIB)]
            htr_ev[i] = emit_transposes(
                pairs, pre_wait=[(s_h1.s, h1_v[i])], ev_pre_wait=epw or None)

            def _dn_pe(t, i=i, es=es, e=e):
                t.wait_ge(s_tev.s, htr_ev[i])
                t.wait_ge(s_cw.s, dn_rdy[e])
                for j in range(4):
                    gidx = 4 * i + j
                    if gidx >= 2:
                        t.wait_ge(s_ye.s, ye_base + gidx - 1)
                    last = None
                    for ib in range(NIB):
                        last = t.matmul(ps_d[j % 2][:],
                                        H1T[i % 2][:, ib * 128:(ib + 1) * 128],
                                        wd_sb[es][:, ib * 2048 + j * 512: ib * 2048 + (j + 1) * 512],
                                        start=(ib == 0), stop=(ib == NIB - 1))
                    last.then_inc(s_pe2.s, 1)
            block.tensor(_dn_pe)
            s_pe2.v += 4
            dn_pe2[i] = s_pe2.v

            if e >= 1 and e + 1 < NEXP:
                rngs = [(0, 8), (8, 16), (16, 24), (24, 32), (32, 36)]
                emit_wchunks(e + 1, *rngs[c])

            if i + 2 < NTILE:
                gather(i + 2)
            if i >= 1:
                sa(i - 1)

        ip = NTILE - 1
        ep, cp = ip // CAP_T, ip % CAP_T
        def _ye_last(v, ip=ip, ep=ep, cp=cp):
            v.wait_ge(s_sa2[ip % 2].s, sa_v[ip - 2])
            tk2 = tk_sb[ep % 2][:].rearrange("p (c w) -> p c w", w=2)
            for j in range(4):
                v.wait_ge(s_pe2.s, pe2_base + 4 * ip + j + 1)
                v.tensor_scalar(ye[ip % 2][:, j * 512:(j + 1) * 512],
                                ps_d[j % 2][:], tk2[:, cp, 1:2], None,
                                op0=ALU.mult).then_inc(s_ye.s, 1)
        block.vector(_ye_last)
        s_ye.v += 4
        sa(NTILE - 1)

        def _fin(g, sv0=s_sa2[0].v, sv1=s_sa2[1].v, outv=s_out.v):
            g.wait_ge(s_sa2[0].s, sv0)
            g.wait_ge(s_sa2[1].s, sv1)
            g.wait_ge(s_out.s, outv)
        block.gpsimd(_fin)
        ctxD.close()

    ctx.close()
    return nc


def _prep_inputs(L0_attn_x, L0_attn_Wq, L0_attn_bq, L0_attn_Wk, L0_attn_bk,
                 L0_attn_Wv, L0_attn_bv, L0_ffn_router_w, L0_ffn_w_gate,
                 L0_ffn_w_up, L0_ffn_w_down):
    f = np.ascontiguousarray
    x = np.asarray(L0_attn_x, np.float32)
    wq = np.asarray(L0_attn_Wq, np.float32)
    bq = np.asarray(L0_attn_bq, np.float32)
    wk = np.asarray(L0_attn_Wk, np.float32)
    bk = np.asarray(L0_attn_bk, np.float32)
    wv = np.asarray(L0_attn_Wv, np.float32)
    bv = np.asarray(L0_attn_bv, np.float32)
    rw = np.asarray(L0_ffn_router_w, np.float32)
    wg = np.asarray(L0_ffn_w_gate, np.float32)
    wu = np.asarray(L0_ffn_w_up, np.float32)
    wd = np.asarray(L0_ffn_w_down, np.float32)

    wq_s = f(wq.transpose(1, 0, 2).reshape(2048, 2048).reshape(NC_, 128, 2048).transpose(1, 0, 2))
    wqb_s = f(bq.reshape(1, 2048))
    wk_s = f(wk.reshape(G, NC_, 128, 128).transpose(2, 0, 1, 3).reshape(128, G * NC_ * 128))
    bkT_s = f(bk.T)
    wv_s = f(wv.transpose(1, 0, 2).reshape(2048, 512).reshape(NC_, 128, 512).transpose(1, 0, 2).reshape(128, NC_ * 512))
    bv_s = f(bv.reshape(1, 512))
    rw_pad = np.zeros((2048, 128), np.float32)
    rw_pad[:, :32] = rw.T
    rw_s = f(rw_pad.reshape(NC_, 128, 128).transpose(1, 0, 2).reshape(128, NC_ * 128))
    ident = np.eye(128, dtype=ml_dtypes.bfloat16)

    wgt = wg.reshape(NEXP, 16, 128, I).transpose(0, 2, 1, 3)
    wut = wu.reshape(NEXP, 16, 128, I).transpose(0, 2, 1, 3)
    wgu_s = f(np.concatenate([wgt, wut], axis=3).reshape(NEXP, 128, 16 * 1536))
    wd_s = f(wd.reshape(NEXP, NIB, 128, 2048).transpose(0, 2, 1, 3).reshape(NEXP, 128, NIB * 2048))

    tri_c = f(np.triu(np.ones((128, 128), np.float32)).astype(ml_dtypes.bfloat16))
    ones_c = np.ones((128, 128), ml_dtypes.bfloat16)
    ecap_c = f(np.broadcast_to(np.tile(np.arange(NEXP, dtype=np.float32) * CAP - 1.0, NOWN),
                               (128, NOWN * 32)).copy())
    tokid_c = f((np.arange(NOWN, dtype=np.float32)[None, :] * 128
                 + np.arange(128, dtype=np.float32)[:, None]))

    kk_ar = np.arange(128)
    masks_by_r = []
    for r in range(4):
        m = np.zeros((NOWN * 4, 128, 128), np.float32)
        for k in range(NOWN):
            qpos = (4 * k + r) * 128 + kk_ar
            for jj in range(4):
                kpos = (4 * k + jj) * 128 + kk_ar
                m[k * 4 + jj] = (kpos[:, None] <= qpos[None, :])
        masks_by_r.append(f(m.astype(ml_dtypes.bfloat16).transpose(1, 0, 2).reshape(128, NOWN * 4 * 128)))

    shared = dict(wq_s=wq_s, wqb_s=wqb_s, wk_s=wk_s, bkT_s=bkT_s, wv_s=wv_s,
                  bv_s=bv_s, rw_s=rw_s, wgu_in=wgu_s, wd_in=wd_s, ident=ident,
                  tri_c=tri_c, ones_c=ones_c, ecap_c=ecap_c, tokid_c=tokid_c)
    in_maps = []
    for c in range(8):
        b, r = c // 4, c % 4
        xb = x[b].reshape(NT, 128, 2048)
        m = dict(shared)
        m["x_own"] = f(xb[r::4])
        m["x_full"] = f(xb)
        m["masks"] = masks_by_r[r]
        in_maps.append(m)
    return in_maps


_NC_CACHE = {}


def kernel(**inputs):
    key = (5, False)
    if key not in _NC_CACHE:
        _NC_CACHE[key] = build(phases=4, dbg=False)
    nc = _NC_CACHE[key]
    in_maps = _prep_inputs(**inputs)
    res = run_bass_kernel_spmd(nc, in_maps, core_ids=list(range(8)))
    out = np.empty((B, S, E), np.float32)
    for c in range(8):
        b, r = c // 4, c % 4
        out[b].reshape(NT, 128, 2048)[r::4] = res.results[c]["y"][:NOWN * 128].reshape(NOWN, 128, E)
    return out
